# version 05
# baseline (speedup 1.0000x reference)
import numpy as np
import concourse.bass as bass
import concourse.mybir as mybir
from concourse.bass_utils import run_bass_kernel_spmd

F32 = mybir.dt.float32
AF = mybir.ActivationFunctionType
ALU = mybir.AluOpType
AX = mybir.AxisListType

D = 4096
TOK = 1024
NKD = 32
PROJ = 8512
MIN = 67
NCV = 34
NQ, Q = 8, 128
NT = 8
EPS = 1e-5
SCL = float(1.0 / np.sqrt(128.0))

DEBUG = False
PHASE_LIMIT = 99


class Ctr:
    __slots__ = ("sem", "v")

    def __init__(self, nc, name):
        self.sem = nc.alloc_semaphore(name)
        self.v = 0

    def inc(self, ins, n):
        ins.then_inc(self.sem, n)
        self.v += n
        return self.v


def _consts():
    ident = np.eye(128, dtype=np.float32)
    ones1 = np.ones((1, 128), np.float32)
    s_idx = np.arange(128)
    maskneg = np.where(s_idx[:, None] <= s_idx[None, :], 0.0, -1e30).astype(np.float32)
    sel32 = (np.arange(4096)[None, :] // 32 == s_idx[:, None]).astype(np.float32)
    eselg = np.zeros((28, 256), np.float32)
    for e in range(28):
        eselg[e, e * 8:(e + 1) * 8] = 2.0
    esel8 = np.zeros((8, 128), np.float32)
    for e in range(8):
        esel8[e, e * 16:(e + 1) * 16] = 1.0
    return ident, ones1, maskneg, sel32, eselg, esel8


def build_nc():
    nc = bass.Bass("TRN2", target_bir_lowering=False, debug=False, num_devices=8)

    def inp(name, shape):
        return nc.dram_tensor(name, list(shape), F32, kind="ExternalInput").ap()

    x_h = inp("x", (TOK, D))
    xh_h = inp("xh", (3, D))
    w_in_h = inp("w_in", (MIN, 128, NKD, 128))
    mupv_h = inp("mupv", (128, MIN))
    lnw_h = inp("lnw", (128, NKD))
    normw_h = inp("normw", (128, NKD))
    convw_h = inp("convw", (NCV, 128, 4))
    dtb_h = inp("dtb", (64, 1))
    alog_h = inp("alog", (64, 1))
    dvecr_h = inp("dvecr", (1, 64))
    w_out_h = inp("w_out", (NKD, 128, NKD, 128))
    w_base_h = inp("w_base", (NKD, 128, NKD, 128))
    w_gate_h = inp("w_gate", (NKD, 128, 28))
    w_la_h = inp("w_la", (NKD, 128, 256))
    w_lb_h = inp("w_lb", (NKD, 2, 128, 128))
    wq_sl_h = inp("wq_sl", (4, 128, 128, 32))
    bq_h = inp("bq", (128, 32))
    bk_h = inp("bk", (128, 1))
    bv_h = inp("bv", (128, 1))
    w_k_h = inp("w_k", (NKD, 128, 128))
    w_v_h = inp("w_v", (NKD, 128, 128))
    w_route_h = inp("w_route", (NKD, 128, 8))
    w_a1_h = inp("w_a1", (NKD, 128, 128))
    w_b1_h = inp("w_b1", (128, D))
    pm8_h = inp("pm8", (1, 8))
    ppw_h = inp("ppw", (1, 64))

    out_h = nc.dram_tensor("out", [TOK, D], F32, kind="ExternalOutput").ap()

    skind = "ExternalOutput" if DEBUG else "Internal"

    def scr(name, shape):
        return nc.dram_tensor(name, list(shape), F32, kind=skind).ap()

    zx_scr = scr("zx_scr", (MIN, 128, TOK))
    y_scr = scr("y_scr", (NQ, 128, D))
    y2_scr = scr("y2_scr", (NKD, 128, TOK))
    ffn_scr = scr("ffn_scr", (NKD, 128, TOK))
    sum_scr = scr("sum_scr", (NT, 128, D))
    s_scr = scr("s_scr", (NQ, 64, 128))
    rs2_scr = scr("rs2_scr", (8, 128))
    ag_wq_in = nc.dram_tensor("ag_wq_in", [512, 128], F32, kind="Internal")
    ag_wq_out = nc.dram_tensor("ag_wq_out", [4096, 128], F32, kind="Internal")
    ag_s_in = nc.dram_tensor("ag_s_in", [128, 4160], F32, kind="Internal")
    ag_s_out = nc.dram_tensor("ag_s_out", [1024, 4160], F32, kind="Internal")
    if DEBUG:
        dbg_ags = nc.dram_tensor("dbg_ags", [1024, 4160], F32,
                                 kind="ExternalOutput").ap()
        dbg_xnT = nc.dram_tensor("dbg_xnT", [NKD, 128, TOK], F32,
                                 kind="ExternalOutput").ap()
        dbg_r2 = nc.dram_tensor("dbg_r2", [1, TOK], F32,
                                 kind="ExternalOutput").ap()
        dbg_r2t = nc.dram_tensor("dbg_r2t", [8, 128], F32,
                                 kind="ExternalOutput").ap()
        dbg_r2c = nc.dram_tensor("dbg_r2c", [128, 8], F32,
                                 kind="ExternalOutput").ap()
        dbg_h = nc.dram_tensor("dbg_h", [2, 128, TOK], F32,
                               kind="ExternalOutput").ap()
        dbg_gf = nc.dram_tensor("dbg_gf", [28, TOK], F32,
                                kind="ExternalOutput").ap()
        dbg_lg = nc.dram_tensor("dbg_lg", [28, TOK], F32,
                                kind="ExternalOutput").ap()
        dbg_lgT = nc.dram_tensor("dbg_lgT", [128, NT, 28], F32,
                                 kind="ExternalOutput").ap()
        dbg_gT = nc.dram_tensor("dbg_gT", [128, NT, 28], F32,
                                kind="ExternalOutput").ap()
        dbg_sdt = nc.dram_tensor("dbg_sdt", [NQ, 128, 128], F32,
                                 kind="ExternalOutput").ap()
        dbg_lamr = nc.dram_tensor("dbg_lamr", [1, NQ, 64], F32,
                                  kind="ExternalOutput").ap()
        dbg_eall = nc.dram_tensor("dbg_eall", [128, NQ, 64], F32,
                                  kind="ExternalOutput").ap()
        dbg_srow = nc.dram_tensor("dbg_srow", [NQ, 8192], F32,
                                  kind="ExternalOutput").ap()

    ident_np, ones1_np, maskneg_np, sel32_np, eselg_np, esel8_np = _consts()
    ident_h = nc.inline_tensor(ident_np, name="identc")
    ones1_h = nc.inline_tensor(ones1_np, name="ones1c")
    maskneg_h = nc.inline_tensor(maskneg_np, name="masknegc")
    sel32_h = nc.inline_tensor(sel32_np, name="sel32c")
    eselg_h = nc.inline_tensor(eselg_np, name="eselgc")
    esel8_h = nc.inline_tensor(esel8_np, name="esel8c")

    ps = [nc.alloc_psum_tensor(f"ps{i}", [128, 512], F32) for i in range(8)]

    A = nc.alloc_sbuf_tensor
    id_sb = A("id_sb", [128, 128], F32)
    on1 = A("on1", [1, 128], F32)
    mneg = A("mneg", [128, 128], F32)
    eselg = A("eselg", [28, 256], F32)
    esel8 = A("esel8", [8, 128], F32)
    lnw = A("sb_lnw", [128, NKD], F32)
    normw = A("sb_normw", [128, NKD], F32)
    mupv = A("sb_mupv", [128, MIN], F32)
    convw = A("sb_convw", [128, NCV, 4], F32)
    dtb = A("sb_dtb", [64, 1], F32)
    negA = A("negA", [64, 1], F32)
    dvrow = A("dvrow", [1, 64], F32)
    db64 = A("db64", [128, 64], F32)
    bq32 = A("bq32", [128, 32], F32)
    bqs = A("bqs", [128, 1], F32)
    bkc = A("bkc", [128, 1], F32)
    bvc = A("bvc", [128, 1], F32)
    pm8 = A("sb_pm8", [1, 8], F32)
    ppw = A("sb_ppw", [1, 64], F32)
    halo_my = A("halo_my", [128, NCV, 3], F32)
    rs2row = A("rs2row", [1, TOK], F32)
    rs2hb = A("rs2hb", [128, 3], F32)
    e_all = A("e_all", [128, NQ, 64], F32)
    cm_all = A("cm_all", [128, NQ, 128], F32)
    lamr = A("lamr", [1, NQ, 64], F32)
    lamtot = A("lamtot", [1, 64], F32)
    lamcum = A("lamcum", [1, NQ, 64], F32)
    wrow = A("wrow", [1, 8, 64], F32)
    lam_all = A("lam_all", [1, 8, 64], F32)
    s1row = A("s1row", [1, 64], F32)

    mmc = Ctr(nc, "mmc")
    vec = Ctr(nc, "vec")
    act = Ctr(nc, "act")
    cc = Ctr(nc, "cc")
    ctrs = (mmc, vec, act, cc)

    class DmaPool:
        def __init__(self, n, name):
            self.sems = [nc.alloc_semaphore(f"{name}{i}") for i in range(n)]
            self.vals = [0] * n
            self.rr = 0
            self.n = n

        def issue(self, eng, **kw):
            i = self.rr
            self.rr = (self.rr + 1) % self.n
            eng.wait_ge(self.sems[i], self.vals[i])
            ins = eng.dma_start(**kw)
            ins.then_inc(self.sems[i], 16)
            self.vals[i] += 16
            return (self.sems[i], self.vals[i])

        def all_tokens(self):
            return [(s, v) for s, v in zip(self.sems, self.vals)]

    dpool = DmaPool(12, "dp")
    TK = []

    def dm(eng, **kw):
        TK.append(dpool.issue(eng, **kw))

    def seal():
        t = list(TK)
        TK.clear()
        return t

    def wt(eng, toks):
        for s, v in toks:
            eng.wait_ge(s, v)

    block_cm = nc.Block()
    block = block_cm.__enter__()

    def barrier():
        toks = dpool.all_tokens()
        for emit in (block.sync, block.tensor, block.vector, block.scalar,
                     block.gpsimd):
            def f(eng):
                for c in ctrs:
                    eng.wait_ge(c.sem, c.v)
                for s, v in toks:
                    eng.wait_ge(s, v)
            emit(f)

    gA = []

    def scA(name, shape):
        g = nc.sbuf_tensor(name, shape, F32)
        gA.append(g)
        return g.__enter__()

    kv = {}

    gWq = nc.sbuf_tensor("wqsp", [128, 4, 128], F32)
    wqsp = gWq.__enter__()
    gWq2 = nc.sbuf_tensor("wqbig", [128, 128, 32], F32)
    wqbig = gWq2.__enter__()

    def f_consts(sync):
        for dst, src in ((id_sb[:], ident_h.ap()), (on1[:], ones1_h.ap()),
                         (mneg[:], maskneg_h.ap()), (eselg[:], eselg_h.ap()),
                         (esel8[:], esel8_h.ap()), (lnw[:], lnw_h),
                         (normw[:], normw_h), (mupv[:], mupv_h),
                         (convw[:], convw_h.transpose([1, 0, 2])),
                         (dtb[:], dtb_h), (negA[:], alog_h), (dvrow[:], dvecr_h),
                         (pm8[:], pm8_h), (ppw[:], ppw_h), (bq32[:], bq_h),
                         (bkc[:], bk_h), (bvc[:], bv_h)):
            dm(sync, out=dst, in_=src)
        kv["consts"] = seal()
    block.sync(f_consts)

    for j in range(4):
        def f(sync, j=j):
            if j >= 1:
                sync.wait_ge(vec.sem, kv[f"wqr{j - 1}"])
            dm(sync, out=wqbig[:], in_=wq_sl_h[j])
            kv[f"wq{j}"] = seal()
        block.sync(f)

        def f(vector, j=j):
            wt(vector, kv[f"wq{j}"])
            ins = vector.tensor_reduce(wqsp[:, j, :],
                                       wqbig[:].rearrange("p d h -> p d h"),
                                       AX.X, ALU.add)
            vec.inc(ins, 1)
            vector.wait_ge(vec.sem, vec.v)
            kv[f"wqr{j}"] = vec.v
        block.vector(f)

    def f(sync):
        sync.wait_ge(vec.sem, kv["wqr3"])
        agw = ag_wq_in.ap().rearrange("(j p) d -> j p d", p=128)
        for j in range(4):
            dm(sync, out=agw[j], in_=wqsp[:, j, :])
        kv["wq_in"] = seal()
    block.sync(f)

    def f(gpsimd):
        wt(gpsimd, kv["wq_in"])
        ins = gpsimd.collective_compute(
            "AllGather", ALU.bypass, replica_groups=[list(range(8))],
            ins=[ag_wq_in.ap().opt()], outs=[ag_wq_out.ap().opt()])
        cc.inc(ins, 1)
        kv["ag_wq"] = cc.v
    block.gpsimd(f)

    gWq2.__exit__(None, None, None)
    gWq.__exit__(None, None, None)

    xnT = scA("xnT", [128, NKD, TOK])
    xnhT = scA("xnhT", [128, NKD, 3])
    gA2 = []

    def scA2(name, shape):
        g = nc.sbuf_tensor(name, shape, F32)
        gA2.append(g)
        return g.__enter__()

    xtok = scA2("xtok", [128, D])
    cstat = scA2("cstat", [128, 8])
    rs2c = scA2("rs2c", [128, 8])
    scr_sq = scA2("scr_sq", [128, D])
    xh_sb = scA2("xh_sb", [3, D])
    rs2t = scA2("rs2t", [8, 128])
    rs2h = scA2("rs2h", [3, 1])
    rs2h_row = scA2("rs2h_row", [1, 3])

    def f(scalar):
        wt(scalar, kv["consts"])
        act.inc(scalar.activation(negA[:], negA[:], AF.Exp), 1)
        kv["negA_e"] = act.v
    block.scalar(f)

    def f(vector):
        vector.wait_ge(act.sem, kv["negA_e"])
        vec.inc(vector.tensor_scalar_mul(negA[:], negA[:], -1.0), 1)
        vector.wait_ge(vec.sem, vec.v)
        vec.inc(vector.tensor_reduce(bqs[:], bq32[:], AX.X, ALU.add), 1)
        vector.wait_ge(vec.sem, vec.v)
        kv["smalls"] = vec.v
    block.vector(f)

    def f(tensor):
        wt(tensor, kv["consts"])
        ins = nc.tensor.matmul(ps[6][:, 0:64], on1[:], dvrow[:],
                               start=True, stop=True)
        mmc.inc(ins, 1)
        kv["db_mm"] = mmc.v
    block.tensor(f)

    def f(vector):
        vector.wait_ge(mmc.sem, kv["db_mm"])
        vec.inc(vector.tensor_copy(db64[:], ps[6][:, 0:64]), 1)
        vector.wait_ge(vec.sem, vec.v)
        kv["db"] = vec.v
    block.vector(f)

    def emit_ln(tag, xbuf, nrow, rs_out, out_fn, dma_key):
        def f(vector, tag=tag):
            wt(vector, kv[dma_key])
            mu = cstat[0:nrow, 0:1]
            sq = cstat[0:nrow, 1:2]
            vec.inc(vector.tensor_reduce(mu, xbuf[0:nrow, :], AX.X, ALU.add), 1)
            vector.wait_ge(vec.sem, vec.v)
            vec.inc(vector.tensor_mul(scr_sq[0:nrow, :], xbuf[0:nrow, :],
                                      xbuf[0:nrow, :]), 1)
            vector.wait_ge(vec.sem, vec.v)
            vec.inc(vector.tensor_reduce(sq, scr_sq[0:nrow, :], AX.X, ALU.add), 1)
            vector.wait_ge(vec.sem, vec.v)
            vec.inc(vector.tensor_scalar_mul(mu, mu, 1.0 / D), 1)
            vector.wait_ge(vec.sem, vec.v)
            vec.inc(vector.tensor_scalar_mul(sq, sq, 1.0 / D), 1)
            vector.wait_ge(vec.sem, vec.v)
            vec.inc(vector.tensor_mul(cstat[0:nrow, 2:3], mu, mu), 1)
            vector.wait_ge(vec.sem, vec.v)
            vec.inc(vector.tensor_tensor(sq, sq, cstat[0:nrow, 2:3], ALU.subtract), 1)
            vector.wait_ge(vec.sem, vec.v)
            vec.inc(vector.tensor_scalar_add(sq, sq, EPS), 1)
            vector.wait_ge(vec.sem, vec.v)
            kv[f"ln1_{tag}"] = vec.v
        block.vector(f)

        def f(scalar, tag=tag):
            scalar.wait_ge(vec.sem, kv[f"ln1_{tag}"])
            act.inc(scalar.activation(cstat[0:nrow, 1:2], cstat[0:nrow, 1:2],
                                      AF.Sqrt), 1)
            kv[f"lns_{tag}"] = act.v
        block.scalar(f)

        def f(vector, tag=tag):
            vector.wait_ge(act.sem, kv[f"lns_{tag}"])
            vec.inc(vector.reciprocal(cstat[0:nrow, 1:2], cstat[0:nrow, 1:2]), 1)
            vector.wait_ge(vec.sem, vec.v)
            vec.inc(vector.tensor_scalar(xbuf[0:nrow, :], xbuf[0:nrow, :],
                                         cstat[0:nrow, 0:1], cstat[0:nrow, 1:2],
                                         ALU.subtract, ALU.mult), 1)
            vector.wait_ge(vec.sem, vec.v)
            vec.inc(vector.tensor_mul(scr_sq[0:nrow, :], xbuf[0:nrow, :],
                                      xbuf[0:nrow, :]), 1)
            vector.wait_ge(vec.sem, vec.v)
            vec.inc(vector.tensor_reduce(cstat[0:nrow, 3:4], scr_sq[0:nrow, :],
                                         AX.X, ALU.add), 1)
            vector.wait_ge(vec.sem, vec.v)
            vec.inc(vector.tensor_scalar(cstat[0:nrow, 3:4], cstat[0:nrow, 3:4],
                                         1.0 / D, EPS, ALU.mult, ALU.add), 1)
            vector.wait_ge(vec.sem, vec.v)
            kv[f"ln2_{tag}"] = vec.v
        block.vector(f)

        def f(scalar, tag=tag):
            scalar.wait_ge(vec.sem, kv[f"ln2_{tag}"])
            act.inc(scalar.activation(cstat[0:nrow, 3:4], cstat[0:nrow, 3:4],
                                      AF.Sqrt), 1)
            kv[f"lns2_{tag}"] = act.v
        block.scalar(f)

        def f(vector, tag=tag):
            vector.wait_ge(act.sem, kv[f"lns2_{tag}"])
            vec.inc(vector.reciprocal(rs_out, cstat[0:nrow, 3:4]), 1)
            vector.wait_ge(vec.sem, vec.v)
            kv[f"lnr_{tag}"] = vec.v
        block.vector(f)

        for rnd in range(4):
            def f(tensor, tag=tag, rnd=rnd):
                if rnd == 0:
                    tensor.wait_ge(vec.sem, kv[f"ln2_{tag}"])
                else:
                    tensor.wait_ge(vec.sem, kv[f"lntr_ev_{tag}_{rnd - 1}"])
                for i in range(8):
                    k = rnd * 8 + i
                    bank = i // 2
                    slot = i % 2
                    ins = nc.tensor.transpose(
                        ps[bank][:, slot * 128:slot * 128 + nrow],
                        xbuf[0:nrow, k * 128:(k + 1) * 128],
                        id_sb[0:nrow, 0:nrow])
                mmc.inc(ins, 1)
                kv[f"lntr_{tag}_{rnd}"] = mmc.v
            block.tensor(f)

            def f(vector, tag=tag, rnd=rnd):
                vector.wait_ge(mmc.sem, kv[f"lntr_{tag}_{rnd}"])
                for i in range(8):
                    k = rnd * 8 + i
                    bank = i // 2
                    slot = i % 2
                    ins = vector.tensor_copy(
                        out_fn(k), ps[bank][:, slot * 128:slot * 128 + nrow])
                vec.inc(ins, 1)
                vector.wait_ge(vec.sem, vec.v)
                kv[f"lntr_ev_{tag}_{rnd}"] = vec.v
            block.vector(f)

    for tau in range(NT):
        def f(sync, tau=tau):
            if tau == 0:
                sync.wait_ge(vec.sem, kv["wqr3"])
                wt(sync, kv["wq_in"])
            sync.wait_ge(vec.sem, kv.get(f"lntr_ev_{tau - 1}_3", 0))
            dm(sync, out=xtok[:],
                                 in_=x_h[tau * 128:(tau + 1) * 128, :])
            kv[f"xld_{tau}"] = seal()
        block.sync(f)

        emit_ln(tau, xtok[:], 128, rs2c[:, tau:tau + 1],
                (lambda k, tau=tau: xnT[:, k, tau * 128:(tau + 1) * 128]),
                f"xld_{tau}")

    def f(sync):
        dm(sync, out=xh_sb[:], in_=xh_h)
        kv["xld_h"] = seal()
    block.sync(f)

    emit_ln("h", xh_sb[:], 3, rs2h[:], (lambda k: xnhT[:, k, :]), "xld_h")

    def f(tensor):
        tensor.wait_ge(vec.sem, kv["lnr_7"])
        mmc.inc(nc.tensor.transpose(ps[4][0:8, 0:128], rs2c[:], id_sb[:]), 1)
        kv["rs2_tr"] = mmc.v
    block.tensor(f)

    def f(vector):
        vector.wait_ge(mmc.sem, kv["rs2_tr"])
        vec.inc(vector.tensor_copy(rs2t[:], ps[4][0:8, 0:128]), 1)
        vector.wait_ge(vec.sem, vec.v)
        kv["rs2_ev"] = vec.v
    block.vector(f)

    def f(sync):
        sync.wait_ge(vec.sem, kv["rs2_ev"])
        dm(sync, out=rs2_scr, in_=rs2t[:])
        wt(sync, TK[-1:])
        dm(sync, 
            out=rs2row[:],
            in_=rs2_scr.rearrange("a b -> (a b)").unsqueeze(0))
        kv["rs2row"] = seal()
    block.sync(f)

    def f(tensor):
        tensor.wait_ge(vec.sem, kv["lnr_h"])
        tensor.wait_ge(vec.sem, kv["rs2_ev"])
        mmc.inc(nc.tensor.transpose(ps[4][0:1, 0:3], rs2h[:], id_sb[0:3, 0:3]), 1)
        kv["rs2h_tr"] = mmc.v
    block.tensor(f)

    def f(vector):
        vector.wait_ge(mmc.sem, kv["rs2h_tr"])
        vec.inc(vector.tensor_copy(rs2h_row[:], ps[4][0:1, 0:3]), 1)
        vector.wait_ge(vec.sem, vec.v)
        kv["rs2h_row"] = vec.v
    block.vector(f)

    def f(tensor):
        tensor.wait_ge(vec.sem, kv["rs2h_row"])
        ins = nc.tensor.matmul(ps[6][:, 64:67], on1[:], rs2h_row[:],
                               start=True, stop=True)
        mmc.inc(ins, 1)
        kv["rs2h_mm"] = mmc.v
    block.tensor(f)

    def f(vector):
        vector.wait_ge(mmc.sem, kv["rs2h_mm"])
        vec.inc(vector.tensor_copy(rs2hb[:], ps[6][:, 64:67]), 1)
        vector.wait_ge(vec.sem, vec.v)
        for k in range(NKD):
            ins = vector.tensor_scalar_mul(xnhT[:, k, :], xnhT[:, k, :],
                                           lnw[:, k:k + 1])
        vec.inc(ins, 1)
        vector.wait_ge(vec.sem, vec.v)
        kv["xnhT_done"] = vec.v
    block.vector(f)

    for g in reversed(gA2):
        g.__exit__(None, None, None)

    wbig = scA("wbig", [128, 8192])
    wla = wbig[:].rearrange("p (k c) -> p k c", c=256)
    wbuf = wbig[:].rearrange("p (b k c) -> p b k c", b=2, c=128)
    h_sb = scA("h_sb", [128, 2, TOK])
    hg_sb = h_sb
    gB1 = []

    def scB1(name, shape):
        g = nc.sbuf_tensor(name, shape, F32)
        gB1.append(g)
        return g.__enter__()

    wgate = scB1("wgate", [128, NKD, 28])
    logits = scB1("logits", [28, TOK])
    lgT = scB1("lgT", [128, NT, 28])
    gstat = scB1("gstat", [128, 8])
    gmask = scB1("gmask", [128, 2, 28])
    gateT = scB1("gateT", [128, NT, 28])
    gate_f = scB1("gate_f", [28, TOK])

    def f(sync):
        wt(sync, kv["rs2row"])
        sync.wait_ge(mmc.sem, kv["rs2h_mm"])
        sync.wait_ge(vec.sem, kv["xnhT_done"])
        dm(sync, out=wgate[:], in_=w_gate_h.transpose([1, 0, 2]))
        dm(sync, out=wla, in_=w_la_h.transpose([1, 0, 2]))
        if DEBUG:
            sync.wait_ge(vec.sem, kv["lntr_ev_7_3"])
            wt(sync, kv["rs2row"])
            for k in range(NKD):
                dm(sync, out=dbg_xnT[k], in_=xnT[:, k, :])
            dm(sync, out=dbg_r2, in_=rs2row[:])
            dm(sync, out=dbg_r2t, in_=rs2t[:])
            dm(sync, out=dbg_r2c, in_=rs2c[:])
        kv["b1_w"] = seal()
    block.sync(f)

    for half in range(2):
        def f(tensor, half=half):
            wt(tensor, kv["b1_w"])
            if half == 0:
                tensor.wait_ge(vec.sem, kv["lntr_ev_7_3"])
            else:
                tensor.wait_ge(vec.sem, kv["b1_gev_0"])
            for k in range(NKD):
                ins = nc.tensor.matmul(
                    ps[0][0:28, 0:512], wgate[:, k, :],
                    xnT[:, k, half * 512:(half + 1) * 512],
                    start=(k == 0), stop=(k == NKD - 1))
            mmc.inc(ins, 1)
            kv[f"b1_gmm_{half}"] = mmc.v
        block.tensor(f)

        def f(vector, half=half):
            vector.wait_ge(mmc.sem, kv[f"b1_gmm_{half}"])
            ins = vector.tensor_copy(logits[:, half * 512:(half + 1) * 512],
                                     ps[0][0:28, 0:512])
            vec.inc(ins, 1)
            vector.wait_ge(vec.sem, vec.v)
            kv[f"b1_gev_{half}"] = vec.v
        block.vector(f)

    for t in range(NT):
        def f(tensor, t=t):
            if t == 0:
                tensor.wait_ge(vec.sem, kv["b1_gev_1"])
            else:
                tensor.wait_ge(vec.sem, kv[f"b1_ltr_ev_{t - 1}"])
            ins = nc.tensor.transpose(ps[1][0:128, 0:28],
                                      logits[:, t * 128:(t + 1) * 128],
                                      id_sb[0:28, 0:28])
            mmc.inc(ins, 1)
            kv[f"b1_ltr_{t}"] = mmc.v
        block.tensor(f)

        def f(vector, t=t):
            vector.wait_ge(mmc.sem, kv[f"b1_ltr_{t}"])
            vec.inc(vector.tensor_copy(lgT[:, t, :], ps[1][0:128, 0:28]), 1)
            vector.wait_ge(vec.sem, vec.v)
            kv[f"b1_ltr_ev_{t}"] = vec.v
        block.vector(f)

    for t in range(NT):
        def f(vector, t=t):
            vector.wait_ge(vec.sem, kv[f"b1_ltr_ev_{t}"])
            lt = lgT[:, t, :]
            m1 = gstat[:, 0:1]
            m2 = gstat[:, 1:2]
            vec.inc(vector.reduce_max(m1, lt, AX.X), 1)
            vector.wait_ge(vec.sem, vec.v)
            vec.inc(vector.tensor_tensor(gmask[:, 0, :], lt,
                                         m1.broadcast_to((128, 28)), ALU.is_ge), 1)
            vector.wait_ge(vec.sem, vec.v)
            vec.inc(vector.scalar_tensor_tensor(gmask[:, 1, :], gmask[:, 0, :],
                                                -1e30, lt, ALU.mult, ALU.add), 1)
            vector.wait_ge(vec.sem, vec.v)
            vec.inc(vector.reduce_max(m2, gmask[:, 1, :], AX.X), 1)
            vector.wait_ge(vec.sem, vec.v)
            vec.inc(vector.tensor_tensor(gmask[:, 1, :], gmask[:, 1, :],
                                         m2.broadcast_to((128, 28)), ALU.is_ge), 1)
            vector.wait_ge(vec.sem, vec.v)
            vec.inc(vector.tensor_scalar_mul(m2, m2, -1.0), 1)
            vector.wait_ge(vec.sem, vec.v)
            kv[f"b1_t2a_{t}"] = vec.v
        block.vector(f)

        def f(scalar, t=t):
            scalar.wait_ge(vec.sem, kv[f"b1_t2a_{t}"])
            ins = scalar.activation(gstat[:, 2:3], gstat[:, 0:1], AF.Sigmoid,
                                    bias=gstat[:, 1:2])
            act.inc(ins, 1)
            kv[f"b1_t2b_{t}"] = act.v
        block.scalar(f)

        def f(vector, t=t):
            vector.wait_ge(act.sem, kv[f"b1_t2b_{t}"])
            vec.inc(vector.tensor_scalar(gstat[:, 3:4], gstat[:, 2:3], -1.0, 1.0,
                                         ALU.mult, ALU.add), 1)
            vector.wait_ge(vec.sem, vec.v)
            vec.inc(vector.tensor_scalar_mul(gmask[:, 0, :], gmask[:, 0, :],
                                             gstat[:, 2:3]), 1)
            vector.wait_ge(vec.sem, vec.v)
            vec.inc(vector.scalar_tensor_tensor(gateT[:, t, :], gmask[:, 1, :],
                                                gstat[:, 3:4], gmask[:, 0, :],
                                                ALU.mult, ALU.add), 1)
            vector.wait_ge(vec.sem, vec.v)
            kv[f"b1_t2c_{t}"] = vec.v
        block.vector(f)

    for t in range(NT):
        def f(tensor, t=t):
            tensor.wait_ge(vec.sem, kv[f"b1_t2c_{t}"])
            if t >= 1:
                tensor.wait_ge(vec.sem, kv[f"b1_gtr_ev_{t - 1}"])
            ins = nc.tensor.transpose(ps[1][0:28, 0:128], gateT[:, t, :], id_sb[:])
            mmc.inc(ins, 1)
            kv[f"b1_gtr_{t}"] = mmc.v
        block.tensor(f)

        def f(vector, t=t):
            vector.wait_ge(mmc.sem, kv[f"b1_gtr_{t}"])
            ins = vector.tensor_copy(gate_f[:, t * 128:(t + 1) * 128],
                                     ps[1][0:28, 0:128])
            vec.inc(ins, 1)
            vector.wait_ge(vec.sem, vec.v)
            kv[f"b1_gtr_ev_{t}"] = vec.v
        block.vector(f)

    def f(tensor):
        tensor.wait_ge(vec.sem, kv["b1_gtr_ev_7"])
        for mh in range(2):
            for half in range(2):
                bank = mh * 2 + half
                for k in range(NKD):
                    ins = nc.tensor.matmul(
                        ps[bank][:, 0:512], wla[:, k, mh * 128:(mh + 1) * 128],
                        xnT[:, k, half * 512:(half + 1) * 512],
                        start=(k == 0), stop=(k == NKD - 1))
                mmc.inc(ins, 1)
        kv["b1_hmm"] = mmc.v
    block.tensor(f)

    def f(vector):
        vector.wait_ge(mmc.sem, kv["b1_hmm"])
        for mh in range(2):
            for half in range(2):
                bank = mh * 2 + half
                ins = vector.tensor_copy(h_sb[:, mh, half * 512:(half + 1) * 512],
                                         ps[bank][:, 0:512])
                vec.inc(ins, 1)
                vector.wait_ge(vec.sem, vec.v)
        kv["b1_hev"] = vec.v
    block.vector(f)

    def f(tensor):
        tensor.wait_ge(vec.sem, kv["b1_hev"])
        for mh in range(2):
            for half in range(2):
                bank = mh * 2 + half
                ins = nc.tensor.matmul(ps[bank][:, 0:512],
                                       eselg[:, mh * 128:(mh + 1) * 128],
                                       gate_f[:, half * 512:(half + 1) * 512],
                                       start=True, stop=True)
                mmc.inc(ins, 1)
        kv["b1_gexp"] = mmc.v
    block.tensor(f)

    def f(vector):
        vector.wait_ge(mmc.sem, kv["b1_gexp"])
        for mh in range(2):
            for half in range(2):
                bank = mh * 2 + half
                ins = vector.tensor_tensor(
                    hg_sb[:, mh, half * 512:(half + 1) * 512],
                    h_sb[:, mh, half * 512:(half + 1) * 512],
                    ps[bank][:, 0:512], ALU.mult)
                vec.inc(ins, 1)
                vector.wait_ge(vec.sem, vec.v)
        kv["b1_hg"] = vec.v
    block.vector(f)

    if DEBUG:
        def f(sync):
            sync.wait_ge(vec.sem, kv["b1_hg"])
            dm(sync, out=dbg_h[0], in_=h_sb[:, 0, :])
            dm(sync, out=dbg_h[1], in_=h_sb[:, 1, :])
            dm(sync, out=dbg_gf, in_=gate_f[:])
            dm(sync, out=dbg_lg, in_=logits[:])
            dm(sync, out=dbg_lgT, in_=lgT[:])
            dm(sync, out=dbg_gT, in_=gateT[:])
        block.sync(f)

    for g in reversed(gB1):
        g.__exit__(None, None, None)
    gB2 = []

    def scB2(name, shape):
        g = nc.sbuf_tensor(name, shape, F32)
        gB2.append(g)
        return g.__enter__()

    wlbm = scB2("wlbm", [128, 2, 2, 128])
    ffn_ev = scB2("ffn_ev", [128, 2, TOK])

    for m in range(NKD):
        b = m % 2

        def f(sync, m=m, b=b):
            if m >= 2:
                sync.wait_ge(mmc.sem, kv[f"b2_mm_{m - 2}"])
            else:
                sync.wait_ge(mmc.sem, kv["b1_gexp"])
                wt(sync, dpool.all_tokens())
            dm(sync, out=wbuf[:, b], in_=w_base_h[m])
            dm(sync, out=wlbm[:, b],
                                   in_=w_lb_h[m].transpose([1, 0, 2]))
            kv[f"b2_w_{m}"] = seal()
        block.sync(f)

        def f(tensor, m=m, b=b):
            if m == 0:
                tensor.wait_ge(vec.sem, kv["b1_hg"])
            wt(tensor, kv[f"b2_w_{m}"])
            if m >= 2:
                tensor.wait_ge(vec.sem, kv[f"b2_ev_{m - 2}"])
            for half in range(2):
                bank = b * 2 + half
                for k in range(NKD):
                    nc.tensor.matmul(ps[bank][:, 0:512], wbuf[:, b, k, :],
                                     xnT[:, k, half * 512:(half + 1) * 512],
                                     start=(k == 0), stop=False)
                for k2 in range(2):
                    ins = nc.tensor.matmul(
                        ps[bank][:, 0:512], wlbm[:, b, k2, :],
                        hg_sb[:, k2, half * 512:(half + 1) * 512],
                        start=False, stop=(k2 == 1))
                mmc.inc(ins, 1)
            kv[f"b2_mm_{m}"] = mmc.v
        block.tensor(f)

        def f(vector, m=m, b=b):
            vector.wait_ge(mmc.sem, kv[f"b2_mm_{m}"])
            if m >= 2:
                wt(vector, kv[f"b2_st_{m - 2}"])
            for half in range(2):
                bank = b * 2 + half
                ins = vector.tensor_copy(ffn_ev[:, b, half * 512:(half + 1) * 512],
                                         ps[bank][:, 0:512])
                vec.inc(ins, 1)
                vector.wait_ge(vec.sem, vec.v)
            kv[f"b2_ev_{m}"] = vec.v
        block.vector(f)

        def f(sync, m=m, b=b):
            sync.wait_ge(vec.sem, kv[f"b2_ev_{m}"])
            dm(sync, out=ffn_scr[m], in_=ffn_ev[:, b])
            kv[f"b2_st_{m}"] = seal()
        block.sync(f)

    for g in reversed(gB2):
        g.__exit__(None, None, None)
    gB3 = []

    def scB3(name, shape):
        g = nc.sbuf_tensor(name, shape, F32)
        gB3.append(g)
        return g.__enter__()

    zx_ev = scB3("zx_ev", [128, 2, TOK])
    hstat = scB3("hstat", [128, 3])
    rs2b = scB3("rs2b", [128, TOK])

    def f(vector):
        vector.wait_ge(mmc.sem, kv["b2_mm_31"])
        for k in range(NKD):
            ins = vector.tensor_scalar_mul(xnT[:, k, :], xnT[:, k, :],
                                           lnw[:, k:k + 1])
            vec.inc(ins, 1)
            vector.wait_ge(vec.sem, vec.v)
        kv["b3_lnw"] = vec.v
    block.vector(f)

    def f(tensor):
        wt(tensor, kv["rs2row"])
        tensor.wait_ge(vec.sem, kv["b3_lnw"])
        for half in range(2):
            ins = nc.tensor.matmul(ps[4 + half][:, 0:512], on1[:],
                                   rs2row[0:1, half * 512:(half + 1) * 512],
                                   start=True, stop=True)
            mmc.inc(ins, 1)
        kv["b3_rs2b"] = mmc.v
    block.tensor(f)

    def f(vector):
        vector.wait_ge(mmc.sem, kv["b3_rs2b"])
        wt(vector, kv["b2_st_31"])
        wt(vector, kv["b2_st_30"])
        for half in range(2):
            ins = vector.tensor_copy(rs2b[:, half * 512:(half + 1) * 512],
                                     ps[4 + half][:, 0:512])
            vec.inc(ins, 1)
            vector.wait_ge(vec.sem, vec.v)
        kv["b3_rs2bc"] = vec.v
    block.vector(f)

    for m in range(MIN):
        b = m % 2

        def f(sync, m=m, b=b):
            if m >= 2:
                sync.wait_ge(mmc.sem, kv[f"b3_mm_{m - 2}"])
            else:
                sync.wait_ge(mmc.sem, kv["b2_mm_31"])
            dm(sync, out=wbuf[:, b], in_=w_in_h[m])
            kv[f"b3_w_{m}"] = seal()
        block.sync(f)

        def f(tensor, m=m, b=b):
            if m == 0:
                tensor.wait_ge(vec.sem, kv["b3_rs2bc"])
                tensor.wait_ge(vec.sem, kv["xnhT_done"])
            wt(tensor, kv[f"b3_w_{m}"])
            if m >= 2:
                tensor.wait_ge(vec.sem, kv[f"b3_ev_{m - 2}"])
            for half in range(2):
                bank = b * 2 + half
                for k in range(NKD):
                    ins = nc.tensor.matmul(
                        ps[bank][:, 0:512], wbuf[:, b, k, :],
                        xnT[:, k, half * 512:(half + 1) * 512],
                        start=(k == 0), stop=(k == NKD - 1))
                mmc.inc(ins, 1)
            if 32 <= m < 66:
                if m >= 33:
                    tensor.wait_ge(vec.sem, kv[f"b3_ev_{m - 1}"])
                for k in range(NKD):
                    ins = nc.tensor.matmul(
                        ps[6][:, 128:131], wbuf[:, b, k, :], xnhT[:, k, :],
                        start=(k == 0), stop=(k == NKD - 1))
                mmc.inc(ins, 1)
            kv[f"b3_mm_{m}"] = mmc.v
        block.tensor(f)

        def f(vector, m=m, b=b):
            vector.wait_ge(mmc.sem, kv[f"b3_mm_{m}"])
            if m >= 2:
                wt(vector, kv[f"b3_st_{m - 2}"])
            else:
                wt(vector, kv["b2_st_31"])
            for half in range(2):
                bank = b * 2 + half
                ins = vector.scalar_tensor_tensor(
                    zx_ev[:, b, half * 512:(half + 1) * 512],
                    ps[bank][:, 0:512], mupv[:, m:m + 1],
                    rs2b[:, half * 512:(half + 1) * 512], ALU.mult, ALU.mult)
                vec.inc(ins, 1)
                vector.wait_ge(vec.sem, vec.v)
            if 32 <= m < 66:
                ins = vector.tensor_tensor(hstat[:, 0:3], ps[6][:, 128:131],
                                           rs2hb[:], ALU.mult)
                vec.inc(ins, 1)
                vector.wait_ge(vec.sem, vec.v)
                ins = vector.tensor_scalar_mul(halo_my[:, m - 32, :],
                                               hstat[:, 0:3], mupv[:, m:m + 1])
                vec.inc(ins, 1)
                vector.wait_ge(vec.sem, vec.v)
            kv[f"b3_ev_{m}"] = vec.v
        block.vector(f)

        def f(sync, m=m, b=b):
            sync.wait_ge(vec.sem, kv[f"b3_ev_{m}"])
            dm(sync, out=zx_scr[m], in_=zx_ev[:, b])
            kv[f"b3_st_{m}"] = seal()
        block.sync(f)

    barrier()
    for g in reversed(gB3):
        g.__exit__(None, None, None)
    for g in reversed(gA):
        g.__exit__(None, None, None)

    if PHASE_LIMIT < 2:
        block_cm.__exit__(None, None, None)
        return nc

    gC = []

    def scC(name, shape):
        g = nc.sbuf_tensor(name, shape, F32)
        gC.append(g)
        return g.__enter__()

    h_run = scC("h_run", [128, D])
    conv_in = scC("conv_in", [128, 2, NCV, 131])
    sx = scC("sx", [128, NCV, 128])
    xsT = scC("xsT", [128, NKD, 128])
    btq = scC("btq", [128, 128])
    sdt = scC("sdt", [128, 128])
    sdtT = scC("sdtT", [128, 128])
    dtmp = scC("dtmp", [64, 128])
    dtr = scC("dtr", [64, 128])
    ones64 = scC("ones64", [64, 128])
    srow = scC("srow", [1, 8192])
    ytok = scC("ytok", [128, 2, D])
    dsb = scC("dsb", [128, 2, 512])
    wtsb = scC("wtsb", [128, 2, 512])
    w4 = scC("w4", [128, 2, 8])
    dtxw = scC("dtxw", [128, 2, 256])
    t1sb = scC("t1sb", [128, 256])

    def f(vector):
        vec.inc(vector.memset(ones64[:], 1.0), 1)
        vector.wait_ge(vec.sem, vec.v)
        vec.inc(vector.memset(h_run[:], 0.0), 1)
        vector.wait_ge(vec.sem, vec.v)
        kv["c_init"] = vec.v
    block.vector(f)

    for q in range(NQ):
        cb = q % 2

        def f(sync, q=q, cb=cb):
            if q >= 2:
                sync.wait_ge(act.sem, kv[f"c_silu_{q - 2}"])
            if q >= 1:
                sync.wait_ge(vec.sem, kv[f"c_S_{q - 1}"])
            c0 = q * 128 - 3
            for g in range(4):
                j0, j1 = g * 9, min((g + 1) * 9, NCV)
                if q == 0:
                    dm(sync,
                        out=conv_in[:, cb, j0:j1, 3:131],
                        in_=zx_scr[32 + j0:32 + j1, :, 0:128].transpose([1, 0, 2]))
                else:
                    dm(sync, 
                        out=conv_in[:, cb, j0:j1, :],
                        in_=zx_scr[32 + j0:32 + j1, :, c0:c0 + 131].transpose([1, 0, 2]))
            dm(sync, out=dtmp[:],
                                 in_=zx_scr[66, 0:64, q * 128:(q + 1) * 128])
            kv[f"c_ld_{q}"] = seal()
        block.sync(f)

        def f(vector, q=q, cb=cb):
            wt(vector, kv[f"c_ld_{q}"])
            if q == 0:
                for j in range(NCV):
                    vector.tensor_copy(conv_in[:, cb, j, 0:3], halo_my[:, j, :])
            if q >= 1:
                vector.wait_ge(act.sem, kv[f"c_silu_{q - 1}"])
                vector.wait_ge(vec.sem, kv[f"c_trev_{q - 1}_B"])
                vector.wait_ge(mmc.sem, kv[f"c_GT_{q - 1}"])
            for j in range(NCV):
                vector.tensor_scalar_mul(sx[:, j, :], conv_in[:, cb, j, 0:128],
                                         convw[:, j, 0:1])
                for kk in range(1, 4):
                    ins = vector.scalar_tensor_tensor(
                        sx[:, j, :], conv_in[:, cb, j, kk:kk + 128],
                        convw[:, j, kk:kk + 1], sx[:, j, :], ALU.mult, ALU.add)
            vec.inc(ins, 1)
            vector.wait_ge(vec.sem, vec.v)
            kv[f"c_conv_{q}"] = vec.v
            ins = vector.tensor_scalar(sdt[64:128, :], dtmp[:], dtb[:, 0:1], None,
                                       ALU.add)
            vec.inc(ins, 1)
            vector.wait_ge(vec.sem, vec.v)
            ins = vector.tensor_scalar_max(dtr[:], sdt[64:128, :], 0.0)
            vec.inc(ins, 1)
            vector.wait_ge(vec.sem, vec.v)
            kv[f"c_dt1_{q}"] = vec.v
        block.vector(f)

        def f(scalar, q=q):
            scalar.wait_ge(vec.sem, kv[f"c_conv_{q}"])
            ins = scalar.activation(sx[:].rearrange("p a b -> p (a b)"),
                                    sx[:].rearrange("p a b -> p (a b)"), AF.Silu)
            act.inc(ins, 1)
            kv[f"c_silu_{q}"] = act.v
            act.inc(scalar.activation(dtmp[:], sdt[64:128, :], AF.Abs), 1)
            act.inc(scalar.activation(dtmp[:], dtmp[:], AF.Exp, scale=-1.0), 1)
            act.inc(scalar.activation(dtmp[:], dtmp[:], AF.Ln, bias=1.0), 1)
            kv[f"c_dt2_{q}"] = act.v
        block.scalar(f)

        def f(vector, q=q):
            vector.wait_ge(act.sem, kv[f"c_dt2_{q}"])
            vec.inc(vector.tensor_add(sdt[64:128, :], dtr[:], dtmp[:]), 1)
            vector.wait_ge(vec.sem, vec.v)
            vec.inc(vector.tensor_scalar_mul(dtmp[:], sdt[64:128, :], negA[:, 0:1]), 1)
            vector.wait_ge(vec.sem, vec.v)
            ins = vector.tensor_tensor_scan(sdt[0:64, :], dtmp[:], ones64[:],
                                            0.0, ALU.add, ALU.mult)
            vec.inc(ins, 1)
            vector.wait_ge(vec.sem, vec.v)
            kv[f"c_S_{q}"] = vec.v
        block.vector(f)

        for rnd in range(4):
            def f(tensor, q=q, rnd=rnd):
                if rnd == 0:
                    tensor.wait_ge(act.sem, kv[f"c_silu_{q}"])
                    if q >= 1:
                        tensor.wait_ge(vec.sem, kv[f"c_hupd_{q - 1}_15"])
                else:
                    tensor.wait_ge(vec.sem, kv[f"c_trev_{q}_{rnd - 1}"])
                for i in range(8):
                    j = rnd * 8 + i
                    bank = i // 2
                    slot = i % 2
                    ins = nc.tensor.transpose(
                        ps[bank][:, slot * 128:(slot + 1) * 128],
                        sx[:, j, :], id_sb[:])
                mmc.inc(ins, 1)
                kv[f"c_tr_{q}_{rnd}"] = mmc.v
            block.tensor(f)

            def f(vector, q=q, rnd=rnd):
                vector.wait_ge(mmc.sem, kv[f"c_tr_{q}_{rnd}"])
                for i in range(8):
                    j = rnd * 8 + i
                    bank = i // 2
                    slot = i % 2
                    ins = vector.tensor_copy(xsT[:, j, :],
                                             ps[bank][:, slot * 128:(slot + 1) * 128])
                vec.inc(ins, 1)
                vector.wait_ge(vec.sem, vec.v)
                kv[f"c_trev_{q}_{rnd}"] = vec.v
            block.vector(f)

        def f(tensor, q=q):
            tensor.wait_ge(vec.sem, kv[f"c_trev_{q}_3"])
            mmc.inc(nc.tensor.transpose(ps[0][:, 0:128], sx[:, 32, :], id_sb[:]), 1)
            tensor.wait_ge(vec.sem, kv[f"c_S_{q}"])
            mmc.inc(nc.tensor.transpose(ps[1][:, 0:128], sdt[:], id_sb[:]), 1)
            kv[f"c_trBS_{q}"] = mmc.v
        block.tensor(f)

        def f(vector, q=q):
            vector.wait_ge(mmc.sem, kv[f"c_trBS_{q}"])
            vec.inc(vector.tensor_copy(btq[:], ps[0][:, 0:128]), 1)
            vector.wait_ge(vec.sem, vec.v)
            vec.inc(vector.tensor_copy(sdtT[:], ps[1][:, 0:128]), 1)
            vector.wait_ge(vec.sem, vec.v)
            vec.inc(vector.tensor_copy(cm_all[:, q, :], sx[:, 33, :]), 1)
            vector.wait_ge(vec.sem, vec.v)
            kv[f"c_trev_{q}_B"] = vec.v
        block.vector(f)

        def f(tensor, q=q):
            tensor.wait_ge(vec.sem, kv[f"c_trev_{q}_B"])
            ins = nc.tensor.matmul(ps[7][:, 0:128], sx[:, 32, :], sx[:, 33, :],
                                   start=True, stop=True)
            mmc.inc(ins, 1)
            kv[f"c_GT_{q}"] = mmc.v
        block.tensor(f)

        def f(sync, q=q):
            sync.wait_ge(vec.sem, kv[f"c_S_{q}"])
            if q >= 1:
                sync.wait_ge(mmc.sem, kv[f"c_srb_{q - 1}_15"])
            dm(sync, out=s_scr[q], in_=sdt[0:64, :])
            wt(sync, TK[-1:])
            dm(sync, 
                out=srow[:], in_=s_scr[q].rearrange("a b -> (a b)").unsqueeze(0))
            kv[f"c_srow_{q}"] = seal()
        block.sync(f)

        def f(scalar, q=q):
            scalar.wait_ge(vec.sem, kv[f"c_trev_{q}_B"])
            act.inc(scalar.activation(e_all[:, q, :], sdtT[:, 0:64], AF.Exp), 1)
            wt(scalar, kv[f"c_srow_{q}"])
            sl_row = srow[0:1, :].rearrange("a (h t) -> a h t", t=128)[:, :, 127]
            act.inc(scalar.activation(lamr[:, q, :], sl_row, AF.Exp), 1)
            kv[f"c_E_{q}"] = act.v
        block.scalar(f)

        def f(tensor, q=q):
            tensor.wait_ge(act.sem, kv[f"c_E_{q}"])
            sl_row = srow[0:1, :].rearrange("a (h t) -> a h t", t=128)[:, :, 127]
            ins = nc.tensor.matmul(ps[7][:, 128:192], on1[:], sl_row,
                                   start=True, stop=True)
            mmc.inc(ins, 1)
            ins = nc.tensor.matmul(ps[7][:, 192:256], on1[:], lamr[:, q, :],
                                   start=True, stop=True)
            mmc.inc(ins, 1)
            kv[f"c_SLb_{q}"] = mmc.v
        block.tensor(f)

        if DEBUG:
            def f(sync, q=q):
                sync.wait_ge(act.sem, kv[f"c_E_{q}"])
                dm(sync, out=dbg_sdt[q], in_=sdt[:])
                dm(sync, out=dbg_srow[q:q+1, :], in_=srow[0:1, :])
                kv[f"c_dbg_{q}"] = seal()
            block.sync(f)

        xsflat = xsT[:].rearrange("p a b -> p (a b)")
        for hg in range(16):
            h0 = hg * 4
            ab = hg % 2

            def f(tensor, q=q, hg=hg, h0=h0, ab=ab):
                if hg == 0:
                    tensor.wait_ge(mmc.sem, kv[f"c_SLb_{q}"])
                if hg >= 1:
                    tensor.wait_ge(vec.sem, kv[f"c_D_{q}_{hg - 1}"])
                ins = nc.tensor.matmul(ps[6][:, 0:512], on1[:],
                                       srow[0:1, h0 * 128:(h0 + 4) * 128],
                                       start=True, stop=True)
                mmc.inc(ins, 1)
                kv[f"c_srb_{q}_{hg}"] = mmc.v
            block.tensor(f)

            def f(vector, q=q, hg=hg, h0=h0, ab=ab):
                vector.wait_ge(mmc.sem, kv[f"c_srb_{q}_{hg}"])
                if hg >= 2:
                    vector.wait_ge(act.sem, kv[f"c_expD_{q}_{hg - 2}"])
                d3 = dsb[:, ab, :].rearrange("p (h t) -> p h t", t=128)
                sr3 = ps[6][:, 0:512].rearrange("p (h t) -> p h t", t=128)
                stc = sdtT[:, h0:h0 + 4].unsqueeze(2).broadcast_to((128, 4, 128))
                vec.inc(vector.tensor_tensor(d3, sr3, stc, ALU.subtract), 1)
                vector.wait_ge(vec.sem, vec.v)
                mn3 = mneg[:].unsqueeze(1).broadcast_to((128, 4, 128))
                vec.inc(vector.tensor_tensor(d3, d3, mn3, ALU.add), 1)
                vector.wait_ge(vec.sem, vec.v)
                ins = vector.tensor_tensor(w4[:, ab, 0:4],
                                           ps[7][:, 128 + h0:128 + h0 + 4],
                                           sdtT[:, h0:h0 + 4], ALU.subtract)
                vec.inc(ins, 1)
                vector.wait_ge(vec.sem, vec.v)
                kv[f"c_D_{q}_{hg}"] = vec.v
            block.vector(f)

            def f(scalar, q=q, hg=hg, ab=ab):
                scalar.wait_ge(vec.sem, kv[f"c_D_{q}_{hg}"])
                act.inc(scalar.activation(dsb[:, ab, :], dsb[:, ab, :], AF.Exp), 1)
                act.inc(scalar.activation(w4[:, ab, 4:8], w4[:, ab, 0:4], AF.Exp), 1)
                kv[f"c_expD_{q}_{hg}"] = act.v
            block.scalar(f)

            def f(vector, q=q, hg=hg, h0=h0, ab=ab):
                vector.wait_ge(act.sem, kv[f"c_expD_{q}_{hg}"])
                if hg >= 2:
                    vector.wait_ge(mmc.sem, kv[f"c_hgmm_{q}_{hg - 2}"])
                w3 = wtsb[:, ab, :].rearrange("p (h t) -> p h t", t=128)
                e3 = dsb[:, ab, :].rearrange("p (h t) -> p h t", t=128)
                dtc = sdtT[:, 64 + h0:64 + h0 + 4].unsqueeze(2) \
                    .broadcast_to((128, 4, 128))
                vec.inc(vector.tensor_tensor(w3, e3, dtc, ALU.mult), 1)
                vector.wait_ge(vec.sem, vec.v)
                gt3 = ps[7][:, 0:128].unsqueeze(1).broadcast_to((128, 4, 128))
                vec.inc(vector.tensor_tensor(w3, w3, gt3, ALU.mult), 1)
                vector.wait_ge(vec.sem, vec.v)
                vec.inc(vector.tensor_tensor(w4[:, ab, 0:4], w4[:, ab, 4:8],
                                             sdtT[:, 64 + h0:64 + h0 + 4],
                                             ALU.mult), 1)
                vector.wait_ge(vec.sem, vec.v)
                dx3 = dtxw[:, ab, :].rearrange("p (h d) -> p h d", d=64)
                xs3 = xsflat[:, h0 * 64:(h0 + 4) * 64] \
                    .rearrange("p (h d) -> p h d", d=64)
                wb3 = w4[:, ab, 0:4].unsqueeze(2).broadcast_to((128, 4, 64))
                vec.inc(vector.tensor_tensor(dx3, xs3, wb3, ALU.mult), 1)
                vector.wait_ge(vec.sem, vec.v)
                kv[f"c_WT_{q}_{hg}"] = vec.v
            block.vector(f)

            def f(tensor, q=q, hg=hg, h0=h0, ab=ab):
                tensor.wait_ge(vec.sem, kv[f"c_WT_{q}_{hg}"])
                if hg >= 2:
                    tensor.wait_ge(vec.sem, kv[f"c_hupd_{q}_{hg - 2}"])
                for i in range(4):
                    h = h0 + i
                    nc.tensor.matmul(ps[0 + ab][:, i * 64:(i + 1) * 64],
                                     wtsb[:, ab, i * 128:(i + 1) * 128],
                                     xsflat[:, h * 64:(h + 1) * 64],
                                     start=True, stop=True)
                    nc.tensor.matmul(ps[4 + ab][:, i * 64:(i + 1) * 64],
                                     cm_all[:, q, :],
                                     h_run[:, h * 64:(h + 1) * 64],
                                     start=True, stop=True)
                    ins = nc.tensor.matmul(ps[2 + ab][:, i * 64:(i + 1) * 64],
                                           btq[:],
                                           dtxw[:, ab, i * 64:(i + 1) * 64],
                                           start=True, stop=True)
                mmc.inc(ins, 1)
                kv[f"c_hgmm_{q}_{hg}"] = mmc.v
            block.tensor(f)

            def f(vector, q=q, hg=hg, h0=h0, ab=ab, cb=cb):
                vector.wait_ge(mmc.sem, kv[f"c_hgmm_{q}_{hg}"])
                if hg == 0 and q >= 2:
                    wt(vector, kv[f"c_yst_{q - 2}"])
                t13 = t1sb[:].rearrange("p (h d) -> p h d", d=64)
                pb3 = ps[4 + ab][:, 0:256].rearrange("p (h d) -> p h d", d=64)
                e3 = e_all[:, q, h0:h0 + 4].unsqueeze(2).broadcast_to((128, 4, 64))
                vec.inc(vector.tensor_tensor(t13, pb3, e3, ALU.mult), 1)
                vector.wait_ge(vec.sem, vec.v)
                vec.inc(vector.tensor_add(t1sb[:], t1sb[:], ps[0 + ab][:, 0:256]), 1)
                vector.wait_ge(vec.sem, vec.v)
                y3 = ytok[:, cb, h0 * 64:(h0 + 4) * 64] \
                    .rearrange("p (h d) -> p h d", d=64)
                xs3 = xsflat[:, h0 * 64:(h0 + 4) * 64] \
                    .rearrange("p (h d) -> p h d", d=64)
                db3 = db64[:, h0:h0 + 4].unsqueeze(2).broadcast_to((128, 4, 64))
                vec.inc(vector.tensor_tensor(y3, xs3, db3, ALU.mult), 1)
                vector.wait_ge(vec.sem, vec.v)
                vec.inc(vector.tensor_add(ytok[:, cb, h0 * 64:(h0 + 4) * 64],
                                          ytok[:, cb, h0 * 64:(h0 + 4) * 64],
                                          t1sb[:]), 1)
                vector.wait_ge(vec.sem, vec.v)
                hr3 = h_run[:, h0 * 64:(h0 + 4) * 64] \
                    .rearrange("p (h d) -> p h d", d=64)
                lb3 = ps[7][:, 192 + h0:192 + h0 + 4].unsqueeze(2) \
                    .broadcast_to((128, 4, 64))
                vec.inc(vector.tensor_tensor(hr3, hr3, lb3, ALU.mult), 1)
                vector.wait_ge(vec.sem, vec.v)
                vec.inc(vector.tensor_add(h_run[:, h0 * 64:(h0 + 4) * 64],
                                          h_run[:, h0 * 64:(h0 + 4) * 64],
                                          ps[2 + ab][:, 0:256]), 1)
                vector.wait_ge(vec.sem, vec.v)
                kv[f"c_hupd_{q}_{hg}"] = vec.v
                kv[f"c_yev_{q}_{hg}"] = vec.v
            block.vector(f)

        def f(sync, q=q, cb=cb):
            sync.wait_ge(vec.sem, kv[f"c_yev_{q}_15"])
            dm(sync, out=y_scr[q], in_=ytok[:, cb, :])
            kv[f"c_yst_{q}"] = seal()
        block.sync(f)

    def f(vector):
        vector.wait_ge(act.sem, kv["c_E_7"])
        vec.inc(vector.memset(lamcum[:, 0, :], 1.0), 1)
        vector.wait_ge(vec.sem, vec.v)
        for q in range(1, NQ):
            ins = vector.tensor_tensor(lamcum[:, q, :], lamcum[:, q - 1, :],
                                       lamr[:, q - 1, :], ALU.mult)
            vec.inc(ins, 1)
            vector.wait_ge(vec.sem, vec.v)
        ins = vector.tensor_tensor(lamtot[:], lamcum[:, NQ - 1, :],
                                   lamr[:, NQ - 1, :], ALU.mult)
        vec.inc(ins, 1)
        vector.wait_ge(vec.sem, vec.v)
        kv["c_lams"] = vec.v
    block.vector(f)

    def f(sync):
        if DEBUG:
            sync.wait_ge(act.sem, kv["c_E_7"])
            dm(sync, out=dbg_lamr, in_=lamr[:])
            dm(sync, out=dbg_eall, in_=e_all[:])
        sync.wait_ge(vec.sem, kv["c_hupd_7_15"])
        dm(sync, out=ag_s_in.ap()[:, 0:4096], in_=h_run[:])
        sync.wait_ge(vec.sem, kv["c_lams"])
        dm(sync, out=ag_s_in.ap()[0:1, 4096:4160], in_=lamtot[:])
        kv["c_agin"] = seal()
    block.sync(f)

    def f(gpsimd):
        wt(gpsimd, kv["c_agin"])
        ins = gpsimd.collective_compute(
            "AllGather", ALU.bypass, replica_groups=[list(range(8))],
            ins=[ag_s_in.ap().opt()], outs=[ag_s_out.ap().opt()])
        cc.inc(ins, 1)
        kv["c_ag"] = cc.v
        if DEBUG:
            dmg.inc(gpsimd.dma_start(out=dbg_ags, in_=ag_s_out.ap()), 16)
    block.gpsimd(f)

    barrier()
    for g in reversed(gC):
        g.__exit__(None, None, None)

    if PHASE_LIMIT < 3:
        block_cm.__exit__(None, None, None)
        return nc

    gC2 = []

    def scC2(name, shape):
        g = nc.sbuf_tensor(name, shape, F32)
        gC2.append(g)
        return g.__enter__()

    h_in = scC2("h_in", [128, D])
    hloc_j = scC2("hloc_j", [128, D])
    hq_sb = scC2("hq_sb", [128, D])
    ytok2 = scC2("ytok2", [128, D])
    ztok = scC2("ztok", [128, D])
    zf = scC2("zf", [128, NKD, 128])
    gs2 = scC2("gs2", [128, 4])
    sqg = scC2("sqg", [128, D])
    y2ev = scC2("y2ev", [128, NKD, 128])

    def f(sync):
        sync.wait_ge(cc.sem, kv["c_ag"])
        dm(sync, 
            out=lam_all[:],
            in_=ag_s_out.ap()[0:1024:128, 4096:4160].unsqueeze(0))
        kv["c2_lam"] = seal()
    block.sync(f)

    def f(vector):
        wt(vector, kv["c2_lam"])
        for j in range(8):
            vec.inc(vector.memset(wrow[:, j, :], 1.0), 1)
            vector.wait_ge(vec.sem, vec.v)
            for k in range(8):
                ins = vector.tensor_scalar(s1row[:], lam_all[:, k, :],
                                           1.0, ppw[0:1, j * 8 + k:j * 8 + k + 1],
                                           ALU.subtract, ALU.mult)
                vec.inc(ins, 1)
                vector.wait_ge(vec.sem, vec.v)
                vec.inc(vector.tensor_scalar_add(s1row[:], s1row[:], 1.0), 1)
                vector.wait_ge(vec.sem, vec.v)
                vec.inc(vector.tensor_tensor(wrow[:, j, :], wrow[:, j, :],
                                             s1row[:], ALU.mult), 1)
                vector.wait_ge(vec.sem, vec.v)
            vec.inc(vector.tensor_scalar_mul(wrow[:, j, :], wrow[:, j, :],
                                             pm8[0:1, j:j + 1]), 1)
            vector.wait_ge(vec.sem, vec.v)
        vec.inc(vector.memset(h_in[:], 0.0), 1)
        vector.wait_ge(vec.sem, vec.v)
        kv["c2_w"] = vec.v
    block.vector(f)

    for j in range(8):
        def f(sync, j=j):
            sync.wait_ge(vec.sem, kv["c2_w"] if j == 0 else kv[f"c2_acc_{j - 1}"])
            dm(sync, out=hloc_j[:],
                                 in_=ag_s_out.ap()[j * 128:(j + 1) * 128, 0:4096])
            kv[f"c2_hl_{j}"] = seal()
        block.sync(f)

        def f(tensor, j=j):
            tensor.wait_ge(vec.sem, kv["c2_w"] if j == 0 else kv[f"c2_acc_{j - 1}"])
            ins = nc.tensor.matmul(ps[6][:, 0:64], on1[:], wrow[:, j, :],
                                   start=True, stop=True)
            mmc.inc(ins, 1)
            kv[f"c2_wb_{j}"] = mmc.v
        block.tensor(f)

        def f(vector, j=j):
            wt(vector, kv[f"c2_hl_{j}"])
            vector.wait_ge(mmc.sem, kv[f"c2_wb_{j}"])
            hl3 = hloc_j[:].rearrange("p (h d) -> p h d", d=64)
            wb3 = ps[6][:, 0:64].unsqueeze(2).broadcast_to((128, 64, 64))
            vec.inc(vector.tensor_tensor(hl3, hl3, wb3, ALU.mult), 1)
            vector.wait_ge(vec.sem, vec.v)
            vec.inc(vector.tensor_add(h_in[:], h_in[:], hloc_j[:]), 1)
            vector.wait_ge(vec.sem, vec.v)
            kv[f"c2_acc_{j}"] = vec.v
        block.vector(f)

    for q in range(NQ):
        def f(sync, q=q):
            sync.wait_ge(vec.sem, kv["c2_acc_7"] if q == 0 else kv[f"c2_rms_{q - 1}"])
            if q >= 1:
                sync.wait_ge(mmc.sem, kv[f"c2_ytr_{q - 1}_3"])
            dm(sync, out=ytok2[:], in_=y_scr[q])
            for g in range(4):
                dm(sync, 
                    out=zf[:, g * 8:(g + 1) * 8, :],
                    in_=zx_scr[g * 8:(g + 1) * 8, :, q * 128:(q + 1) * 128]
                    .transpose([1, 0, 2]))
            kv[f"c2_ld_{q}"] = seal()
        block.sync(f)

        def f(tensor, q=q):
            tensor.wait_ge(vec.sem, kv["c2_acc_7"] if q == 0 else kv[f"c2_hq_{q - 1}"])
            ins = nc.tensor.matmul(ps[6][:, 0:64], on1[:], lamcum[:, q, :],
                                   start=True, stop=True)
            mmc.inc(ins, 1)
            kv[f"c2_lcb_{q}"] = mmc.v
        block.tensor(f)

        def f(vector, q=q):
            vector.wait_ge(mmc.sem, kv[f"c2_lcb_{q}"])
            hi3 = h_in[:].rearrange("p (h d) -> p h d", d=64)
            hq3 = hq_sb[:].rearrange("p (h d) -> p h d", d=64)
            lc3 = ps[6][:, 0:64].unsqueeze(2).broadcast_to((128, 64, 64))
            vec.inc(vector.tensor_tensor(hq3, hi3, lc3, ALU.mult), 1)
            vector.wait_ge(vec.sem, vec.v)
            kv[f"c2_hq_{q}"] = vec.v
        block.vector(f)

        for hg in range(16):
            h0 = hg * 4
            ab = hg % 2

            def f(tensor, q=q, hg=hg, h0=h0, ab=ab):
                if hg == 0:
                    tensor.wait_ge(vec.sem, kv[f"c2_hq_{q}"])
                if hg >= 2:
                    tensor.wait_ge(vec.sem, kv[f"c2_cev_{q}_{hg - 2}"])
                for i in range(4):
                    h = h0 + i
                    ins = nc.tensor.matmul(ps[0 + ab][:, i * 64:(i + 1) * 64],
                                           cm_all[:, q, :],
                                           hq_sb[:, h * 64:(h + 1) * 64],
                                           start=True, stop=True)
                mmc.inc(ins, 1)
                kv[f"c2_cmm_{q}_{hg}"] = mmc.v
            block.tensor(f)

            def f(vector, q=q, hg=hg, h0=h0, ab=ab):
                if hg == 0:
                    wt(vector, kv[f"c2_ld_{q}"])
                vector.wait_ge(mmc.sem, kv[f"c2_cmm_{q}_{hg}"])
                pc3 = ps[0 + ab][:, 0:256].rearrange("p (h d) -> p h d", d=64)
                e3 = e_all[:, q, h0:h0 + 4].unsqueeze(2).broadcast_to((128, 4, 64))
                t3 = sqg[:, h0 * 64:(h0 + 4) * 64].rearrange("p (h d) -> p h d", d=64)
                vec.inc(vector.tensor_tensor(t3, pc3, e3, ALU.mult), 1)
                vector.wait_ge(vec.sem, vec.v)
                vec.inc(vector.tensor_add(ytok2[:, h0 * 64:(h0 + 4) * 64],
                                          ytok2[:, h0 * 64:(h0 + 4) * 64],
                                          sqg[:, h0 * 64:(h0 + 4) * 64]), 1)
                vector.wait_ge(vec.sem, vec.v)
                kv[f"c2_cev_{q}_{hg}"] = vec.v
            block.vector(f)

        for rnd in range(4):
            def f(tensor, q=q, rnd=rnd):
                if rnd == 0:
                    wt(tensor, kv[f"c2_ld_{q}"])
                    tensor.wait_ge(vec.sem, kv[f"c2_cev_{q}_15"])
                else:
                    tensor.wait_ge(vec.sem, kv[f"c2_ztr_ev_{q}_{rnd - 1}"])
                for i in range(8):
                    k = rnd * 8 + i
                    bank = i // 2
                    slot = i % 2
                    ins = nc.tensor.transpose(
                        ps[bank][:, slot * 128:(slot + 1) * 128],
                        zf[:, k, :], id_sb[:])
                mmc.inc(ins, 1)
                kv[f"c2_ztr_{q}_{rnd}"] = mmc.v
            block.tensor(f)

            def f(vector, q=q, rnd=rnd):
                vector.wait_ge(mmc.sem, kv[f"c2_ztr_{q}_{rnd}"])
                for i in range(8):
                    k = rnd * 8 + i
                    bank = i // 2
                    slot = i % 2
                    ins = vector.tensor_copy(ztok[:, k * 128:(k + 1) * 128],
                                             ps[bank][:, slot * 128:(slot + 1) * 128])
                vec.inc(ins, 1)
                vector.wait_ge(vec.sem, vec.v)
                kv[f"c2_ztr_ev_{q}_{rnd}"] = vec.v
            block.vector(f)

        def f(scalar, q=q):
            scalar.wait_ge(vec.sem, kv[f"c2_ztr_ev_{q}_3"])
            act.inc(scalar.activation(ztok[:], ztok[:], AF.Silu), 1)
            kv[f"c2_silu_{q}"] = act.v
        block.scalar(f)

        def f(vector, q=q):
            vector.wait_ge(act.sem, kv[f"c2_silu_{q}"])
            vec.inc(vector.tensor_mul(ytok2[:], ytok2[:], ztok[:]), 1)
            vector.wait_ge(vec.sem, vec.v)
            vec.inc(vector.tensor_mul(sqg[:], ytok2[:], ytok2[:]), 1)
            vector.wait_ge(vec.sem, vec.v)
            vec.inc(vector.tensor_reduce(gs2[:, 0:1], sqg[:], AX.X, ALU.add), 1)
            vector.wait_ge(vec.sem, vec.v)
            vec.inc(vector.tensor_scalar(gs2[:, 0:1], gs2[:, 0:1], 1.0 / D, EPS,
                                         ALU.mult, ALU.add), 1)
            vector.wait_ge(vec.sem, vec.v)
            kv[f"c2_rms1_{q}"] = vec.v
        block.vector(f)

        def f(scalar, q=q):
            scalar.wait_ge(vec.sem, kv[f"c2_rms1_{q}"])
            act.inc(scalar.activation(gs2[:, 0:1], gs2[:, 0:1], AF.Sqrt), 1)
            kv[f"c2_rms2_{q}"] = act.v
        block.scalar(f)

        def f(vector, q=q):
            vector.wait_ge(act.sem, kv[f"c2_rms2_{q}"])
            vec.inc(vector.reciprocal(gs2[:, 1:2], gs2[:, 0:1]), 1)
            vector.wait_ge(vec.sem, vec.v)
            vec.inc(vector.tensor_scalar_mul(ytok2[:], ytok2[:], gs2[:, 1:2]), 1)
            vector.wait_ge(vec.sem, vec.v)
            kv[f"c2_rms_{q}"] = vec.v
        block.vector(f)

        for rnd in range(4):
            def f(tensor, q=q, rnd=rnd):
                if rnd == 0:
                    tensor.wait_ge(vec.sem, kv[f"c2_rms_{q}"])
                else:
                    tensor.wait_ge(vec.sem, kv[f"c2_ytr_ev_{q}_{rnd - 1}"])
                for i in range(8):
                    k = rnd * 8 + i
                    bank = i // 2
                    slot = i % 2
                    ins = nc.tensor.transpose(
                        ps[bank][:, slot * 128:(slot + 1) * 128],
                        ytok2[:, k * 128:(k + 1) * 128], id_sb[:])
                mmc.inc(ins, 1)
                kv[f"c2_ytr_{q}_{rnd}"] = mmc.v
            block.tensor(f)

            def f(vector, q=q, rnd=rnd):
                vector.wait_ge(mmc.sem, kv[f"c2_ytr_{q}_{rnd}"])
                if q >= 1 and rnd == 0:
                    wt(vector, kv[f"c2_yst_{q - 1}"])
                for i in range(8):
                    k = rnd * 8 + i
                    bank = i // 2
                    slot = i % 2
                    ins = vector.tensor_scalar_mul(
                        y2ev[:, k, :], ps[bank][:, slot * 128:(slot + 1) * 128],
                        normw[:, k:k + 1])
                vec.inc(ins, 1)
                vector.wait_ge(vec.sem, vec.v)
                kv[f"c2_ytr_ev_{q}_{rnd}"] = vec.v
            block.vector(f)

        def f(sync, q=q):
            sync.wait_ge(vec.sem, kv[f"c2_ytr_ev_{q}_3"])
            for g in range(4):
                dm(sync, 
                    out=y2_scr[g * 8:(g + 1) * 8, :, q * 128:(q + 1) * 128]
                    .transpose([1, 0, 2]),
                    in_=y2ev[:, g * 8:(g + 1) * 8, :])
            kv[f"c2_yst_{q}"] = seal()
        block.sync(f)

    barrier()
    for g in reversed(gC2):
        g.__exit__(None, None, None)

    if PHASE_LIMIT < 4:
        block_cm.__exit__(None, None, None)
        return nc

    gE = []

    def scE(name, shape):
        g = nc.sbuf_tensor(name, shape, F32)
        gE.append(g)
        return g.__enter__()

    y2T = scE("y2T", [128, NKD, TOK])
    wbufE = scE("wbufE", [128, 2, NKD, 128])
    ffnt = scE("ffnt", [128, 2, TOK])
    sumf = scE("sumf", [128, 2, TOK])
    sumt = scE("sumt", [128, 2, 128])

    def f(sync):
        for k in range(NKD):
            dm(sync, out=y2T[:, k, :], in_=y2_scr[k])
        kv["e_y2"] = seal()
    block.sync(f)

    for m in range(NKD):
        b = m % 2

        def f(sync, m=m, b=b):
            if m >= 2:
                sync.wait_ge(mmc.sem, kv[f"e_mm_{m - 2}"])
                sync.wait_ge(vec.sem, kv[f"e_sum_{m - 2}"])
            dm(sync, out=wbufE[:, b], in_=w_out_h[m])
            dm(sync, out=ffnt[:, b], in_=ffn_scr[m])
            kv[f"e_w_{m}"] = seal()
        block.sync(f)

        def f(tensor, m=m, b=b):
            if m == 0:
                wt(tensor, kv["e_y2"])
            wt(tensor, kv[f"e_w_{m}"])
            if m >= 2:
                tensor.wait_ge(vec.sem, kv[f"e_sum_{m - 2}"])
            for half in range(2):
                bank = b * 2 + half
                for k in range(NKD):
                    ins = nc.tensor.matmul(
                        ps[bank][:, 0:512], wbufE[:, b, k, :],
                        y2T[:, k, half * 512:(half + 1) * 512],
                        start=(k == 0), stop=(k == NKD - 1))
                mmc.inc(ins, 1)
            kv[f"e_mm_{m}"] = mmc.v
        block.tensor(f)

        def f(vector, m=m, b=b):
            vector.wait_ge(mmc.sem, kv[f"e_mm_{m}"])
            for half in range(2):
                bank = b * 2 + half
                ins = vector.tensor_tensor(
                    sumf[:, b, half * 512:(half + 1) * 512], ps[bank][:, 0:512],
                    ffnt[:, b, half * 512:(half + 1) * 512], ALU.add)
                vec.inc(ins, 1)
                vector.wait_ge(vec.sem, vec.v)
            kv[f"e_sum_{m}"] = vec.v
        block.vector(f)

        for t in range(NT):
            tb = t % 2

            def f(tensor, m=m, b=b, t=t, tb=tb):
                tensor.wait_ge(vec.sem, kv[f"e_sum_{m}"])
                if (m, t) > (0, 1):
                    pm, pt = (m, t - 2) if t >= 2 else (m - 1, t + 6)
                    tensor.wait_ge(vec.sem, kv[f"e_tev_{pm}_{pt}"])
                ins = nc.tensor.transpose(ps[4 + tb][:, 0:128],
                                          sumf[:, b, t * 128:(t + 1) * 128],
                                          id_sb[:])
                mmc.inc(ins, 1)
                kv[f"e_tr_{m}_{t}"] = mmc.v
            block.tensor(f)

            def f(vector, m=m, t=t, tb=tb):
                vector.wait_ge(mmc.sem, kv[f"e_tr_{m}_{t}"])
                if (m, t) > (0, 1):
                    pm, pt = (m, t - 2) if t >= 2 else (m - 1, t + 6)
                    wt(vector, kv[f"e_tst_{pm}_{pt}"])
                vec.inc(vector.tensor_copy(sumt[:, tb, :], ps[4 + tb][:, 0:128]), 1)
                vector.wait_ge(vec.sem, vec.v)
                kv[f"e_tev_{m}_{t}"] = vec.v
            block.vector(f)

            def f(sync, m=m, t=t, tb=tb):
                sync.wait_ge(vec.sem, kv[f"e_tev_{m}_{t}"])
                dm(sync, out=sum_scr[t, :, m * 128:(m + 1) * 128],
                                     in_=sumt[:, tb, :])
                kv[f"e_tst_{m}_{t}"] = seal()
            block.sync(f)

    barrier()
    for g in reversed(gE):
        g.__exit__(None, None, None)

    if PHASE_LIMIT < 5:
        block_cm.__exit__(None, None, None)
        return nc

    gF = []

    def scF(name, shape):
        g = nc.sbuf_tensor(name, shape, F32)
        gF.append(g)
        return g.__enter__()

    sel32 = scF("sel32", [128, D])
    wqs = scF("wqs", [128, NKD, 128])
    wk1 = scF("wk1", [128, NKD, 128])
    wv1 = scF("wv1", [128, NKD, 128])
    wa1 = scF("wa1", [128, NKD, 128])
    wrt = scF("wrt", [128, NKD, 8])
    wb1 = scF("wb1", [128, D])
    x1sb = scF("x1sb", [128, D])
    sumtok = scF("sumtok", [128, D])
    xn1T = scF("xn1T", [128, NKD, 128])
    qs_sb = scF("qs_sb", [128, 128])
    k_sb = scF("k_sb", [128, 128])
    v_sb = scF("v_sb", [128, 128])
    h1_sb = scF("h1_sb", [128, 128])
    rt_sb = scF("rt_sb", [8, 128])
    rtT = scF("rtT", [128, 8])
    rwf = scF("rwf", [8, 128])
    wh_sb = scF("wh_sb", [128, 128])
    kT = scF("kT", [128, 128])
    vT = scF("vT", [128, 128])
    krow = scF("krow", [1, 2, 1024])
    vrow = scF("vrow", [1, 2, 1024])
    psb = scF("psb", [128, 2, 512])
    prodb = scF("prodb", [128, 512])
    numt = scF("numt", [128, 128])
    dent = scF("dent", [128, 128])
    o_sb = scF("o_sb", [128, 128])
    fstat = scF("fstat", [128, 8])
    cstat2 = scF("cstat2", [128, 4])
    crow = scF("crow", [1, 4])
    cm2 = scF("cm2", [2, 1])
    cbias = scF("cbias", [128, 1])
    outev = scF("outev", [128, 2, 512])

    def f(sync):
        dm(sync, out=sel32[:], in_=sel32_h.ap())
        dm(sync, 
            out=wqs[:], in_=ag_wq_out.ap().rearrange("(k p) d -> p k d", p=128))
        dm(sync, out=wk1[:], in_=w_k_h.transpose([1, 0, 2]))
        dm(sync, out=wv1[:], in_=w_v_h.transpose([1, 0, 2]))
        dm(sync, out=wa1[:], in_=w_a1_h.transpose([1, 0, 2]))
        dm(sync, out=wrt[:], in_=w_route_h.transpose([1, 0, 2]))
        dm(sync, out=wb1[:], in_=w_b1_h)
        kv["f_w"] = seal()
    block.sync(f)

    out_toks = []

    for tau in range(NT):
        def f(sync, tau=tau):
            if tau >= 1:
                sync.wait_ge(vec.sem, kv[f"f_oev_{tau - 1}_7"])
            dm(sync, out=x1sb[:],
                                   in_=x_h[tau * 128:(tau + 1) * 128, :])
            dm(sync, out=sumtok[:], in_=sum_scr[tau])
            kv[f"f_ld_{tau}"] = seal()
        block.sync(f)

        def f(vector, tau=tau):
            wt(vector, kv[f"f_ld_{tau}"])
            vec.inc(vector.tensor_add(x1sb[:], x1sb[:], sumtok[:]), 1)
            vector.wait_ge(vec.sem, vec.v)
            vec.inc(vector.tensor_reduce(fstat[:, 0:1], x1sb[:], AX.X, ALU.add), 1)
            vector.wait_ge(vec.sem, vec.v)
            vec.inc(vector.tensor_mul(sumtok[:], x1sb[:], x1sb[:]), 1)
            vector.wait_ge(vec.sem, vec.v)
            vec.inc(vector.tensor_reduce(fstat[:, 1:2], sumtok[:], AX.X, ALU.add), 1)
            vector.wait_ge(vec.sem, vec.v)
            vec.inc(vector.tensor_scalar_mul(fstat[:, 0:1], fstat[:, 0:1], 1.0 / D), 1)
            vector.wait_ge(vec.sem, vec.v)
            vec.inc(vector.tensor_scalar_mul(fstat[:, 1:2], fstat[:, 1:2], 1.0 / D), 1)
            vector.wait_ge(vec.sem, vec.v)
            vec.inc(vector.tensor_mul(fstat[:, 2:3], fstat[:, 0:1], fstat[:, 0:1]), 1)
            vector.wait_ge(vec.sem, vec.v)
            vec.inc(vector.tensor_tensor(fstat[:, 1:2], fstat[:, 1:2],
                                         fstat[:, 2:3], ALU.subtract), 1)
            vector.wait_ge(vec.sem, vec.v)
            vec.inc(vector.tensor_scalar_add(fstat[:, 1:2], fstat[:, 1:2], EPS), 1)
            vector.wait_ge(vec.sem, vec.v)
            kv[f"f_ln1_{tau}"] = vec.v
        block.vector(f)

        def f(scalar, tau=tau):
            scalar.wait_ge(vec.sem, kv[f"f_ln1_{tau}"])
            act.inc(scalar.activation(fstat[:, 1:2], fstat[:, 1:2], AF.Sqrt), 1)
            kv[f"f_ln2_{tau}"] = act.v
        block.scalar(f)

        def f(vector, tau=tau):
            vector.wait_ge(act.sem, kv[f"f_ln2_{tau}"])
            vec.inc(vector.reciprocal(fstat[:, 1:2], fstat[:, 1:2]), 1)
            vector.wait_ge(vec.sem, vec.v)
            vec.inc(vector.tensor_scalar(sumtok[:], x1sb[:], fstat[:, 0:1],
                                         fstat[:, 1:2], ALU.subtract, ALU.mult), 1)
            vector.wait_ge(vec.sem, vec.v)
            kv[f"f_xn1_{tau}"] = vec.v
        block.vector(f)

        for rnd in range(4):
            def f(tensor, tau=tau, rnd=rnd):
                if rnd == 0:
                    tensor.wait_ge(vec.sem, kv[f"f_xn1_{tau}"])
                    wt(tensor, kv["f_w"])
                else:
                    tensor.wait_ge(vec.sem, kv[f"f_trev_{tau}_{rnd - 1}"])
                for i in range(8):
                    k = rnd * 8 + i
                    bank = i // 2
                    slot = i % 2
                    ins = nc.tensor.transpose(
                        ps[bank][:, slot * 128:(slot + 1) * 128],
                        sumtok[:, k * 128:(k + 1) * 128], id_sb[:])
                mmc.inc(ins, 1)
                kv[f"f_tr_{tau}_{rnd}"] = mmc.v
            block.tensor(f)

            def f(vector, tau=tau, rnd=rnd):
                vector.wait_ge(mmc.sem, kv[f"f_tr_{tau}_{rnd}"])
                for i in range(8):
                    k = rnd * 8 + i
                    bank = i // 2
                    slot = i % 2
                    ins = vector.tensor_copy(xn1T[:, k, :],
                                             ps[bank][:, slot * 128:(slot + 1) * 128])
                vec.inc(ins, 1)
                vector.wait_ge(vec.sem, vec.v)
                kv[f"f_trev_{tau}_{rnd}"] = vec.v
            block.vector(f)

        def f(tensor, tau=tau):
            tensor.wait_ge(vec.sem, kv[f"f_trev_{tau}_3"])
            for bank, w in ((0, wqs), (1, wk1), (2, wv1), (3, wa1)):
                for k in range(NKD):
                    ins = nc.tensor.matmul(ps[bank][:, 0:128], w[:, k, :],
                                           xn1T[:, k, :],
                                           start=(k == 0), stop=(k == NKD - 1))
                mmc.inc(ins, 1)
            for k in range(NKD):
                ins = nc.tensor.matmul(ps[6][0:8, 0:128], wrt[:, k, :],
                                       xn1T[:, k, :],
                                       start=(k == 0), stop=(k == NKD - 1))
            mmc.inc(ins, 1)
            kv[f"f_gmm_{tau}"] = mmc.v
        block.tensor(f)

        def f(vector, tau=tau):
            vector.wait_ge(mmc.sem, kv[f"f_gmm_{tau}"])
            vec.inc(vector.tensor_scalar(qs_sb[:], ps[0][:, 0:128], bqs[:, 0:1],
                                         SCL, ALU.add, ALU.mult), 1)
            vector.wait_ge(vec.sem, vec.v)
            vec.inc(vector.tensor_scalar(k_sb[:], ps[1][:, 0:128], bkc[:, 0:1],
                                         None, ALU.add), 1)
            vector.wait_ge(vec.sem, vec.v)
            vec.inc(vector.tensor_scalar(v_sb[:], ps[2][:, 0:128], bvc[:, 0:1],
                                         None, ALU.add), 1)
            vector.wait_ge(vec.sem, vec.v)
            vec.inc(vector.tensor_copy(h1_sb[:], ps[3][:, 0:128]), 1)
            vector.wait_ge(vec.sem, vec.v)
            vec.inc(vector.tensor_copy(rt_sb[:], ps[6][0:8, 0:128]), 1)
            vector.wait_ge(vec.sem, vec.v)
            kv[f"f_gev_{tau}"] = vec.v
        block.vector(f)

        def f(tensor, tau=tau):
            tensor.wait_ge(vec.sem, kv[f"f_gev_{tau}"])
            mmc.inc(nc.tensor.transpose(ps[4][0:128, 0:8], rt_sb[:], id_sb[0:8, 0:8]), 1)
            kv[f"f_rtr_{tau}"] = mmc.v
        block.tensor(f)

        def f(vector, tau=tau):
            vector.wait_ge(mmc.sem, kv[f"f_rtr_{tau}"])
            vec.inc(vector.tensor_copy(rtT[:], ps[4][0:128, 0:8]), 1)
            vector.wait_ge(vec.sem, vec.v)
            vec.inc(vector.reduce_max(fstat[:, 3:4], rtT[:], AX.X), 1)
            vector.wait_ge(vec.sem, vec.v)
            vec.inc(vector.tensor_tensor(rtT[:], rtT[:],
                                         fstat[:, 3:4].broadcast_to((128, 8)),
                                         ALU.is_ge), 1)
            vector.wait_ge(vec.sem, vec.v)
            vec.inc(vector.tensor_scalar_mul(rtT[:], rtT[:], fstat[:, 3:4]), 1)
            vector.wait_ge(vec.sem, vec.v)
            kv[f"f_rw_{tau}"] = vec.v
        block.vector(f)

        def f(tensor, tau=tau):
            tensor.wait_ge(vec.sem, kv[f"f_rw_{tau}"])
            mmc.inc(nc.tensor.transpose(ps[4][0:8, 0:128], rtT[:], id_sb[:]), 1)
            kv[f"f_rwtr_{tau}"] = mmc.v
        block.tensor(f)

        def f(vector, tau=tau):
            vector.wait_ge(mmc.sem, kv[f"f_rwtr_{tau}"])
            vec.inc(vector.tensor_copy(rwf[:], ps[4][0:8, 0:128]), 1)
            vector.wait_ge(vec.sem, vec.v)
            kv[f"f_rwf_{tau}"] = vec.v
        block.vector(f)

        def f(tensor, tau=tau):
            tensor.wait_ge(vec.sem, kv[f"f_rwf_{tau}"])
            ins = nc.tensor.matmul(ps[4][:, 0:128], esel8[:], rwf[:],
                                   start=True, stop=True)
            mmc.inc(ins, 1)
            kv[f"f_rwexp_{tau}"] = mmc.v
        block.tensor(f)

        def f(vector, tau=tau):
            vector.wait_ge(mmc.sem, kv[f"f_rwexp_{tau}"])
            vec.inc(vector.tensor_tensor(wh_sb[:], h1_sb[:], ps[4][:, 0:128],
                                         ALU.mult), 1)
            vector.wait_ge(vec.sem, vec.v)
            kv[f"f_wh_{tau}"] = vec.v
        block.vector(f)

        def f(tensor, tau=tau):
            tensor.wait_ge(vec.sem, kv[f"f_gev_{tau}"])
            mmc.inc(nc.tensor.transpose(ps[5][:, 0:128], k_sb[:], id_sb[:]), 1)
            kv[f"f_ktr_{tau}"] = mmc.v
        block.tensor(f)

        def f(vector, tau=tau):
            vector.wait_ge(mmc.sem, kv[f"f_ktr_{tau}"])
            vec.inc(vector.tensor_copy(kT[:], ps[5][:, 0:128]), 1)
            vector.wait_ge(vec.sem, vec.v)
            kv[f"f_ktev_{tau}"] = vec.v
        block.vector(f)

        def f(tensor, tau=tau):
            tensor.wait_ge(vec.sem, kv[f"f_ktev_{tau}"])
            mmc.inc(nc.tensor.transpose(ps[5][:, 0:128], v_sb[:], id_sb[:]), 1)
            kv[f"f_vtr_{tau}"] = mmc.v
        block.tensor(f)

        def f(vector, tau=tau):
            vector.wait_ge(mmc.sem, kv[f"f_vtr_{tau}"])
            vec.inc(vector.tensor_copy(vT[:], ps[5][:, 0:128]), 1)
            vector.wait_ge(vec.sem, vec.v)
            vec.inc(vector.reduce_max(cstat2[:, 0:1], qs_sb[:], AX.X,
                                      apply_absolute_value=True), 1)
            vector.wait_ge(vec.sem, vec.v)
            vec.inc(vector.reduce_max(cstat2[:, 1:2], k_sb[:], AX.X,
                                      apply_absolute_value=True), 1)
            vector.wait_ge(vec.sem, vec.v)
            kv[f"f_vtev_{tau}"] = vec.v
        block.vector(f)

        def f(tensor, tau=tau):
            tensor.wait_ge(vec.sem, kv[f"f_vtev_{tau}"])
            mmc.inc(nc.tensor.transpose(ps[5][0:4, 0:128], cstat2[:], id_sb[:]), 1)
            kv[f"f_ctr_{tau}"] = mmc.v
        block.tensor(f)

        def f(vector, tau=tau):
            vector.wait_ge(mmc.sem, kv[f"f_ctr_{tau}"])
            vec.inc(vector.reduce_max(cm2[:], ps[5][0:2, 0:128], AX.X), 1)
            vector.wait_ge(vec.sem, vec.v)
            kv[f"f_c1_{tau}"] = vec.v
        block.vector(f)

        def f(tensor, tau=tau):
            tensor.wait_ge(vec.sem, kv[f"f_c1_{tau}"])
            mmc.inc(nc.tensor.transpose(ps[5][0:1, 128:130], cm2[:],
                                        id_sb[0:2, 0:2]), 1)
            kv[f"f_c2_{tau}"] = mmc.v
        block.tensor(f)

        def f(vector, tau=tau):
            vector.wait_ge(mmc.sem, kv[f"f_c2_{tau}"])
            vec.inc(vector.tensor_copy(crow[:, 0:2], ps[5][0:1, 128:130]), 1)
            vector.wait_ge(vec.sem, vec.v)
            vec.inc(vector.tensor_mul(crow[:, 0:2], crow[:, 0:2], crow[:, 0:2]), 1)
            vector.wait_ge(vec.sem, vec.v)
            vec.inc(vector.tensor_reduce(crow[:, 2:3], crow[:, 0:2], AX.X,
                                         ALU.add), 1)
            vector.wait_ge(vec.sem, vec.v)
            vec.inc(vector.tensor_scalar_mul(crow[:, 3:4], crow[:, 2:3], -0.5), 1)
            vector.wait_ge(vec.sem, vec.v)
            kv[f"f_c_{tau}"] = vec.v
        block.vector(f)

        def f(tensor, tau=tau):
            tensor.wait_ge(vec.sem, kv[f"f_c_{tau}"])
            ins = nc.tensor.matmul(ps[6][:, 128:129], on1[:], crow[:, 3:4],
                                   start=True, stop=True)
            mmc.inc(ins, 1)
            kv[f"f_cbmm_{tau}"] = mmc.v
        block.tensor(f)

        def f(vector, tau=tau):
            vector.wait_ge(mmc.sem, kv[f"f_cbmm_{tau}"])
            vec.inc(vector.tensor_copy(cbias[:], ps[6][:, 128:129]), 1)
            vector.wait_ge(vec.sem, vec.v)
            kv[f"f_cb_{tau}"] = vec.v
        block.vector(f)

        for g in range(16):
            rb = g % 2

            def f(sync, tau=tau, g=g, rb=rb):
                if g == 0:
                    sync.wait_ge(vec.sem, kv[f"f_vtev_{tau}"])
                elif g == 1:
                    pass
                else:
                    sync.wait_ge(vec.sem, kv[f"f_nd4_{tau}_{(g - 2) * 2 + 1}"])
                dm(sync, out=krow[:, rb, :],
                                       in_=kT[g * 8:(g + 1) * 8, :])
                dm(sync, out=vrow[:, rb, :],
                                       in_=vT[g * 8:(g + 1) * 8, :])
                kv[f"f_row_{tau}_{g}"] = seal()
            block.sync(f)

            for sg in range(2):
                u = g * 2 + sg
                t0 = u * 4
                kb = u % 2

                def f(tensor, tau=tau, g=g, sg=sg, rb=rb, kb=kb, u=u):
                    wt(tensor, kv[f"f_row_{tau}_{g}"])
                    if u >= 2:
                        tensor.wait_ge(act.sem, kv[f"f_exp_{tau}_{u - 2}"])
                        tensor.wait_ge(vec.sem, kv[f"f_nd4_{tau}_{u - 2}"])
                    ins = nc.tensor.matmul(ps[0 + kb][:, 0:512], on1[:],
                                           krow[0:1, rb, sg * 512:(sg + 1) * 512],
                                           start=True, stop=True)
                    mmc.inc(ins, 1)
                    ins = nc.tensor.matmul(ps[2 + kb][:, 0:512], on1[:],
                                           vrow[0:1, rb, sg * 512:(sg + 1) * 512],
                                           start=True, stop=True)
                    mmc.inc(ins, 1)
                    kv[f"f_rep_{tau}_{u}"] = mmc.v
                block.tensor(f)

                def f(scalar, tau=tau, u=u, kb=kb, t0=t0):
                    scalar.wait_ge(mmc.sem, kv[f"f_rep_{tau}_{u}"])
                    if u == 0:
                        scalar.wait_ge(vec.sem, kv[f"f_cb_{tau}"])
                    if u >= 2:
                        scalar.wait_ge(vec.sem, kv[f"f_nd4_{tau}_{u - 2}"])
                    for i in range(4):
                        t = t0 + i
                        ins = scalar.activation(
                            psb[:, kb, i * 128:(i + 1) * 128],
                            ps[0 + kb][:, i * 128:(i + 1) * 128],
                            AF.Exp, scale=qs_sb[:, t:t + 1], bias=cbias[:, 0:1])
                        act.inc(ins, 1)
                    kv[f"f_exp_{tau}_{u}"] = act.v
                block.scalar(f)

                def f(vector, tau=tau, u=u, kb=kb, t0=t0):
                    vector.wait_ge(act.sem, kv[f"f_exp_{tau}_{u}"])
                    ins = vector.tensor_tensor(prodb[:], psb[:, kb, :],
                                               ps[2 + kb][:, 0:512], ALU.mult)
                    vec.inc(ins, 1)
                    vector.wait_ge(vec.sem, vec.v)
                    pr3 = prodb[:].rearrange("p (t e) -> p t e", e=128)
                    vec.inc(vector.tensor_reduce(numt[:, t0:t0 + 4], pr3,
                                                 AX.X, ALU.add), 1)
                    vector.wait_ge(vec.sem, vec.v)
                    vector.wait_ge(vec.sem, vec.v)
                    p3 = psb[:, kb, :].rearrange("p (t e) -> p t e", e=128)
                    vec.inc(vector.tensor_reduce(dent[:, t0:t0 + 4], p3,
                                                 AX.X, ALU.add), 1)
                    vector.wait_ge(vec.sem, vec.v)
                    vector.wait_ge(vec.sem, vec.v)
                    kv[f"f_nd4_{tau}_{u}"] = vec.v
                block.vector(f)

        def f(vector, tau=tau):
            vector.wait_ge(vec.sem, kv[f"f_nd4_{tau}_31"])
            vec.inc(vector.reciprocal(dent[:], dent[:]), 1)
            vector.wait_ge(vec.sem, vec.v)
            vec.inc(vector.tensor_mul(o_sb[:], numt[:], dent[:]), 1)
            vector.wait_ge(vec.sem, vec.v)
            kv[f"f_o_{tau}"] = vec.v
        block.vector(f)

        for fc in range(8):
            ob = fc % 2

            def f(tensor, tau=tau, fc=fc, ob=ob):
                if fc == 0:
                    tensor.wait_ge(vec.sem, kv[f"f_o_{tau}"])
                    tensor.wait_ge(vec.sem, kv[f"f_wh_{tau}"])
                if fc >= 2:
                    tensor.wait_ge(vec.sem, kv[f"f_oev_{tau}_{fc - 2}"])
                nc.tensor.matmul(ps[0 + ob][:, 0:512], wh_sb[:],
                                 wb1[:, fc * 512:(fc + 1) * 512],
                                 start=True, stop=False)
                ins = nc.tensor.matmul(ps[0 + ob][:, 0:512], o_sb[:],
                                       sel32[:, fc * 512:(fc + 1) * 512],
                                       start=False, stop=True)
                mmc.inc(ins, 1)
                kv[f"f_omm_{tau}_{fc}"] = mmc.v
            block.tensor(f)

            def f(vector, tau=tau, fc=fc, ob=ob):
                vector.wait_ge(mmc.sem, kv[f"f_omm_{tau}_{fc}"])
                if fc >= 2:
                    wt(vector, kv[f"f_ost_{tau}_{fc - 2}"])
                ins = vector.tensor_tensor(outev[:, ob, :],
                                           x1sb[:, fc * 512:(fc + 1) * 512],
                                           ps[0 + ob][:, 0:512], ALU.add)
                vec.inc(ins, 1)
                vector.wait_ge(vec.sem, vec.v)
                kv[f"f_oev_{tau}_{fc}"] = vec.v
            block.vector(f)

            def f(sync, tau=tau, fc=fc, ob=ob):
                sync.wait_ge(vec.sem, kv[f"f_oev_{tau}_{fc}"])
                dm(sync, 
                    out=out_h[tau * 128:(tau + 1) * 128, fc * 512:(fc + 1) * 512],
                    in_=outev[:, ob, :])
                kv[f"f_ost_{tau}_{fc}"] = seal()
                out_toks.extend(kv[f"f_ost_{tau}_{fc}"])
            block.sync(f)

    def f(sync):
        wt(sync, out_toks)
    block.sync(f)

    for g in reversed(gF):
        g.__exit__(None, None, None)

    block_cm.__exit__(None, None, None)
    return nc


def stage_inputs(inputs):
    f4 = np.float32
    C = np.ascontiguousarray

    x = np.asarray(inputs["L0_attn_hidden_states"], dtype=f4)

    w_in = np.zeros((4096, 8576), f4)
    w_in[:, :PROJ] = np.asarray(inputs["L0_attn_in_proj_w"], f4)
    w_in_st = C(w_in.reshape(NKD, 128, MIN, 128).transpose(2, 1, 0, 3))

    mup = np.zeros(8576, f4)
    mup[:PROJ] = np.asarray(inputs["L0_attn_mup_vector"], f4)
    mup_st = C(mup.reshape(MIN, 128).T)

    lnw_st = C(np.asarray(inputs["L0_attn_ln_w"], f4).reshape(NKD, 128).T)
    normw_st = C(np.asarray(inputs["L0_attn_norm_w"], f4).reshape(NKD, 128).T)
    convw_st = C(np.asarray(inputs["L0_attn_conv_w"], f4).reshape(NCV, 128, 4))
    dtb_st = C(np.asarray(inputs["L0_attn_dt_bias"], f4).reshape(64, 1))
    alog_st = C(np.asarray(inputs["L0_attn_A_log"], f4).reshape(64, 1))
    dvec_st = C(np.asarray(inputs["L0_attn_D"], f4).reshape(1, 64))

    w_out = np.asarray(inputs["L0_attn_out_proj_w"], f4)
    w_out_st = C(w_out.reshape(NKD, 128, NKD, 128).transpose(2, 1, 0, 3))

    w_base = np.asarray(inputs["L0_ffn_base_W"], f4)
    w_base_st = C(w_base.T.reshape(NKD, 128, NKD, 128).transpose(2, 1, 0, 3))

    w_gate_st = C(np.asarray(inputs["L0_ffn_gate_W"], f4).T.reshape(NKD, 128, 28))

    la = np.asarray(inputs["L0_ffn_lora_A"], f4).reshape(224, 4096)
    w_la = np.zeros((4096, 256), f4)
    w_la[:, :224] = la.T
    w_la_st = C(w_la.reshape(NKD, 128, 256))

    lb = np.asarray(inputs["L0_ffn_lora_B"], f4).transpose(0, 2, 1).reshape(224, 4096)
    w_lb = np.zeros((256, 4096), f4)
    w_lb[:224] = lb
    w_lb_st = C(w_lb.reshape(2, 128, NKD, 128).transpose(2, 0, 1, 3))

    wq = np.asarray(inputs["L1_attn_Wq"], f4)
    bq_st = C(np.asarray(inputs["L1_attn_bq"], f4))
    bk_st = C(np.asarray(inputs["L1_attn_bk"], f4))
    bv_st = C(np.asarray(inputs["L1_attn_bv"], f4))
    w_k_st = C(np.asarray(inputs["L1_attn_Wk"], f4)[:, :, 0].reshape(NKD, 128, 128))
    w_v_st = C(np.asarray(inputs["L1_attn_Wv"], f4)[:, :, 0].reshape(NKD, 128, 128))
    w_route_st = C(np.asarray(inputs["L1_ffn_route_w"], f4).T.reshape(NKD, 128, 8))
    a1 = np.asarray(inputs["L1_ffn_A"], f4).reshape(128, 4096)
    w_a1_st = C(a1.T.reshape(NKD, 128, 128))
    b1 = np.asarray(inputs["L1_ffn_Bw"], f4).transpose(0, 2, 1).reshape(128, 4096)
    w_b1_st = C(b1)

    shared = dict(
        w_in=w_in_st, mupv=mup_st, lnw=lnw_st, normw=normw_st, convw=convw_st,
        dtb=dtb_st, alog=alog_st, dvecr=dvec_st, w_out=w_out_st, w_base=w_base_st,
        w_gate=w_gate_st, w_la=w_la_st, w_lb=w_lb_st, bq=bq_st, bk=bk_st,
        bv=bv_st, w_k=w_k_st, w_v=w_v_st, w_route=w_route_st, w_a1=w_a1_st,
        w_b1=w_b1_st)

    in_maps = []
    for c in range(8):
        bb, ci = c // 4, c % 4
        m = dict(shared)
        m["x"] = C(x[bb, ci * 1024:(ci + 1) * 1024, :])
        if ci == 0:
            m["xh"] = np.zeros((3, 4096), f4)
        else:
            m["xh"] = C(x[bb, ci * 1024 - 3:ci * 1024, :])
        m["wq_sl"] = C(wq[c * 512:(c + 1) * 512].reshape(4, 128, 128, 32))
        pm = np.zeros((1, 8), f4)
        pw = np.zeros((1, 64), f4)
        for j in range(8):
            jb, jci = j // 4, j % 4
            if jb == bb and jci < ci:
                pm[0, j] = 1.0
                for k in range(8):
                    kb, kci = k // 4, k % 4
                    if kb == bb and jci < kci < ci:
                        pw[0, j * 8 + k] = 1.0
        m["pm8"] = pm
        m["ppw"] = pw
        in_maps.append(m)
    return in_maps


def kernel(**inputs):
    in_maps = stage_inputs(inputs)
    nc = build_nc()
    res = run_bass_kernel_spmd(nc, in_maps, core_ids=list(range(8)))
    out = np.empty((2, 4096, 4096), np.float32)
    for c in range(8):
        bb, ci = c // 4, c % 4
        out[bb, ci * 1024:(ci + 1) * 1024, :] = res.results[c]["out"]
    return out


# version 06
# speedup vs baseline: 1.0054x; 1.0054x over previous
import numpy as np
import concourse.bass as bass
import concourse.mybir as mybir
from concourse.bass_utils import run_bass_kernel_spmd

F32 = mybir.dt.float32
AF = mybir.ActivationFunctionType
ALU = mybir.AluOpType
AX = mybir.AxisListType

D = 4096
TOK = 1024
NKD = 32
PROJ = 8512
MIN = 67
NCV = 34
NQ, Q = 8, 128
NT = 8
EPS = 1e-5
SCL = float(1.0 / np.sqrt(128.0))

DEBUG = False
PHASE_LIMIT = 99


class Ctr:
    __slots__ = ("sem", "v")

    def __init__(self, nc, name):
        self.sem = nc.alloc_semaphore(name)
        self.v = 0

    def inc(self, ins, n):
        ins.then_inc(self.sem, n)
        self.v += n
        return self.v


def _consts():
    ident = np.eye(128, dtype=np.float32)
    ones1 = np.ones((1, 128), np.float32)
    s_idx = np.arange(128)
    maskneg = np.where(s_idx[:, None] <= s_idx[None, :], 0.0, -1e30).astype(np.float32)
    sel32 = (np.arange(4096)[None, :] // 32 == s_idx[:, None]).astype(np.float32)
    eselg = np.zeros((28, 256), np.float32)
    for e in range(28):
        eselg[e, e * 8:(e + 1) * 8] = 2.0
    esel8 = np.zeros((8, 128), np.float32)
    for e in range(8):
        esel8[e, e * 16:(e + 1) * 16] = 1.0
    return ident, ones1, maskneg, sel32, eselg, esel8


def build_nc():
    nc = bass.Bass("TRN2", target_bir_lowering=False, debug=False, num_devices=8)

    def inp(name, shape):
        return nc.dram_tensor(name, list(shape), F32, kind="ExternalInput").ap()

    x_h = inp("x", (TOK, D))
    xh_h = inp("xh", (3, D))
    w_in_h = inp("w_in", (MIN, 128, NKD, 128))
    mupv_h = inp("mupv", (128, MIN))
    lnw_h = inp("lnw", (128, NKD))
    normw_h = inp("normw", (128, NKD))
    convw_h = inp("convw", (NCV, 128, 4))
    dtb_h = inp("dtb", (64, 1))
    alog_h = inp("alog", (64, 1))
    dvecr_h = inp("dvecr", (1, 64))
    w_out_h = inp("w_out", (NKD, 128, NKD, 128))
    w_base_h = inp("w_base", (NKD, 128, NKD, 128))
    w_gate_h = inp("w_gate", (NKD, 128, 28))
    w_la_h = inp("w_la", (NKD, 128, 256))
    w_lb_h = inp("w_lb", (NKD, 2, 128, 128))
    wq_sl_h = inp("wq_sl", (4, 128, 128, 32))
    bq_h = inp("bq", (128, 32))
    bk_h = inp("bk", (128, 1))
    bv_h = inp("bv", (128, 1))
    w_k_h = inp("w_k", (NKD, 128, 128))
    w_v_h = inp("w_v", (NKD, 128, 128))
    w_route_h = inp("w_route", (NKD, 128, 8))
    w_a1_h = inp("w_a1", (NKD, 128, 128))
    w_b1_h = inp("w_b1", (128, D))
    pm8_h = inp("pm8", (1, 8))
    ppw_h = inp("ppw", (1, 64))

    out_h = nc.dram_tensor("out", [TOK, D], F32, kind="ExternalOutput").ap()

    skind = "ExternalOutput" if DEBUG else "Internal"

    def scr(name, shape):
        return nc.dram_tensor(name, list(shape), F32, kind=skind).ap()

    zx_scr = scr("zx_scr", (MIN, 128, TOK))
    y_scr = scr("y_scr", (NQ, 128, D))
    y2_scr = scr("y2_scr", (NKD, 128, TOK))
    ffn_scr = scr("ffn_scr", (NKD, 128, TOK))
    sum_scr = scr("sum_scr", (NT, 128, D))
    s_scr = scr("s_scr", (NQ, 64, 128))
    rs2_scr = scr("rs2_scr", (8, 128))
    ag_wq_in = nc.dram_tensor("ag_wq_in", [512, 128], F32, kind="Internal")
    ag_wq_out = nc.dram_tensor("ag_wq_out", [4096, 128], F32, kind="Internal")
    ag_s_in = nc.dram_tensor("ag_s_in", [128, 4160], F32, kind="Internal")
    ag_s_out = nc.dram_tensor("ag_s_out", [1024, 4160], F32, kind="Internal")
    if DEBUG:
        dbg_ags = nc.dram_tensor("dbg_ags", [1024, 4160], F32,
                                 kind="ExternalOutput").ap()
        dbg_xnT = nc.dram_tensor("dbg_xnT", [NKD, 128, TOK], F32,
                                 kind="ExternalOutput").ap()
        dbg_r2 = nc.dram_tensor("dbg_r2", [1, TOK], F32,
                                 kind="ExternalOutput").ap()
        dbg_r2t = nc.dram_tensor("dbg_r2t", [8, 128], F32,
                                 kind="ExternalOutput").ap()
        dbg_r2c = nc.dram_tensor("dbg_r2c", [128, 8], F32,
                                 kind="ExternalOutput").ap()
        dbg_h = nc.dram_tensor("dbg_h", [2, 128, TOK], F32,
                               kind="ExternalOutput").ap()
        dbg_gf = nc.dram_tensor("dbg_gf", [28, TOK], F32,
                                kind="ExternalOutput").ap()
        dbg_lg = nc.dram_tensor("dbg_lg", [28, TOK], F32,
                                kind="ExternalOutput").ap()
        dbg_lgT = nc.dram_tensor("dbg_lgT", [128, NT, 28], F32,
                                 kind="ExternalOutput").ap()
        dbg_gT = nc.dram_tensor("dbg_gT", [128, NT, 28], F32,
                                kind="ExternalOutput").ap()
        dbg_sdt = nc.dram_tensor("dbg_sdt", [NQ, 128, 128], F32,
                                 kind="ExternalOutput").ap()
        dbg_lamr = nc.dram_tensor("dbg_lamr", [1, NQ, 64], F32,
                                  kind="ExternalOutput").ap()
        dbg_eall = nc.dram_tensor("dbg_eall", [128, NQ, 64], F32,
                                  kind="ExternalOutput").ap()
        dbg_srow = nc.dram_tensor("dbg_srow", [NQ, 8192], F32,
                                  kind="ExternalOutput").ap()

    ident_np, ones1_np, maskneg_np, sel32_np, eselg_np, esel8_np = _consts()
    ident_h = nc.inline_tensor(ident_np, name="identc")
    ones1_h = nc.inline_tensor(ones1_np, name="ones1c")
    maskneg_h = nc.inline_tensor(maskneg_np, name="masknegc")
    sel32_h = nc.inline_tensor(sel32_np, name="sel32c")
    eselg_h = nc.inline_tensor(eselg_np, name="eselgc")
    esel8_h = nc.inline_tensor(esel8_np, name="esel8c")

    ps = [nc.alloc_psum_tensor(f"ps{i}", [128, 512], F32) for i in range(8)]

    A = nc.alloc_sbuf_tensor
    id_sb = A("id_sb", [128, 128], F32)
    on1 = A("on1", [1, 128], F32)
    mneg = A("mneg", [128, 128], F32)
    eselg = A("eselg", [28, 256], F32)
    esel8 = A("esel8", [8, 128], F32)
    lnw = A("sb_lnw", [128, NKD], F32)
    normw = A("sb_normw", [128, NKD], F32)
    mupv = A("sb_mupv", [128, MIN], F32)
    convw = A("sb_convw", [128, NCV, 4], F32)
    dtb = A("sb_dtb", [64, 1], F32)
    negA = A("negA", [64, 1], F32)
    dvrow = A("dvrow", [1, 64], F32)
    db64 = A("db64", [128, 64], F32)
    bq32 = A("bq32", [128, 32], F32)
    bqs = A("bqs", [128, 1], F32)
    bkc = A("bkc", [128, 1], F32)
    bvc = A("bvc", [128, 1], F32)
    pm8 = A("sb_pm8", [1, 8], F32)
    ppw = A("sb_ppw", [1, 64], F32)
    halo_my = A("halo_my", [128, NCV, 3], F32)
    rs2row = A("rs2row", [1, TOK], F32)
    rs2hb = A("rs2hb", [128, 3], F32)
    e_all = A("e_all", [128, NQ, 64], F32)
    cm_all = A("cm_all", [128, NQ, 128], F32)
    lamr = A("lamr", [1, NQ, 64], F32)
    lamtot = A("lamtot", [1, 64], F32)
    lamcum = A("lamcum", [1, NQ, 64], F32)
    wrow = A("wrow", [1, 8, 64], F32)
    lam_all = A("lam_all", [1, 8, 64], F32)
    s1row = A("s1row", [1, 64], F32)

    mmc = Ctr(nc, "mmc")
    vec = Ctr(nc, "vec")
    act = Ctr(nc, "act")
    cc = Ctr(nc, "cc")
    ctrs = (mmc, vec, act, cc)

    class DmaPool:
        def __init__(self, n, name):
            self.sems = [nc.alloc_semaphore(f"{name}{i}") for i in range(n)]
            self.vals = [0] * n
            self.rr = 0
            self.n = n

        def issue(self, eng, **kw):
            i = self.rr
            self.rr = (self.rr + 1) % self.n
            eng.wait_ge(self.sems[i], self.vals[i])
            ins = eng.dma_start(**kw)
            ins.then_inc(self.sems[i], 16)
            self.vals[i] += 16
            return (self.sems[i], self.vals[i])

        def all_tokens(self):
            return [(s, v) for s, v in zip(self.sems, self.vals)]

    dpool = DmaPool(12, "dp")
    TK = []

    def dm(eng, **kw):
        TK.append(dpool.issue(eng, **kw))

    def seal():
        t = list(TK)
        TK.clear()
        return t

    def wt(eng, toks):
        for s, v in toks:
            eng.wait_ge(s, v)

    block_cm = nc.Block()
    block = block_cm.__enter__()

    def barrier():
        toks = dpool.all_tokens()
        for emit in (block.sync, block.tensor, block.vector, block.scalar,
                     block.gpsimd):
            def f(eng):
                for c in ctrs:
                    eng.wait_ge(c.sem, c.v)
                for s, v in toks:
                    eng.wait_ge(s, v)
            emit(f)

    gA = []

    def scA(name, shape):
        g = nc.sbuf_tensor(name, shape, F32)
        gA.append(g)
        return g.__enter__()

    kv = {}

    gWq = nc.sbuf_tensor("wqsp", [128, 4, 128], F32)
    wqsp = gWq.__enter__()
    gWq2 = nc.sbuf_tensor("wqbig", [128, 128, 32], F32)
    wqbig = gWq2.__enter__()

    def f_consts(sync):
        for dst, src in ((id_sb[:], ident_h.ap()), (on1[:], ones1_h.ap()),
                         (mneg[:], maskneg_h.ap()), (eselg[:], eselg_h.ap()),
                         (esel8[:], esel8_h.ap()), (lnw[:], lnw_h),
                         (normw[:], normw_h), (mupv[:], mupv_h),
                         (convw[:], convw_h.transpose([1, 0, 2])),
                         (dtb[:], dtb_h), (negA[:], alog_h), (dvrow[:], dvecr_h),
                         (pm8[:], pm8_h), (ppw[:], ppw_h), (bq32[:], bq_h),
                         (bkc[:], bk_h), (bvc[:], bv_h)):
            dm(sync, out=dst, in_=src)
        kv["consts"] = seal()
    block.sync(f_consts)

    for j in range(4):
        def f(sync, j=j):
            if j >= 1:
                sync.wait_ge(vec.sem, kv[f"wqr{j - 1}"])
            dm(sync, out=wqbig[:], in_=wq_sl_h[j])
            kv[f"wq{j}"] = seal()
        block.sync(f)

        def f(vector, j=j):
            wt(vector, kv[f"wq{j}"])
            ins = vector.tensor_reduce(wqsp[:, j, :],
                                       wqbig[:].rearrange("p d h -> p d h"),
                                       AX.X, ALU.add)
            vec.inc(ins, 1)
            vector.wait_ge(vec.sem, vec.v)
            kv[f"wqr{j}"] = vec.v
        block.vector(f)

    def f(sync):
        sync.wait_ge(vec.sem, kv["wqr3"])
        agw = ag_wq_in.ap().rearrange("(j p) d -> j p d", p=128)
        for j in range(4):
            dm(sync, out=agw[j], in_=wqsp[:, j, :])
        kv["wq_in"] = seal()
    block.sync(f)

    def f(gpsimd):
        wt(gpsimd, kv["wq_in"])
        ins = gpsimd.collective_compute(
            "AllGather", ALU.bypass, replica_groups=[list(range(8))],
            ins=[ag_wq_in.ap().opt()], outs=[ag_wq_out.ap().opt()])
        cc.inc(ins, 1)
        kv["ag_wq"] = cc.v
    block.gpsimd(f)

    gWq2.__exit__(None, None, None)
    gWq.__exit__(None, None, None)

    xnT = scA("xnT", [128, NKD, TOK])
    xnhT = scA("xnhT", [128, NKD, 3])
    gA2 = []

    def scA2(name, shape):
        g = nc.sbuf_tensor(name, shape, F32)
        gA2.append(g)
        return g.__enter__()

    xtok = scA2("xtok", [128, D])
    cstat = scA2("cstat", [128, 8])
    rs2c = scA2("rs2c", [128, 8])
    scr_sq = scA2("scr_sq", [128, D])
    xh_sb = scA2("xh_sb", [3, D])
    rs2t = scA2("rs2t", [8, 128])
    rs2h = scA2("rs2h", [3, 1])
    rs2h_row = scA2("rs2h_row", [1, 3])

    def f(scalar):
        wt(scalar, kv["consts"])
        act.inc(scalar.activation(negA[:], negA[:], AF.Exp), 1)
        kv["negA_e"] = act.v
    block.scalar(f)

    def f(vector):
        vector.wait_ge(act.sem, kv["negA_e"])
        vec.inc(vector.tensor_scalar_mul(negA[:], negA[:], -1.0), 1)
        vector.wait_ge(vec.sem, vec.v)
        vec.inc(vector.tensor_reduce(bqs[:], bq32[:], AX.X, ALU.add), 1)
        vector.wait_ge(vec.sem, vec.v)
        kv["smalls"] = vec.v
    block.vector(f)

    def f(tensor):
        wt(tensor, kv["consts"])
        ins = nc.tensor.matmul(ps[6][:, 0:64], on1[:], dvrow[:],
                               start=True, stop=True)
        mmc.inc(ins, 1)
        kv["db_mm"] = mmc.v
    block.tensor(f)

    def f(vector):
        vector.wait_ge(mmc.sem, kv["db_mm"])
        vec.inc(vector.tensor_copy(db64[:], ps[6][:, 0:64]), 1)
        vector.wait_ge(vec.sem, vec.v)
        kv["db"] = vec.v
    block.vector(f)

    def emit_ln(tag, xbuf, nrow, rs_out, out_fn, dma_key):
        def f(vector, tag=tag):
            wt(vector, kv[dma_key])
            mu = cstat[0:nrow, 0:1]
            sq = cstat[0:nrow, 1:2]
            vec.inc(vector.tensor_reduce(mu, xbuf[0:nrow, :], AX.X, ALU.add), 1)
            vector.wait_ge(vec.sem, vec.v)
            vec.inc(vector.tensor_mul(scr_sq[0:nrow, :], xbuf[0:nrow, :],
                                      xbuf[0:nrow, :]), 1)
            vector.wait_ge(vec.sem, vec.v)
            vec.inc(vector.tensor_reduce(sq, scr_sq[0:nrow, :], AX.X, ALU.add), 1)
            vector.wait_ge(vec.sem, vec.v)
            vec.inc(vector.tensor_scalar_mul(mu, mu, 1.0 / D), 1)
            vector.wait_ge(vec.sem, vec.v)
            vec.inc(vector.tensor_scalar_mul(sq, sq, 1.0 / D), 1)
            vector.wait_ge(vec.sem, vec.v)
            vec.inc(vector.tensor_mul(cstat[0:nrow, 2:3], mu, mu), 1)
            vector.wait_ge(vec.sem, vec.v)
            vec.inc(vector.tensor_tensor(sq, sq, cstat[0:nrow, 2:3], ALU.subtract), 1)
            vector.wait_ge(vec.sem, vec.v)
            vec.inc(vector.tensor_scalar_add(sq, sq, EPS), 1)
            vector.wait_ge(vec.sem, vec.v)
            kv[f"ln1_{tag}"] = vec.v
        block.vector(f)

        def f(scalar, tag=tag):
            scalar.wait_ge(vec.sem, kv[f"ln1_{tag}"])
            act.inc(scalar.activation(cstat[0:nrow, 1:2], cstat[0:nrow, 1:2],
                                      AF.Sqrt), 1)
            kv[f"lns_{tag}"] = act.v
        block.scalar(f)

        def f(vector, tag=tag):
            vector.wait_ge(act.sem, kv[f"lns_{tag}"])
            vec.inc(vector.reciprocal(cstat[0:nrow, 1:2], cstat[0:nrow, 1:2]), 1)
            vector.wait_ge(vec.sem, vec.v)
            vec.inc(vector.tensor_scalar(xbuf[0:nrow, :], xbuf[0:nrow, :],
                                         cstat[0:nrow, 0:1], cstat[0:nrow, 1:2],
                                         ALU.subtract, ALU.mult), 1)
            vector.wait_ge(vec.sem, vec.v)
            vec.inc(vector.tensor_mul(scr_sq[0:nrow, :], xbuf[0:nrow, :],
                                      xbuf[0:nrow, :]), 1)
            vector.wait_ge(vec.sem, vec.v)
            vec.inc(vector.tensor_reduce(cstat[0:nrow, 3:4], scr_sq[0:nrow, :],
                                         AX.X, ALU.add), 1)
            vector.wait_ge(vec.sem, vec.v)
            vec.inc(vector.tensor_scalar(cstat[0:nrow, 3:4], cstat[0:nrow, 3:4],
                                         1.0 / D, EPS, ALU.mult, ALU.add), 1)
            vector.wait_ge(vec.sem, vec.v)
            kv[f"ln2_{tag}"] = vec.v
        block.vector(f)

        def f(scalar, tag=tag):
            scalar.wait_ge(vec.sem, kv[f"ln2_{tag}"])
            act.inc(scalar.activation(cstat[0:nrow, 3:4], cstat[0:nrow, 3:4],
                                      AF.Sqrt), 1)
            kv[f"lns2_{tag}"] = act.v
        block.scalar(f)

        def f(vector, tag=tag):
            vector.wait_ge(act.sem, kv[f"lns2_{tag}"])
            vec.inc(vector.reciprocal(rs_out, cstat[0:nrow, 3:4]), 1)
            vector.wait_ge(vec.sem, vec.v)
            kv[f"lnr_{tag}"] = vec.v
        block.vector(f)

        for rnd in range(4):
            def f(tensor, tag=tag, rnd=rnd):
                if rnd == 0:
                    tensor.wait_ge(vec.sem, kv[f"ln2_{tag}"])
                else:
                    tensor.wait_ge(vec.sem, kv[f"lntr_ev_{tag}_{rnd - 1}"])
                for i in range(8):
                    k = rnd * 8 + i
                    bank = i // 2
                    slot = i % 2
                    ins = nc.tensor.transpose(
                        ps[bank][:, slot * 128:slot * 128 + nrow],
                        xbuf[0:nrow, k * 128:(k + 1) * 128],
                        id_sb[0:nrow, 0:nrow])
                mmc.inc(ins, 1)
                kv[f"lntr_{tag}_{rnd}"] = mmc.v
            block.tensor(f)

            def f(vector, tag=tag, rnd=rnd):
                vector.wait_ge(mmc.sem, kv[f"lntr_{tag}_{rnd}"])
                for i in range(8):
                    k = rnd * 8 + i
                    bank = i // 2
                    slot = i % 2
                    ins = vector.tensor_copy(
                        out_fn(k), ps[bank][:, slot * 128:slot * 128 + nrow])
                vec.inc(ins, 1)
                vector.wait_ge(vec.sem, vec.v)
                kv[f"lntr_ev_{tag}_{rnd}"] = vec.v
            block.vector(f)

    for tau in range(NT):
        def f(sync, tau=tau):
            if tau == 0:
                sync.wait_ge(vec.sem, kv["wqr3"])
                wt(sync, kv["wq_in"])
            sync.wait_ge(vec.sem, kv.get(f"lntr_ev_{tau - 1}_3", 0))
            dm(sync, out=xtok[:],
                                 in_=x_h[tau * 128:(tau + 1) * 128, :])
            kv[f"xld_{tau}"] = seal()
        block.sync(f)

        emit_ln(tau, xtok[:], 128, rs2c[:, tau:tau + 1],
                (lambda k, tau=tau: xnT[:, k, tau * 128:(tau + 1) * 128]),
                f"xld_{tau}")

    def f(sync):
        dm(sync, out=xh_sb[:], in_=xh_h)
        kv["xld_h"] = seal()
    block.sync(f)

    emit_ln("h", xh_sb[:], 3, rs2h[:], (lambda k: xnhT[:, k, :]), "xld_h")

    def f(tensor):
        tensor.wait_ge(vec.sem, kv["lnr_7"])
        mmc.inc(nc.tensor.transpose(ps[4][0:8, 0:128], rs2c[:], id_sb[:]), 1)
        kv["rs2_tr"] = mmc.v
    block.tensor(f)

    def f(vector):
        vector.wait_ge(mmc.sem, kv["rs2_tr"])
        vec.inc(vector.tensor_copy(rs2t[:], ps[4][0:8, 0:128]), 1)
        vector.wait_ge(vec.sem, vec.v)
        kv["rs2_ev"] = vec.v
    block.vector(f)

    def f(sync):
        sync.wait_ge(vec.sem, kv["rs2_ev"])
        dm(sync, out=rs2_scr, in_=rs2t[:])
        wt(sync, TK[-1:])
        dm(sync, 
            out=rs2row[:],
            in_=rs2_scr.rearrange("a b -> (a b)").unsqueeze(0))
        kv["rs2row"] = seal()
    block.sync(f)

    def f(tensor):
        tensor.wait_ge(vec.sem, kv["lnr_h"])
        tensor.wait_ge(vec.sem, kv["rs2_ev"])
        mmc.inc(nc.tensor.transpose(ps[4][0:1, 0:3], rs2h[:], id_sb[0:3, 0:3]), 1)
        kv["rs2h_tr"] = mmc.v
    block.tensor(f)

    def f(vector):
        vector.wait_ge(mmc.sem, kv["rs2h_tr"])
        vec.inc(vector.tensor_copy(rs2h_row[:], ps[4][0:1, 0:3]), 1)
        vector.wait_ge(vec.sem, vec.v)
        kv["rs2h_row"] = vec.v
    block.vector(f)

    def f(tensor):
        tensor.wait_ge(vec.sem, kv["rs2h_row"])
        ins = nc.tensor.matmul(ps[6][:, 64:67], on1[:], rs2h_row[:],
                               start=True, stop=True)
        mmc.inc(ins, 1)
        kv["rs2h_mm"] = mmc.v
    block.tensor(f)

    def f(vector):
        vector.wait_ge(mmc.sem, kv["rs2h_mm"])
        vec.inc(vector.tensor_copy(rs2hb[:], ps[6][:, 64:67]), 1)
        vector.wait_ge(vec.sem, vec.v)
        for k in range(NKD):
            ins = vector.tensor_scalar_mul(xnhT[:, k, :], xnhT[:, k, :],
                                           lnw[:, k:k + 1])
        vec.inc(ins, 1)
        vector.wait_ge(vec.sem, vec.v)
        kv["xnhT_done"] = vec.v
    block.vector(f)

    for g in reversed(gA2):
        g.__exit__(None, None, None)

    wbig = scA("wbig", [128, 8192])
    wla = wbig[:].rearrange("p (k c) -> p k c", c=256)
    wbuf = wbig[:].rearrange("p (b k c) -> p b k c", b=2, c=128)
    h_sb = scA("h_sb", [128, 2, TOK])
    hg_sb = h_sb
    gB1 = []

    def scB1(name, shape):
        g = nc.sbuf_tensor(name, shape, F32)
        gB1.append(g)
        return g.__enter__()

    wgate = scB1("wgate", [128, NKD, 28])
    logits = scB1("logits", [28, TOK])
    lgT = scB1("lgT", [128, NT, 28])
    gstat = scB1("gstat", [128, 8])
    gmask = scB1("gmask", [128, 2, 28])
    gateT = scB1("gateT", [128, NT, 28])
    gate_f = scB1("gate_f", [28, TOK])

    def f(sync):
        wt(sync, kv["rs2row"])
        sync.wait_ge(mmc.sem, kv["rs2h_mm"])
        sync.wait_ge(vec.sem, kv["xnhT_done"])
        dm(sync, out=wgate[:], in_=w_gate_h.transpose([1, 0, 2]))
        dm(sync, out=wla, in_=w_la_h.transpose([1, 0, 2]))
        if DEBUG:
            sync.wait_ge(vec.sem, kv["lntr_ev_7_3"])
            wt(sync, kv["rs2row"])
            for k in range(NKD):
                dm(sync, out=dbg_xnT[k], in_=xnT[:, k, :])
            dm(sync, out=dbg_r2, in_=rs2row[:])
            dm(sync, out=dbg_r2t, in_=rs2t[:])
            dm(sync, out=dbg_r2c, in_=rs2c[:])
        kv["b1_w"] = seal()
    block.sync(f)

    for half in range(2):
        def f(tensor, half=half):
            wt(tensor, kv["b1_w"])
            if half == 0:
                tensor.wait_ge(vec.sem, kv["lntr_ev_7_3"])
            else:
                tensor.wait_ge(vec.sem, kv["b1_gev_0"])
            for k in range(NKD):
                ins = nc.tensor.matmul(
                    ps[0][0:28, 0:512], wgate[:, k, :],
                    xnT[:, k, half * 512:(half + 1) * 512],
                    start=(k == 0), stop=(k == NKD - 1))
            mmc.inc(ins, 1)
            kv[f"b1_gmm_{half}"] = mmc.v
        block.tensor(f)

        def f(vector, half=half):
            vector.wait_ge(mmc.sem, kv[f"b1_gmm_{half}"])
            ins = vector.tensor_copy(logits[:, half * 512:(half + 1) * 512],
                                     ps[0][0:28, 0:512])
            vec.inc(ins, 1)
            vector.wait_ge(vec.sem, vec.v)
            kv[f"b1_gev_{half}"] = vec.v
        block.vector(f)

    for t in range(NT):
        def f(tensor, t=t):
            if t == 0:
                tensor.wait_ge(vec.sem, kv["b1_gev_1"])
            else:
                tensor.wait_ge(vec.sem, kv[f"b1_ltr_ev_{t - 1}"])
            ins = nc.tensor.transpose(ps[1][0:128, 0:28],
                                      logits[:, t * 128:(t + 1) * 128],
                                      id_sb[0:28, 0:28])
            mmc.inc(ins, 1)
            kv[f"b1_ltr_{t}"] = mmc.v
        block.tensor(f)

        def f(vector, t=t):
            vector.wait_ge(mmc.sem, kv[f"b1_ltr_{t}"])
            vec.inc(vector.tensor_copy(lgT[:, t, :], ps[1][0:128, 0:28]), 1)
            vector.wait_ge(vec.sem, vec.v)
            kv[f"b1_ltr_ev_{t}"] = vec.v
        block.vector(f)

    for t in range(NT):
        def f(vector, t=t):
            vector.wait_ge(vec.sem, kv[f"b1_ltr_ev_{t}"])
            lt = lgT[:, t, :]
            m1 = gstat[:, 0:1]
            m2 = gstat[:, 1:2]
            vec.inc(vector.reduce_max(m1, lt, AX.X), 1)
            vector.wait_ge(vec.sem, vec.v)
            vec.inc(vector.tensor_tensor(gmask[:, 0, :], lt,
                                         m1.broadcast_to((128, 28)), ALU.is_ge), 1)
            vector.wait_ge(vec.sem, vec.v)
            vec.inc(vector.scalar_tensor_tensor(gmask[:, 1, :], gmask[:, 0, :],
                                                -1e30, lt, ALU.mult, ALU.add), 1)
            vector.wait_ge(vec.sem, vec.v)
            vec.inc(vector.reduce_max(m2, gmask[:, 1, :], AX.X), 1)
            vector.wait_ge(vec.sem, vec.v)
            vec.inc(vector.tensor_tensor(gmask[:, 1, :], gmask[:, 1, :],
                                         m2.broadcast_to((128, 28)), ALU.is_ge), 1)
            vector.wait_ge(vec.sem, vec.v)
            vec.inc(vector.tensor_scalar_mul(m2, m2, -1.0), 1)
            vector.wait_ge(vec.sem, vec.v)
            kv[f"b1_t2a_{t}"] = vec.v
        block.vector(f)

        def f(scalar, t=t):
            scalar.wait_ge(vec.sem, kv[f"b1_t2a_{t}"])
            ins = scalar.activation(gstat[:, 2:3], gstat[:, 0:1], AF.Sigmoid,
                                    bias=gstat[:, 1:2])
            act.inc(ins, 1)
            kv[f"b1_t2b_{t}"] = act.v
        block.scalar(f)

        def f(vector, t=t):
            vector.wait_ge(act.sem, kv[f"b1_t2b_{t}"])
            vec.inc(vector.tensor_scalar(gstat[:, 3:4], gstat[:, 2:3], -1.0, 1.0,
                                         ALU.mult, ALU.add), 1)
            vector.wait_ge(vec.sem, vec.v)
            vec.inc(vector.tensor_scalar_mul(gmask[:, 0, :], gmask[:, 0, :],
                                             gstat[:, 2:3]), 1)
            vector.wait_ge(vec.sem, vec.v)
            vec.inc(vector.scalar_tensor_tensor(gateT[:, t, :], gmask[:, 1, :],
                                                gstat[:, 3:4], gmask[:, 0, :],
                                                ALU.mult, ALU.add), 1)
            vector.wait_ge(vec.sem, vec.v)
            kv[f"b1_t2c_{t}"] = vec.v
        block.vector(f)

    for t in range(NT):
        def f(tensor, t=t):
            tensor.wait_ge(vec.sem, kv[f"b1_t2c_{t}"])
            if t >= 1:
                tensor.wait_ge(vec.sem, kv[f"b1_gtr_ev_{t - 1}"])
            ins = nc.tensor.transpose(ps[1][0:28, 0:128], gateT[:, t, :], id_sb[:])
            mmc.inc(ins, 1)
            kv[f"b1_gtr_{t}"] = mmc.v
        block.tensor(f)

        def f(vector, t=t):
            vector.wait_ge(mmc.sem, kv[f"b1_gtr_{t}"])
            ins = vector.tensor_copy(gate_f[:, t * 128:(t + 1) * 128],
                                     ps[1][0:28, 0:128])
            vec.inc(ins, 1)
            vector.wait_ge(vec.sem, vec.v)
            kv[f"b1_gtr_ev_{t}"] = vec.v
        block.vector(f)

    def f(tensor):
        tensor.wait_ge(vec.sem, kv["b1_gtr_ev_7"])
        for mh in range(2):
            for half in range(2):
                bank = mh * 2 + half
                for k in range(NKD):
                    ins = nc.tensor.matmul(
                        ps[bank][:, 0:512], wla[:, k, mh * 128:(mh + 1) * 128],
                        xnT[:, k, half * 512:(half + 1) * 512],
                        start=(k == 0), stop=(k == NKD - 1))
                mmc.inc(ins, 1)
        kv["b1_hmm"] = mmc.v
    block.tensor(f)

    def f(vector):
        vector.wait_ge(mmc.sem, kv["b1_hmm"])
        for mh in range(2):
            for half in range(2):
                bank = mh * 2 + half
                ins = vector.tensor_copy(h_sb[:, mh, half * 512:(half + 1) * 512],
                                         ps[bank][:, 0:512])
                vec.inc(ins, 1)
                vector.wait_ge(vec.sem, vec.v)
        kv["b1_hev"] = vec.v
    block.vector(f)

    def f(tensor):
        tensor.wait_ge(vec.sem, kv["b1_hev"])
        for mh in range(2):
            for half in range(2):
                bank = mh * 2 + half
                ins = nc.tensor.matmul(ps[bank][:, 0:512],
                                       eselg[:, mh * 128:(mh + 1) * 128],
                                       gate_f[:, half * 512:(half + 1) * 512],
                                       start=True, stop=True)
                mmc.inc(ins, 1)
        kv["b1_gexp"] = mmc.v
    block.tensor(f)

    def f(vector):
        vector.wait_ge(mmc.sem, kv["b1_gexp"])
        for mh in range(2):
            for half in range(2):
                bank = mh * 2 + half
                ins = vector.tensor_tensor(
                    hg_sb[:, mh, half * 512:(half + 1) * 512],
                    h_sb[:, mh, half * 512:(half + 1) * 512],
                    ps[bank][:, 0:512], ALU.mult)
                vec.inc(ins, 1)
                vector.wait_ge(vec.sem, vec.v)
        kv["b1_hg"] = vec.v
    block.vector(f)

    if DEBUG:
        def f(sync):
            sync.wait_ge(vec.sem, kv["b1_hg"])
            dm(sync, out=dbg_h[0], in_=h_sb[:, 0, :])
            dm(sync, out=dbg_h[1], in_=h_sb[:, 1, :])
            dm(sync, out=dbg_gf, in_=gate_f[:])
            dm(sync, out=dbg_lg, in_=logits[:])
            dm(sync, out=dbg_lgT, in_=lgT[:])
            dm(sync, out=dbg_gT, in_=gateT[:])
        block.sync(f)

    for g in reversed(gB1):
        g.__exit__(None, None, None)
    gB2 = []

    def scB2(name, shape):
        g = nc.sbuf_tensor(name, shape, F32)
        gB2.append(g)
        return g.__enter__()

    wlbm = scB2("wlbm", [128, 2, 2, 128])
    ffn_ev = scB2("ffn_ev", [128, 2, TOK])

    for m in range(NKD):
        b = m % 2

        def f(sync, m=m, b=b):
            if m >= 2:
                sync.wait_ge(mmc.sem, kv[f"b2_mm_{m - 2}"])
            else:
                sync.wait_ge(mmc.sem, kv["b1_gexp"])
                wt(sync, dpool.all_tokens())
            dm(sync, out=wbuf[:, b], in_=w_base_h[m])
            dm(sync, out=wlbm[:, b],
                                   in_=w_lb_h[m].transpose([1, 0, 2]))
            kv[f"b2_w_{m}"] = seal()
        block.sync(f)

        def f(tensor, m=m, b=b):
            if m == 0:
                tensor.wait_ge(vec.sem, kv["b1_hg"])
            wt(tensor, kv[f"b2_w_{m}"])
            if m >= 2:
                tensor.wait_ge(vec.sem, kv[f"b2_ev_{m - 2}"])
            for half in range(2):
                bank = b * 2 + half
                for k in range(NKD):
                    nc.tensor.matmul(ps[bank][:, 0:512], wbuf[:, b, k, :],
                                     xnT[:, k, half * 512:(half + 1) * 512],
                                     start=(k == 0), stop=False)
                for k2 in range(2):
                    ins = nc.tensor.matmul(
                        ps[bank][:, 0:512], wlbm[:, b, k2, :],
                        hg_sb[:, k2, half * 512:(half + 1) * 512],
                        start=False, stop=(k2 == 1))
                mmc.inc(ins, 1)
            kv[f"b2_mm_{m}"] = mmc.v
        block.tensor(f)

        def f(vector, m=m, b=b):
            vector.wait_ge(mmc.sem, kv[f"b2_mm_{m}"])
            if m >= 2:
                wt(vector, kv[f"b2_st_{m - 2}"])
            for half in range(2):
                bank = b * 2 + half
                ins = vector.tensor_copy(ffn_ev[:, b, half * 512:(half + 1) * 512],
                                         ps[bank][:, 0:512])
                vec.inc(ins, 1)
                vector.wait_ge(vec.sem, vec.v)
            kv[f"b2_ev_{m}"] = vec.v
        block.vector(f)

        def f(sync, m=m, b=b):
            sync.wait_ge(vec.sem, kv[f"b2_ev_{m}"])
            dm(sync, out=ffn_scr[m], in_=ffn_ev[:, b])
            kv[f"b2_st_{m}"] = seal()
        block.sync(f)

    for g in reversed(gB2):
        g.__exit__(None, None, None)
    gB3 = []

    def scB3(name, shape):
        g = nc.sbuf_tensor(name, shape, F32)
        gB3.append(g)
        return g.__enter__()

    zx_ev = scB3("zx_ev", [128, 2, TOK])
    hstat = scB3("hstat", [128, 3])
    rs2b = scB3("rs2b", [128, TOK])

    def f(vector):
        vector.wait_ge(mmc.sem, kv["b2_mm_31"])
        for k in range(NKD):
            ins = vector.tensor_scalar_mul(xnT[:, k, :], xnT[:, k, :],
                                           lnw[:, k:k + 1])
            vec.inc(ins, 1)
            vector.wait_ge(vec.sem, vec.v)
        kv["b3_lnw"] = vec.v
    block.vector(f)

    def f(tensor):
        wt(tensor, kv["rs2row"])
        tensor.wait_ge(vec.sem, kv["b3_lnw"])
        for half in range(2):
            ins = nc.tensor.matmul(ps[4 + half][:, 0:512], on1[:],
                                   rs2row[0:1, half * 512:(half + 1) * 512],
                                   start=True, stop=True)
            mmc.inc(ins, 1)
        kv["b3_rs2b"] = mmc.v
    block.tensor(f)

    def f(vector):
        vector.wait_ge(mmc.sem, kv["b3_rs2b"])
        wt(vector, kv["b2_st_31"])
        wt(vector, kv["b2_st_30"])
        for half in range(2):
            ins = vector.tensor_copy(rs2b[:, half * 512:(half + 1) * 512],
                                     ps[4 + half][:, 0:512])
            vec.inc(ins, 1)
            vector.wait_ge(vec.sem, vec.v)
        kv["b3_rs2bc"] = vec.v
    block.vector(f)

    for m in range(MIN):
        b = m % 2

        def f(sync, m=m, b=b):
            if m >= 2:
                sync.wait_ge(mmc.sem, kv[f"b3_mm_{m - 2}"])
            else:
                sync.wait_ge(mmc.sem, kv["b2_mm_31"])
            dm(sync, out=wbuf[:, b], in_=w_in_h[m])
            kv[f"b3_w_{m}"] = seal()
        block.sync(f)

        def f(tensor, m=m, b=b):
            if m == 0:
                tensor.wait_ge(vec.sem, kv["b3_rs2bc"])
                tensor.wait_ge(vec.sem, kv["xnhT_done"])
            wt(tensor, kv[f"b3_w_{m}"])
            if m >= 2:
                tensor.wait_ge(vec.sem, kv[f"b3_ev_{m - 2}"])
            for half in range(2):
                bank = b * 2 + half
                for k in range(NKD):
                    ins = nc.tensor.matmul(
                        ps[bank][:, 0:512], wbuf[:, b, k, :],
                        xnT[:, k, half * 512:(half + 1) * 512],
                        start=(k == 0), stop=(k == NKD - 1))
                mmc.inc(ins, 1)
            if 32 <= m < 66:
                if m >= 33:
                    tensor.wait_ge(vec.sem, kv[f"b3_ev_{m - 1}"])
                for k in range(NKD):
                    ins = nc.tensor.matmul(
                        ps[6][:, 128:131], wbuf[:, b, k, :], xnhT[:, k, :],
                        start=(k == 0), stop=(k == NKD - 1))
                mmc.inc(ins, 1)
            kv[f"b3_mm_{m}"] = mmc.v
        block.tensor(f)

        def f(vector, m=m, b=b):
            vector.wait_ge(mmc.sem, kv[f"b3_mm_{m}"])
            if m >= 2:
                wt(vector, kv[f"b3_st_{m - 2}"])
            else:
                wt(vector, kv["b2_st_31"])
            for half in range(2):
                bank = b * 2 + half
                ins = vector.scalar_tensor_tensor(
                    zx_ev[:, b, half * 512:(half + 1) * 512],
                    ps[bank][:, 0:512], mupv[:, m:m + 1],
                    rs2b[:, half * 512:(half + 1) * 512], ALU.mult, ALU.mult)
                vec.inc(ins, 1)
                vector.wait_ge(vec.sem, vec.v)
            if 32 <= m < 66:
                ins = vector.tensor_tensor(hstat[:, 0:3], ps[6][:, 128:131],
                                           rs2hb[:], ALU.mult)
                vec.inc(ins, 1)
                vector.wait_ge(vec.sem, vec.v)
                ins = vector.tensor_scalar_mul(halo_my[:, m - 32, :],
                                               hstat[:, 0:3], mupv[:, m:m + 1])
                vec.inc(ins, 1)
                vector.wait_ge(vec.sem, vec.v)
            kv[f"b3_ev_{m}"] = vec.v
        block.vector(f)

        def f(sync, m=m, b=b):
            sync.wait_ge(vec.sem, kv[f"b3_ev_{m}"])
            dm(sync, out=zx_scr[m], in_=zx_ev[:, b])
            kv[f"b3_st_{m}"] = seal()
        block.sync(f)

    barrier()
    for g in reversed(gB3):
        g.__exit__(None, None, None)
    for g in reversed(gA):
        g.__exit__(None, None, None)

    if PHASE_LIMIT < 2:
        block_cm.__exit__(None, None, None)
        return nc

    gC = []

    def scC(name, shape):
        g = nc.sbuf_tensor(name, shape, F32)
        gC.append(g)
        return g.__enter__()

    h_run = scC("h_run", [128, D])
    conv_in = scC("conv_in", [128, 2, NCV, 131])
    sx = scC("sx", [128, NCV, 128])
    xsT = scC("xsT", [128, NKD, 128])
    btq = scC("btq", [128, 128])
    sdt = scC("sdt", [128, 128])
    sdtT = scC("sdtT", [128, 128])
    dtmp = scC("dtmp", [64, 128])
    dtr = scC("dtr", [64, 128])
    ones64 = scC("ones64", [64, 128])
    srow = scC("srow", [1, 8192])
    ytok = scC("ytok", [128, 2, D])
    dsb = scC("dsb", [128, 2, 512])
    wtsb = scC("wtsb", [128, 2, 512])
    w4 = scC("w4", [128, 2, 8])
    dtxw = scC("dtxw", [128, 2, 256])
    t1sb = scC("t1sb", [128, 256])

    def f(vector):
        vec.inc(vector.memset(ones64[:], 1.0), 1)
        vector.wait_ge(vec.sem, vec.v)
        vec.inc(vector.memset(h_run[:], 0.0), 1)
        vector.wait_ge(vec.sem, vec.v)
        kv["c_init"] = vec.v
    block.vector(f)

    for q in range(NQ):
        cb = q % 2

        def f(sync, q=q, cb=cb):
            if q >= 2:
                sync.wait_ge(act.sem, kv[f"c_silu_{q - 2}"])
            if q >= 1:
                sync.wait_ge(vec.sem, kv[f"c_S_{q - 1}"])
            c0 = q * 128 - 3
            for g in range(4):
                j0, j1 = g * 9, min((g + 1) * 9, NCV)
                if q == 0:
                    dm(sync,
                        out=conv_in[:, cb, j0:j1, 3:131],
                        in_=zx_scr[32 + j0:32 + j1, :, 0:128].transpose([1, 0, 2]))
                else:
                    dm(sync, 
                        out=conv_in[:, cb, j0:j1, :],
                        in_=zx_scr[32 + j0:32 + j1, :, c0:c0 + 131].transpose([1, 0, 2]))
            dm(sync, out=dtmp[:],
                                 in_=zx_scr[66, 0:64, q * 128:(q + 1) * 128])
            kv[f"c_ld_{q}"] = seal()
        block.sync(f)

        def f(vector, q=q, cb=cb):
            wt(vector, kv[f"c_ld_{q}"])
            if q == 0:
                for j in range(NCV):
                    ins = vector.tensor_copy(conv_in[:, cb, j, 0:3],
                                             halo_my[:, j, :])
                vec.inc(ins, 1)
                vector.wait_ge(vec.sem, vec.v)
            if q >= 1:
                vector.wait_ge(act.sem, kv[f"c_silu_{q - 1}"])
                vector.wait_ge(vec.sem, kv[f"c_trev_{q - 1}_B"])
                vector.wait_ge(mmc.sem, kv[f"c_GT_{q - 1}"])
            for j in range(NCV):
                vector.tensor_scalar_mul(sx[:, j, :], conv_in[:, cb, j, 0:128],
                                         convw[:, j, 0:1])
                for kk in range(1, 4):
                    ins = vector.scalar_tensor_tensor(
                        sx[:, j, :], conv_in[:, cb, j, kk:kk + 128],
                        convw[:, j, kk:kk + 1], sx[:, j, :], ALU.mult, ALU.add)
            vec.inc(ins, 1)
            vector.wait_ge(vec.sem, vec.v)
            kv[f"c_conv_{q}"] = vec.v
            ins = vector.tensor_scalar(sdt[64:128, :], dtmp[:], dtb[:, 0:1], None,
                                       ALU.add)
            vec.inc(ins, 1)
            vector.wait_ge(vec.sem, vec.v)
            ins = vector.tensor_scalar_max(dtr[:], sdt[64:128, :], 0.0)
            vec.inc(ins, 1)
            vector.wait_ge(vec.sem, vec.v)
            kv[f"c_dt1_{q}"] = vec.v
        block.vector(f)

        def f(scalar, q=q):
            scalar.wait_ge(vec.sem, kv[f"c_conv_{q}"])
            ins = scalar.activation(sx[:].rearrange("p a b -> p (a b)"),
                                    sx[:].rearrange("p a b -> p (a b)"), AF.Silu)
            act.inc(ins, 1)
            kv[f"c_silu_{q}"] = act.v
            act.inc(scalar.activation(dtmp[:], sdt[64:128, :], AF.Abs), 1)
            act.inc(scalar.activation(dtmp[:], dtmp[:], AF.Exp, scale=-1.0), 1)
            act.inc(scalar.activation(dtmp[:], dtmp[:], AF.Ln, bias=1.0), 1)
            kv[f"c_dt2_{q}"] = act.v
        block.scalar(f)

        def f(vector, q=q):
            vector.wait_ge(act.sem, kv[f"c_dt2_{q}"])
            vec.inc(vector.tensor_add(sdt[64:128, :], dtr[:], dtmp[:]), 1)
            vector.wait_ge(vec.sem, vec.v)
            vec.inc(vector.tensor_scalar_mul(dtmp[:], sdt[64:128, :], negA[:, 0:1]), 1)
            vector.wait_ge(vec.sem, vec.v)
            ins = vector.tensor_tensor_scan(sdt[0:64, :], dtmp[:], ones64[:],
                                            0.0, ALU.add, ALU.mult)
            vec.inc(ins, 1)
            vector.wait_ge(vec.sem, vec.v)
            kv[f"c_S_{q}"] = vec.v
        block.vector(f)

        for rnd in range(4):
            def f(tensor, q=q, rnd=rnd):
                if rnd == 0:
                    tensor.wait_ge(act.sem, kv[f"c_silu_{q}"])
                    if q >= 1:
                        tensor.wait_ge(vec.sem, kv[f"c_hupd_{q - 1}_15"])
                else:
                    tensor.wait_ge(vec.sem, kv[f"c_trev_{q}_{rnd - 1}"])
                for i in range(8):
                    j = rnd * 8 + i
                    bank = i // 2
                    slot = i % 2
                    ins = nc.tensor.transpose(
                        ps[bank][:, slot * 128:(slot + 1) * 128],
                        sx[:, j, :], id_sb[:])
                mmc.inc(ins, 1)
                kv[f"c_tr_{q}_{rnd}"] = mmc.v
            block.tensor(f)

            def f(vector, q=q, rnd=rnd):
                vector.wait_ge(mmc.sem, kv[f"c_tr_{q}_{rnd}"])
                for i in range(8):
                    j = rnd * 8 + i
                    bank = i // 2
                    slot = i % 2
                    ins = vector.tensor_copy(xsT[:, j, :],
                                             ps[bank][:, slot * 128:(slot + 1) * 128])
                vec.inc(ins, 1)
                vector.wait_ge(vec.sem, vec.v)
                kv[f"c_trev_{q}_{rnd}"] = vec.v
            block.vector(f)

        def f(tensor, q=q):
            tensor.wait_ge(vec.sem, kv[f"c_trev_{q}_3"])
            mmc.inc(nc.tensor.transpose(ps[0][:, 0:128], sx[:, 32, :], id_sb[:]), 1)
            tensor.wait_ge(vec.sem, kv[f"c_S_{q}"])
            mmc.inc(nc.tensor.transpose(ps[1][:, 0:128], sdt[:], id_sb[:]), 1)
            kv[f"c_trBS_{q}"] = mmc.v
        block.tensor(f)

        def f(vector, q=q):
            vector.wait_ge(mmc.sem, kv[f"c_trBS_{q}"])
            vec.inc(vector.tensor_copy(btq[:], ps[0][:, 0:128]), 1)
            vector.wait_ge(vec.sem, vec.v)
            vec.inc(vector.tensor_copy(sdtT[:], ps[1][:, 0:128]), 1)
            vector.wait_ge(vec.sem, vec.v)
            vec.inc(vector.tensor_copy(cm_all[:, q, :], sx[:, 33, :]), 1)
            vector.wait_ge(vec.sem, vec.v)
            kv[f"c_trev_{q}_B"] = vec.v
        block.vector(f)

        def f(tensor, q=q):
            tensor.wait_ge(vec.sem, kv[f"c_trev_{q}_B"])
            ins = nc.tensor.matmul(ps[7][:, 0:128], sx[:, 32, :], sx[:, 33, :],
                                   start=True, stop=True)
            mmc.inc(ins, 1)
            kv[f"c_GT_{q}"] = mmc.v
        block.tensor(f)

        def f(sync, q=q):
            sync.wait_ge(vec.sem, kv[f"c_S_{q}"])
            if q >= 1:
                sync.wait_ge(mmc.sem, kv[f"c_srb_{q - 1}_15"])
            dm(sync, out=s_scr[q], in_=sdt[0:64, :])
            wt(sync, TK[-1:])
            dm(sync, 
                out=srow[:], in_=s_scr[q].rearrange("a b -> (a b)").unsqueeze(0))
            kv[f"c_srow_{q}"] = seal()
        block.sync(f)

        def f(scalar, q=q):
            scalar.wait_ge(vec.sem, kv[f"c_trev_{q}_B"])
            act.inc(scalar.activation(e_all[:, q, :], sdtT[:, 0:64], AF.Exp), 1)
            wt(scalar, kv[f"c_srow_{q}"])
            sl_row = srow[0:1, :].rearrange("a (h t) -> a h t", t=128)[:, :, 127]
            act.inc(scalar.activation(lamr[:, q, :], sl_row, AF.Exp), 1)
            kv[f"c_E_{q}"] = act.v
        block.scalar(f)

        def f(tensor, q=q):
            tensor.wait_ge(act.sem, kv[f"c_E_{q}"])
            sl_row = srow[0:1, :].rearrange("a (h t) -> a h t", t=128)[:, :, 127]
            ins = nc.tensor.matmul(ps[7][:, 128:192], on1[:], sl_row,
                                   start=True, stop=True)
            mmc.inc(ins, 1)
            ins = nc.tensor.matmul(ps[7][:, 192:256], on1[:], lamr[:, q, :],
                                   start=True, stop=True)
            mmc.inc(ins, 1)
            kv[f"c_SLb_{q}"] = mmc.v
        block.tensor(f)

        if DEBUG:
            def f(sync, q=q):
                sync.wait_ge(act.sem, kv[f"c_E_{q}"])
                dm(sync, out=dbg_sdt[q], in_=sdt[:])
                dm(sync, out=dbg_srow[q:q+1, :], in_=srow[0:1, :])
                kv[f"c_dbg_{q}"] = seal()
            block.sync(f)

        xsflat = xsT[:].rearrange("p a b -> p (a b)")
        for hg in range(16):
            h0 = hg * 4
            ab = hg % 2

            def f(tensor, q=q, hg=hg, h0=h0, ab=ab):
                if hg == 0:
                    tensor.wait_ge(mmc.sem, kv[f"c_SLb_{q}"])
                if hg >= 1:
                    tensor.wait_ge(vec.sem, kv[f"c_D_{q}_{hg - 1}"])
                ins = nc.tensor.matmul(ps[6][:, 0:512], on1[:],
                                       srow[0:1, h0 * 128:(h0 + 4) * 128],
                                       start=True, stop=True)
                mmc.inc(ins, 1)
                kv[f"c_srb_{q}_{hg}"] = mmc.v
            block.tensor(f)

            def f(vector, q=q, hg=hg, h0=h0, ab=ab):
                vector.wait_ge(mmc.sem, kv[f"c_srb_{q}_{hg}"])
                if hg >= 2:
                    vector.wait_ge(act.sem, kv[f"c_expD_{q}_{hg - 2}"])
                d3 = dsb[:, ab, :].rearrange("p (h t) -> p h t", t=128)
                sr3 = ps[6][:, 0:512].rearrange("p (h t) -> p h t", t=128)
                stc = sdtT[:, h0:h0 + 4].unsqueeze(2).broadcast_to((128, 4, 128))
                vec.inc(vector.tensor_tensor(d3, sr3, stc, ALU.subtract), 1)
                vector.wait_ge(vec.sem, vec.v)
                mn3 = mneg[:].unsqueeze(1).broadcast_to((128, 4, 128))
                vec.inc(vector.tensor_tensor(d3, d3, mn3, ALU.add), 1)
                vector.wait_ge(vec.sem, vec.v)
                ins = vector.tensor_tensor(w4[:, ab, 0:4],
                                           ps[7][:, 128 + h0:128 + h0 + 4],
                                           sdtT[:, h0:h0 + 4], ALU.subtract)
                vec.inc(ins, 1)
                vector.wait_ge(vec.sem, vec.v)
                kv[f"c_D_{q}_{hg}"] = vec.v
            block.vector(f)

            def f(scalar, q=q, hg=hg, ab=ab):
                scalar.wait_ge(vec.sem, kv[f"c_D_{q}_{hg}"])
                act.inc(scalar.activation(dsb[:, ab, :], dsb[:, ab, :], AF.Exp), 1)
                act.inc(scalar.activation(w4[:, ab, 4:8], w4[:, ab, 0:4], AF.Exp), 1)
                kv[f"c_expD_{q}_{hg}"] = act.v
            block.scalar(f)

            def f(vector, q=q, hg=hg, h0=h0, ab=ab):
                vector.wait_ge(act.sem, kv[f"c_expD_{q}_{hg}"])
                if hg >= 2:
                    vector.wait_ge(mmc.sem, kv[f"c_hgmm_{q}_{hg - 2}"])
                w3 = wtsb[:, ab, :].rearrange("p (h t) -> p h t", t=128)
                e3 = dsb[:, ab, :].rearrange("p (h t) -> p h t", t=128)
                dtc = sdtT[:, 64 + h0:64 + h0 + 4].unsqueeze(2) \
                    .broadcast_to((128, 4, 128))
                vec.inc(vector.tensor_tensor(w3, e3, dtc, ALU.mult), 1)
                vector.wait_ge(vec.sem, vec.v)
                gt3 = ps[7][:, 0:128].unsqueeze(1).broadcast_to((128, 4, 128))
                vec.inc(vector.tensor_tensor(w3, w3, gt3, ALU.mult), 1)
                vector.wait_ge(vec.sem, vec.v)
                vec.inc(vector.tensor_tensor(w4[:, ab, 0:4], w4[:, ab, 4:8],
                                             sdtT[:, 64 + h0:64 + h0 + 4],
                                             ALU.mult), 1)
                vector.wait_ge(vec.sem, vec.v)
                dx3 = dtxw[:, ab, :].rearrange("p (h d) -> p h d", d=64)
                xs3 = xsflat[:, h0 * 64:(h0 + 4) * 64] \
                    .rearrange("p (h d) -> p h d", d=64)
                wb3 = w4[:, ab, 0:4].unsqueeze(2).broadcast_to((128, 4, 64))
                vec.inc(vector.tensor_tensor(dx3, xs3, wb3, ALU.mult), 1)
                vector.wait_ge(vec.sem, vec.v)
                kv[f"c_WT_{q}_{hg}"] = vec.v
            block.vector(f)

            def f(tensor, q=q, hg=hg, h0=h0, ab=ab):
                tensor.wait_ge(vec.sem, kv[f"c_WT_{q}_{hg}"])
                if hg >= 2:
                    tensor.wait_ge(vec.sem, kv[f"c_hupd_{q}_{hg - 2}"])
                for i in range(4):
                    h = h0 + i
                    nc.tensor.matmul(ps[0 + ab][:, i * 64:(i + 1) * 64],
                                     wtsb[:, ab, i * 128:(i + 1) * 128],
                                     xsflat[:, h * 64:(h + 1) * 64],
                                     start=True, stop=True)
                    nc.tensor.matmul(ps[4 + ab][:, i * 64:(i + 1) * 64],
                                     cm_all[:, q, :],
                                     h_run[:, h * 64:(h + 1) * 64],
                                     start=True, stop=True)
                    ins = nc.tensor.matmul(ps[2 + ab][:, i * 64:(i + 1) * 64],
                                           btq[:],
                                           dtxw[:, ab, i * 64:(i + 1) * 64],
                                           start=True, stop=True)
                mmc.inc(ins, 1)
                kv[f"c_hgmm_{q}_{hg}"] = mmc.v
            block.tensor(f)

            def f(vector, q=q, hg=hg, h0=h0, ab=ab, cb=cb):
                vector.wait_ge(mmc.sem, kv[f"c_hgmm_{q}_{hg}"])
                if hg == 0 and q >= 2:
                    wt(vector, kv[f"c_yst_{q - 2}"])
                t13 = t1sb[:].rearrange("p (h d) -> p h d", d=64)
                pb3 = ps[4 + ab][:, 0:256].rearrange("p (h d) -> p h d", d=64)
                e3 = e_all[:, q, h0:h0 + 4].unsqueeze(2).broadcast_to((128, 4, 64))
                vec.inc(vector.tensor_tensor(t13, pb3, e3, ALU.mult), 1)
                vector.wait_ge(vec.sem, vec.v)
                vec.inc(vector.tensor_add(t1sb[:], t1sb[:], ps[0 + ab][:, 0:256]), 1)
                vector.wait_ge(vec.sem, vec.v)
                y3 = ytok[:, cb, h0 * 64:(h0 + 4) * 64] \
                    .rearrange("p (h d) -> p h d", d=64)
                xs3 = xsflat[:, h0 * 64:(h0 + 4) * 64] \
                    .rearrange("p (h d) -> p h d", d=64)
                db3 = db64[:, h0:h0 + 4].unsqueeze(2).broadcast_to((128, 4, 64))
                vec.inc(vector.tensor_tensor(y3, xs3, db3, ALU.mult), 1)
                vector.wait_ge(vec.sem, vec.v)
                vec.inc(vector.tensor_add(ytok[:, cb, h0 * 64:(h0 + 4) * 64],
                                          ytok[:, cb, h0 * 64:(h0 + 4) * 64],
                                          t1sb[:]), 1)
                vector.wait_ge(vec.sem, vec.v)
                hr3 = h_run[:, h0 * 64:(h0 + 4) * 64] \
                    .rearrange("p (h d) -> p h d", d=64)
                lb3 = ps[7][:, 192 + h0:192 + h0 + 4].unsqueeze(2) \
                    .broadcast_to((128, 4, 64))
                vec.inc(vector.tensor_tensor(hr3, hr3, lb3, ALU.mult), 1)
                vector.wait_ge(vec.sem, vec.v)
                vec.inc(vector.tensor_add(h_run[:, h0 * 64:(h0 + 4) * 64],
                                          h_run[:, h0 * 64:(h0 + 4) * 64],
                                          ps[2 + ab][:, 0:256]), 1)
                vector.wait_ge(vec.sem, vec.v)
                kv[f"c_hupd_{q}_{hg}"] = vec.v
                kv[f"c_yev_{q}_{hg}"] = vec.v
            block.vector(f)

        def f(sync, q=q, cb=cb):
            sync.wait_ge(vec.sem, kv[f"c_yev_{q}_15"])
            dm(sync, out=y_scr[q], in_=ytok[:, cb, :])
            kv[f"c_yst_{q}"] = seal()
        block.sync(f)

    def f(vector):
        vector.wait_ge(act.sem, kv["c_E_7"])
        vec.inc(vector.memset(lamcum[:, 0, :], 1.0), 1)
        vector.wait_ge(vec.sem, vec.v)
        for q in range(1, NQ):
            ins = vector.tensor_tensor(lamcum[:, q, :], lamcum[:, q - 1, :],
                                       lamr[:, q - 1, :], ALU.mult)
            vec.inc(ins, 1)
            vector.wait_ge(vec.sem, vec.v)
        ins = vector.tensor_tensor(lamtot[:], lamcum[:, NQ - 1, :],
                                   lamr[:, NQ - 1, :], ALU.mult)
        vec.inc(ins, 1)
        vector.wait_ge(vec.sem, vec.v)
        kv["c_lams"] = vec.v
    block.vector(f)

    def f(sync):
        if DEBUG:
            sync.wait_ge(act.sem, kv["c_E_7"])
            dm(sync, out=dbg_lamr, in_=lamr[:])
            dm(sync, out=dbg_eall, in_=e_all[:])
        sync.wait_ge(vec.sem, kv["c_hupd_7_15"])
        dm(sync, out=ag_s_in.ap()[:, 0:4096], in_=h_run[:])
        sync.wait_ge(vec.sem, kv["c_lams"])
        dm(sync, out=ag_s_in.ap()[0:1, 4096:4160], in_=lamtot[:])
        kv["c_agin"] = seal()
    block.sync(f)

    def f(gpsimd):
        wt(gpsimd, kv["c_agin"])
        ins = gpsimd.collective_compute(
            "AllGather", ALU.bypass, replica_groups=[list(range(8))],
            ins=[ag_s_in.ap().opt()], outs=[ag_s_out.ap().opt()])
        cc.inc(ins, 1)
        kv["c_ag"] = cc.v
        if DEBUG:
            dmg.inc(gpsimd.dma_start(out=dbg_ags, in_=ag_s_out.ap()), 16)
    block.gpsimd(f)

    barrier()
    for g in reversed(gC):
        g.__exit__(None, None, None)

    if PHASE_LIMIT < 3:
        block_cm.__exit__(None, None, None)
        return nc

    gC2 = []

    def scC2(name, shape):
        g = nc.sbuf_tensor(name, shape, F32)
        gC2.append(g)
        return g.__enter__()

    h_in = scC2("h_in", [128, D])
    hloc_j = scC2("hloc_j", [128, D])
    hq_sb = scC2("hq_sb", [128, D])
    ytok2 = scC2("ytok2", [128, D])
    ztok = scC2("ztok", [128, D])
    zf = scC2("zf", [128, NKD, 128])
    gs2 = scC2("gs2", [128, 4])
    sqg = scC2("sqg", [128, D])
    y2ev = scC2("y2ev", [128, NKD, 128])

    def f(sync):
        sync.wait_ge(cc.sem, kv["c_ag"])
        dm(sync, 
            out=lam_all[:],
            in_=ag_s_out.ap()[0:1024:128, 4096:4160].unsqueeze(0))
        kv["c2_lam"] = seal()
    block.sync(f)

    def f(vector):
        wt(vector, kv["c2_lam"])
        for j in range(8):
            vec.inc(vector.memset(wrow[:, j, :], 1.0), 1)
            vector.wait_ge(vec.sem, vec.v)
            for k in range(8):
                ins = vector.tensor_scalar(s1row[:], lam_all[:, k, :],
                                           1.0, ppw[0:1, j * 8 + k:j * 8 + k + 1],
                                           ALU.subtract, ALU.mult)
                vec.inc(ins, 1)
                vector.wait_ge(vec.sem, vec.v)
                vec.inc(vector.tensor_scalar_add(s1row[:], s1row[:], 1.0), 1)
                vector.wait_ge(vec.sem, vec.v)
                vec.inc(vector.tensor_tensor(wrow[:, j, :], wrow[:, j, :],
                                             s1row[:], ALU.mult), 1)
                vector.wait_ge(vec.sem, vec.v)
            vec.inc(vector.tensor_scalar_mul(wrow[:, j, :], wrow[:, j, :],
                                             pm8[0:1, j:j + 1]), 1)
            vector.wait_ge(vec.sem, vec.v)
        vec.inc(vector.memset(h_in[:], 0.0), 1)
        vector.wait_ge(vec.sem, vec.v)
        kv["c2_w"] = vec.v
    block.vector(f)

    for j in range(8):
        def f(sync, j=j):
            sync.wait_ge(vec.sem, kv["c2_w"] if j == 0 else kv[f"c2_acc_{j - 1}"])
            dm(sync, out=hloc_j[:],
                                 in_=ag_s_out.ap()[j * 128:(j + 1) * 128, 0:4096])
            kv[f"c2_hl_{j}"] = seal()
        block.sync(f)

        def f(tensor, j=j):
            tensor.wait_ge(vec.sem, kv["c2_w"] if j == 0 else kv[f"c2_acc_{j - 1}"])
            ins = nc.tensor.matmul(ps[6][:, 0:64], on1[:], wrow[:, j, :],
                                   start=True, stop=True)
            mmc.inc(ins, 1)
            kv[f"c2_wb_{j}"] = mmc.v
        block.tensor(f)

        def f(vector, j=j):
            wt(vector, kv[f"c2_hl_{j}"])
            vector.wait_ge(mmc.sem, kv[f"c2_wb_{j}"])
            hl3 = hloc_j[:].rearrange("p (h d) -> p h d", d=64)
            wb3 = ps[6][:, 0:64].unsqueeze(2).broadcast_to((128, 64, 64))
            vec.inc(vector.tensor_tensor(hl3, hl3, wb3, ALU.mult), 1)
            vector.wait_ge(vec.sem, vec.v)
            vec.inc(vector.tensor_add(h_in[:], h_in[:], hloc_j[:]), 1)
            vector.wait_ge(vec.sem, vec.v)
            kv[f"c2_acc_{j}"] = vec.v
        block.vector(f)

    for q in range(NQ):
        def f(sync, q=q):
            sync.wait_ge(vec.sem, kv["c2_acc_7"] if q == 0 else kv[f"c2_rms_{q - 1}"])
            if q >= 1:
                sync.wait_ge(mmc.sem, kv[f"c2_ytr_{q - 1}_3"])
            dm(sync, out=ytok2[:], in_=y_scr[q])
            for g in range(4):
                dm(sync, 
                    out=zf[:, g * 8:(g + 1) * 8, :],
                    in_=zx_scr[g * 8:(g + 1) * 8, :, q * 128:(q + 1) * 128]
                    .transpose([1, 0, 2]))
            kv[f"c2_ld_{q}"] = seal()
        block.sync(f)

        def f(tensor, q=q):
            tensor.wait_ge(vec.sem, kv["c2_acc_7"] if q == 0 else kv[f"c2_hq_{q - 1}"])
            ins = nc.tensor.matmul(ps[6][:, 0:64], on1[:], lamcum[:, q, :],
                                   start=True, stop=True)
            mmc.inc(ins, 1)
            kv[f"c2_lcb_{q}"] = mmc.v
        block.tensor(f)

        def f(vector, q=q):
            vector.wait_ge(mmc.sem, kv[f"c2_lcb_{q}"])
            hi3 = h_in[:].rearrange("p (h d) -> p h d", d=64)
            hq3 = hq_sb[:].rearrange("p (h d) -> p h d", d=64)
            lc3 = ps[6][:, 0:64].unsqueeze(2).broadcast_to((128, 64, 64))
            vec.inc(vector.tensor_tensor(hq3, hi3, lc3, ALU.mult), 1)
            vector.wait_ge(vec.sem, vec.v)
            kv[f"c2_hq_{q}"] = vec.v
        block.vector(f)

        for hg in range(16):
            h0 = hg * 4
            ab = hg % 2

            def f(tensor, q=q, hg=hg, h0=h0, ab=ab):
                if hg == 0:
                    tensor.wait_ge(vec.sem, kv[f"c2_hq_{q}"])
                if hg >= 2:
                    tensor.wait_ge(vec.sem, kv[f"c2_cev_{q}_{hg - 2}"])
                for i in range(4):
                    h = h0 + i
                    ins = nc.tensor.matmul(ps[0 + ab][:, i * 64:(i + 1) * 64],
                                           cm_all[:, q, :],
                                           hq_sb[:, h * 64:(h + 1) * 64],
                                           start=True, stop=True)
                mmc.inc(ins, 1)
                kv[f"c2_cmm_{q}_{hg}"] = mmc.v
            block.tensor(f)

            def f(vector, q=q, hg=hg, h0=h0, ab=ab):
                if hg == 0:
                    wt(vector, kv[f"c2_ld_{q}"])
                vector.wait_ge(mmc.sem, kv[f"c2_cmm_{q}_{hg}"])
                pc3 = ps[0 + ab][:, 0:256].rearrange("p (h d) -> p h d", d=64)
                e3 = e_all[:, q, h0:h0 + 4].unsqueeze(2).broadcast_to((128, 4, 64))
                t3 = sqg[:, h0 * 64:(h0 + 4) * 64].rearrange("p (h d) -> p h d", d=64)
                vec.inc(vector.tensor_tensor(t3, pc3, e3, ALU.mult), 1)
                vector.wait_ge(vec.sem, vec.v)
                vec.inc(vector.tensor_add(ytok2[:, h0 * 64:(h0 + 4) * 64],
                                          ytok2[:, h0 * 64:(h0 + 4) * 64],
                                          sqg[:, h0 * 64:(h0 + 4) * 64]), 1)
                vector.wait_ge(vec.sem, vec.v)
                kv[f"c2_cev_{q}_{hg}"] = vec.v
            block.vector(f)

        for rnd in range(4):
            def f(tensor, q=q, rnd=rnd):
                if rnd == 0:
                    wt(tensor, kv[f"c2_ld_{q}"])
                    tensor.wait_ge(vec.sem, kv[f"c2_cev_{q}_15"])
                else:
                    tensor.wait_ge(vec.sem, kv[f"c2_ztr_ev_{q}_{rnd - 1}"])
                for i in range(8):
                    k = rnd * 8 + i
                    bank = i // 2
                    slot = i % 2
                    ins = nc.tensor.transpose(
                        ps[bank][:, slot * 128:(slot + 1) * 128],
                        zf[:, k, :], id_sb[:])
                mmc.inc(ins, 1)
                kv[f"c2_ztr_{q}_{rnd}"] = mmc.v
            block.tensor(f)

            def f(vector, q=q, rnd=rnd):
                vector.wait_ge(mmc.sem, kv[f"c2_ztr_{q}_{rnd}"])
                for i in range(8):
                    k = rnd * 8 + i
                    bank = i // 2
                    slot = i % 2
                    ins = vector.tensor_copy(ztok[:, k * 128:(k + 1) * 128],
                                             ps[bank][:, slot * 128:(slot + 1) * 128])
                vec.inc(ins, 1)
                vector.wait_ge(vec.sem, vec.v)
                kv[f"c2_ztr_ev_{q}_{rnd}"] = vec.v
            block.vector(f)

        def f(scalar, q=q):
            scalar.wait_ge(vec.sem, kv[f"c2_ztr_ev_{q}_3"])
            act.inc(scalar.activation(ztok[:], ztok[:], AF.Silu), 1)
            kv[f"c2_silu_{q}"] = act.v
        block.scalar(f)

        def f(vector, q=q):
            vector.wait_ge(act.sem, kv[f"c2_silu_{q}"])
            vec.inc(vector.tensor_mul(ytok2[:], ytok2[:], ztok[:]), 1)
            vector.wait_ge(vec.sem, vec.v)
            vec.inc(vector.tensor_mul(sqg[:], ytok2[:], ytok2[:]), 1)
            vector.wait_ge(vec.sem, vec.v)
            vec.inc(vector.tensor_reduce(gs2[:, 0:1], sqg[:], AX.X, ALU.add), 1)
            vector.wait_ge(vec.sem, vec.v)
            vec.inc(vector.tensor_scalar(gs2[:, 0:1], gs2[:, 0:1], 1.0 / D, EPS,
                                         ALU.mult, ALU.add), 1)
            vector.wait_ge(vec.sem, vec.v)
            kv[f"c2_rms1_{q}"] = vec.v
        block.vector(f)

        def f(scalar, q=q):
            scalar.wait_ge(vec.sem, kv[f"c2_rms1_{q}"])
            act.inc(scalar.activation(gs2[:, 0:1], gs2[:, 0:1], AF.Sqrt), 1)
            kv[f"c2_rms2_{q}"] = act.v
        block.scalar(f)

        def f(vector, q=q):
            vector.wait_ge(act.sem, kv[f"c2_rms2_{q}"])
            vec.inc(vector.reciprocal(gs2[:, 1:2], gs2[:, 0:1]), 1)
            vector.wait_ge(vec.sem, vec.v)
            vec.inc(vector.tensor_scalar_mul(ytok2[:], ytok2[:], gs2[:, 1:2]), 1)
            vector.wait_ge(vec.sem, vec.v)
            kv[f"c2_rms_{q}"] = vec.v
        block.vector(f)

        for rnd in range(4):
            def f(tensor, q=q, rnd=rnd):
                if rnd == 0:
                    tensor.wait_ge(vec.sem, kv[f"c2_rms_{q}"])
                else:
                    tensor.wait_ge(vec.sem, kv[f"c2_ytr_ev_{q}_{rnd - 1}"])
                for i in range(8):
                    k = rnd * 8 + i
                    bank = i // 2
                    slot = i % 2
                    ins = nc.tensor.transpose(
                        ps[bank][:, slot * 128:(slot + 1) * 128],
                        ytok2[:, k * 128:(k + 1) * 128], id_sb[:])
                mmc.inc(ins, 1)
                kv[f"c2_ytr_{q}_{rnd}"] = mmc.v
            block.tensor(f)

            def f(vector, q=q, rnd=rnd):
                vector.wait_ge(mmc.sem, kv[f"c2_ytr_{q}_{rnd}"])
                if q >= 1 and rnd == 0:
                    wt(vector, kv[f"c2_yst_{q - 1}"])
                for i in range(8):
                    k = rnd * 8 + i
                    bank = i // 2
                    slot = i % 2
                    ins = vector.tensor_scalar_mul(
                        y2ev[:, k, :], ps[bank][:, slot * 128:(slot + 1) * 128],
                        normw[:, k:k + 1])
                vec.inc(ins, 1)
                vector.wait_ge(vec.sem, vec.v)
                kv[f"c2_ytr_ev_{q}_{rnd}"] = vec.v
            block.vector(f)

        def f(sync, q=q):
            sync.wait_ge(vec.sem, kv[f"c2_ytr_ev_{q}_3"])
            for g in range(4):
                dm(sync, 
                    out=y2_scr[g * 8:(g + 1) * 8, :, q * 128:(q + 1) * 128]
                    .transpose([1, 0, 2]),
                    in_=y2ev[:, g * 8:(g + 1) * 8, :])
            kv[f"c2_yst_{q}"] = seal()
        block.sync(f)

    barrier()
    for g in reversed(gC2):
        g.__exit__(None, None, None)

    if PHASE_LIMIT < 4:
        block_cm.__exit__(None, None, None)
        return nc

    gE = []

    def scE(name, shape):
        g = nc.sbuf_tensor(name, shape, F32)
        gE.append(g)
        return g.__enter__()

    y2T = scE("y2T", [128, NKD, TOK])
    wbufE = scE("wbufE", [128, 2, NKD, 128])
    ffnt = scE("ffnt", [128, 2, TOK])
    sumf = scE("sumf", [128, 2, TOK])
    sumt = scE("sumt", [128, 2, 128])

    def f(sync):
        for k in range(NKD):
            dm(sync, out=y2T[:, k, :], in_=y2_scr[k])
        kv["e_y2"] = seal()
    block.sync(f)

    for m in range(NKD):
        b = m % 2

        def f(sync, m=m, b=b):
            if m >= 2:
                sync.wait_ge(mmc.sem, kv[f"e_mm_{m - 2}"])
                sync.wait_ge(vec.sem, kv[f"e_sum_{m - 2}"])
            dm(sync, out=wbufE[:, b], in_=w_out_h[m])
            dm(sync, out=ffnt[:, b], in_=ffn_scr[m])
            kv[f"e_w_{m}"] = seal()
        block.sync(f)

        def f(tensor, m=m, b=b):
            if m == 0:
                wt(tensor, kv["e_y2"])
            wt(tensor, kv[f"e_w_{m}"])
            if m >= 2:
                tensor.wait_ge(vec.sem, kv[f"e_sum_{m - 2}"])
            for half in range(2):
                bank = b * 2 + half
                for k in range(NKD):
                    ins = nc.tensor.matmul(
                        ps[bank][:, 0:512], wbufE[:, b, k, :],
                        y2T[:, k, half * 512:(half + 1) * 512],
                        start=(k == 0), stop=(k == NKD - 1))
                mmc.inc(ins, 1)
            kv[f"e_mm_{m}"] = mmc.v
        block.tensor(f)

        def f(vector, m=m, b=b):
            vector.wait_ge(mmc.sem, kv[f"e_mm_{m}"])
            for half in range(2):
                bank = b * 2 + half
                ins = vector.tensor_tensor(
                    sumf[:, b, half * 512:(half + 1) * 512], ps[bank][:, 0:512],
                    ffnt[:, b, half * 512:(half + 1) * 512], ALU.add)
                vec.inc(ins, 1)
                vector.wait_ge(vec.sem, vec.v)
            kv[f"e_sum_{m}"] = vec.v
        block.vector(f)

        for t in range(NT):
            tb = t % 2

            def f(tensor, m=m, b=b, t=t, tb=tb):
                tensor.wait_ge(vec.sem, kv[f"e_sum_{m}"])
                if (m, t) > (0, 1):
                    pm, pt = (m, t - 2) if t >= 2 else (m - 1, t + 6)
                    tensor.wait_ge(vec.sem, kv[f"e_tev_{pm}_{pt}"])
                ins = nc.tensor.transpose(ps[4 + tb][:, 0:128],
                                          sumf[:, b, t * 128:(t + 1) * 128],
                                          id_sb[:])
                mmc.inc(ins, 1)
                kv[f"e_tr_{m}_{t}"] = mmc.v
            block.tensor(f)

            def f(vector, m=m, t=t, tb=tb):
                vector.wait_ge(mmc.sem, kv[f"e_tr_{m}_{t}"])
                if (m, t) > (0, 1):
                    pm, pt = (m, t - 2) if t >= 2 else (m - 1, t + 6)
                    wt(vector, kv[f"e_tst_{pm}_{pt}"])
                vec.inc(vector.tensor_copy(sumt[:, tb, :], ps[4 + tb][:, 0:128]), 1)
                vector.wait_ge(vec.sem, vec.v)
                kv[f"e_tev_{m}_{t}"] = vec.v
            block.vector(f)

            def f(sync, m=m, t=t, tb=tb):
                sync.wait_ge(vec.sem, kv[f"e_tev_{m}_{t}"])
                dm(sync, out=sum_scr[t, :, m * 128:(m + 1) * 128],
                                     in_=sumt[:, tb, :])
                kv[f"e_tst_{m}_{t}"] = seal()
            block.sync(f)

    barrier()
    for g in reversed(gE):
        g.__exit__(None, None, None)

    if PHASE_LIMIT < 5:
        block_cm.__exit__(None, None, None)
        return nc

    gF = []

    def scF(name, shape):
        g = nc.sbuf_tensor(name, shape, F32)
        gF.append(g)
        return g.__enter__()

    sel32 = scF("sel32", [128, D])
    wqs = scF("wqs", [128, NKD, 128])
    wk1 = scF("wk1", [128, NKD, 128])
    wv1 = scF("wv1", [128, NKD, 128])
    wa1 = scF("wa1", [128, NKD, 128])
    wrt = scF("wrt", [128, NKD, 8])
    wb1 = scF("wb1", [128, D])
    x1sb = scF("x1sb", [128, D])
    sumtok = scF("sumtok", [128, D])
    xn1T = scF("xn1T", [128, NKD, 128])
    qs_sb = scF("qs_sb", [128, 128])
    k_sb = scF("k_sb", [128, 128])
    v_sb = scF("v_sb", [128, 128])
    h1_sb = scF("h1_sb", [128, 128])
    rt_sb = scF("rt_sb", [8, 128])
    rtT = scF("rtT", [128, 8])
    rwf = scF("rwf", [8, 128])
    wh_sb = scF("wh_sb", [128, 128])
    kT = scF("kT", [128, 128])
    vT = scF("vT", [128, 128])
    krow = scF("krow", [1, 2, 1024])
    vrow = scF("vrow", [1, 2, 1024])
    psb = scF("psb", [128, 2, 512])
    prodb = scF("prodb", [128, 512])
    numt = scF("numt", [128, 128])
    dent = scF("dent", [128, 128])
    o_sb = scF("o_sb", [128, 128])
    fstat = scF("fstat", [128, 8])
    cstat2 = scF("cstat2", [128, 4])
    crow = scF("crow", [1, 4])
    cm2 = scF("cm2", [2, 1])
    cbias = scF("cbias", [128, 1])
    outev = scF("outev", [128, 2, 512])

    def f(sync):
        dm(sync, out=sel32[:], in_=sel32_h.ap())
        dm(sync, 
            out=wqs[:], in_=ag_wq_out.ap().rearrange("(k p) d -> p k d", p=128))
        dm(sync, out=wk1[:], in_=w_k_h.transpose([1, 0, 2]))
        dm(sync, out=wv1[:], in_=w_v_h.transpose([1, 0, 2]))
        dm(sync, out=wa1[:], in_=w_a1_h.transpose([1, 0, 2]))
        dm(sync, out=wrt[:], in_=w_route_h.transpose([1, 0, 2]))
        dm(sync, out=wb1[:], in_=w_b1_h)
        kv["f_w"] = seal()
    block.sync(f)

    out_toks = []

    for tau in range(NT):
        def f(sync, tau=tau):
            if tau >= 1:
                sync.wait_ge(vec.sem, kv[f"f_oev_{tau - 1}_7"])
            dm(sync, out=x1sb[:],
                                   in_=x_h[tau * 128:(tau + 1) * 128, :])
            dm(sync, out=sumtok[:], in_=sum_scr[tau])
            kv[f"f_ld_{tau}"] = seal()
        block.sync(f)

        def f(vector, tau=tau):
            wt(vector, kv[f"f_ld_{tau}"])
            vec.inc(vector.tensor_add(x1sb[:], x1sb[:], sumtok[:]), 1)
            vector.wait_ge(vec.sem, vec.v)
            vec.inc(vector.tensor_reduce(fstat[:, 0:1], x1sb[:], AX.X, ALU.add), 1)
            vector.wait_ge(vec.sem, vec.v)
            vec.inc(vector.tensor_mul(sumtok[:], x1sb[:], x1sb[:]), 1)
            vector.wait_ge(vec.sem, vec.v)
            vec.inc(vector.tensor_reduce(fstat[:, 1:2], sumtok[:], AX.X, ALU.add), 1)
            vector.wait_ge(vec.sem, vec.v)
            vec.inc(vector.tensor_scalar_mul(fstat[:, 0:1], fstat[:, 0:1], 1.0 / D), 1)
            vector.wait_ge(vec.sem, vec.v)
            vec.inc(vector.tensor_scalar_mul(fstat[:, 1:2], fstat[:, 1:2], 1.0 / D), 1)
            vector.wait_ge(vec.sem, vec.v)
            vec.inc(vector.tensor_mul(fstat[:, 2:3], fstat[:, 0:1], fstat[:, 0:1]), 1)
            vector.wait_ge(vec.sem, vec.v)
            vec.inc(vector.tensor_tensor(fstat[:, 1:2], fstat[:, 1:2],
                                         fstat[:, 2:3], ALU.subtract), 1)
            vector.wait_ge(vec.sem, vec.v)
            vec.inc(vector.tensor_scalar_add(fstat[:, 1:2], fstat[:, 1:2], EPS), 1)
            vector.wait_ge(vec.sem, vec.v)
            kv[f"f_ln1_{tau}"] = vec.v
        block.vector(f)

        def f(scalar, tau=tau):
            scalar.wait_ge(vec.sem, kv[f"f_ln1_{tau}"])
            act.inc(scalar.activation(fstat[:, 1:2], fstat[:, 1:2], AF.Sqrt), 1)
            kv[f"f_ln2_{tau}"] = act.v
        block.scalar(f)

        def f(vector, tau=tau):
            vector.wait_ge(act.sem, kv[f"f_ln2_{tau}"])
            vec.inc(vector.reciprocal(fstat[:, 1:2], fstat[:, 1:2]), 1)
            vector.wait_ge(vec.sem, vec.v)
            vec.inc(vector.tensor_scalar(sumtok[:], x1sb[:], fstat[:, 0:1],
                                         fstat[:, 1:2], ALU.subtract, ALU.mult), 1)
            vector.wait_ge(vec.sem, vec.v)
            kv[f"f_xn1_{tau}"] = vec.v
        block.vector(f)

        for rnd in range(4):
            def f(tensor, tau=tau, rnd=rnd):
                if rnd == 0:
                    tensor.wait_ge(vec.sem, kv[f"f_xn1_{tau}"])
                    wt(tensor, kv["f_w"])
                else:
                    tensor.wait_ge(vec.sem, kv[f"f_trev_{tau}_{rnd - 1}"])
                for i in range(8):
                    k = rnd * 8 + i
                    bank = i // 2
                    slot = i % 2
                    ins = nc.tensor.transpose(
                        ps[bank][:, slot * 128:(slot + 1) * 128],
                        sumtok[:, k * 128:(k + 1) * 128], id_sb[:])
                mmc.inc(ins, 1)
                kv[f"f_tr_{tau}_{rnd}"] = mmc.v
            block.tensor(f)

            def f(vector, tau=tau, rnd=rnd):
                vector.wait_ge(mmc.sem, kv[f"f_tr_{tau}_{rnd}"])
                for i in range(8):
                    k = rnd * 8 + i
                    bank = i // 2
                    slot = i % 2
                    ins = vector.tensor_copy(xn1T[:, k, :],
                                             ps[bank][:, slot * 128:(slot + 1) * 128])
                vec.inc(ins, 1)
                vector.wait_ge(vec.sem, vec.v)
                kv[f"f_trev_{tau}_{rnd}"] = vec.v
            block.vector(f)

        def f(tensor, tau=tau):
            tensor.wait_ge(vec.sem, kv[f"f_trev_{tau}_3"])
            for bank, w in ((0, wqs), (1, wk1), (2, wv1), (3, wa1)):
                for k in range(NKD):
                    ins = nc.tensor.matmul(ps[bank][:, 0:128], w[:, k, :],
                                           xn1T[:, k, :],
                                           start=(k == 0), stop=(k == NKD - 1))
                mmc.inc(ins, 1)
            for k in range(NKD):
                ins = nc.tensor.matmul(ps[6][0:8, 0:128], wrt[:, k, :],
                                       xn1T[:, k, :],
                                       start=(k == 0), stop=(k == NKD - 1))
            mmc.inc(ins, 1)
            kv[f"f_gmm_{tau}"] = mmc.v
        block.tensor(f)

        def f(vector, tau=tau):
            vector.wait_ge(mmc.sem, kv[f"f_gmm_{tau}"])
            vec.inc(vector.tensor_scalar(qs_sb[:], ps[0][:, 0:128], bqs[:, 0:1],
                                         SCL, ALU.add, ALU.mult), 1)
            vector.wait_ge(vec.sem, vec.v)
            vec.inc(vector.tensor_scalar(k_sb[:], ps[1][:, 0:128], bkc[:, 0:1],
                                         None, ALU.add), 1)
            vector.wait_ge(vec.sem, vec.v)
            vec.inc(vector.tensor_scalar(v_sb[:], ps[2][:, 0:128], bvc[:, 0:1],
                                         None, ALU.add), 1)
            vector.wait_ge(vec.sem, vec.v)
            vec.inc(vector.tensor_copy(h1_sb[:], ps[3][:, 0:128]), 1)
            vector.wait_ge(vec.sem, vec.v)
            vec.inc(vector.tensor_copy(rt_sb[:], ps[6][0:8, 0:128]), 1)
            vector.wait_ge(vec.sem, vec.v)
            kv[f"f_gev_{tau}"] = vec.v
        block.vector(f)

        def f(tensor, tau=tau):
            tensor.wait_ge(vec.sem, kv[f"f_gev_{tau}"])
            mmc.inc(nc.tensor.transpose(ps[4][0:128, 0:8], rt_sb[:], id_sb[0:8, 0:8]), 1)
            kv[f"f_rtr_{tau}"] = mmc.v
        block.tensor(f)

        def f(vector, tau=tau):
            vector.wait_ge(mmc.sem, kv[f"f_rtr_{tau}"])
            vec.inc(vector.tensor_copy(rtT[:], ps[4][0:128, 0:8]), 1)
            vector.wait_ge(vec.sem, vec.v)
            vec.inc(vector.reduce_max(fstat[:, 3:4], rtT[:], AX.X), 1)
            vector.wait_ge(vec.sem, vec.v)
            vec.inc(vector.tensor_tensor(rtT[:], rtT[:],
                                         fstat[:, 3:4].broadcast_to((128, 8)),
                                         ALU.is_ge), 1)
            vector.wait_ge(vec.sem, vec.v)
            vec.inc(vector.tensor_scalar_mul(rtT[:], rtT[:], fstat[:, 3:4]), 1)
            vector.wait_ge(vec.sem, vec.v)
            kv[f"f_rw_{tau}"] = vec.v
        block.vector(f)

        def f(tensor, tau=tau):
            tensor.wait_ge(vec.sem, kv[f"f_rw_{tau}"])
            mmc.inc(nc.tensor.transpose(ps[4][0:8, 0:128], rtT[:], id_sb[:]), 1)
            kv[f"f_rwtr_{tau}"] = mmc.v
        block.tensor(f)

        def f(vector, tau=tau):
            vector.wait_ge(mmc.sem, kv[f"f_rwtr_{tau}"])
            vec.inc(vector.tensor_copy(rwf[:], ps[4][0:8, 0:128]), 1)
            vector.wait_ge(vec.sem, vec.v)
            kv[f"f_rwf_{tau}"] = vec.v
        block.vector(f)

        def f(tensor, tau=tau):
            tensor.wait_ge(vec.sem, kv[f"f_rwf_{tau}"])
            ins = nc.tensor.matmul(ps[4][:, 0:128], esel8[:], rwf[:],
                                   start=True, stop=True)
            mmc.inc(ins, 1)
            kv[f"f_rwexp_{tau}"] = mmc.v
        block.tensor(f)

        def f(vector, tau=tau):
            vector.wait_ge(mmc.sem, kv[f"f_rwexp_{tau}"])
            vec.inc(vector.tensor_tensor(wh_sb[:], h1_sb[:], ps[4][:, 0:128],
                                         ALU.mult), 1)
            vector.wait_ge(vec.sem, vec.v)
            kv[f"f_wh_{tau}"] = vec.v
        block.vector(f)

        def f(tensor, tau=tau):
            tensor.wait_ge(vec.sem, kv[f"f_gev_{tau}"])
            mmc.inc(nc.tensor.transpose(ps[5][:, 0:128], k_sb[:], id_sb[:]), 1)
            kv[f"f_ktr_{tau}"] = mmc.v
        block.tensor(f)

        def f(vector, tau=tau):
            vector.wait_ge(mmc.sem, kv[f"f_ktr_{tau}"])
            vec.inc(vector.tensor_copy(kT[:], ps[5][:, 0:128]), 1)
            vector.wait_ge(vec.sem, vec.v)
            kv[f"f_ktev_{tau}"] = vec.v
        block.vector(f)

        def f(tensor, tau=tau):
            tensor.wait_ge(vec.sem, kv[f"f_ktev_{tau}"])
            mmc.inc(nc.tensor.transpose(ps[5][:, 0:128], v_sb[:], id_sb[:]), 1)
            kv[f"f_vtr_{tau}"] = mmc.v
        block.tensor(f)

        def f(vector, tau=tau):
            vector.wait_ge(mmc.sem, kv[f"f_vtr_{tau}"])
            vec.inc(vector.tensor_copy(vT[:], ps[5][:, 0:128]), 1)
            vector.wait_ge(vec.sem, vec.v)
            vec.inc(vector.reduce_max(cstat2[:, 0:1], qs_sb[:], AX.X,
                                      apply_absolute_value=True), 1)
            vector.wait_ge(vec.sem, vec.v)
            vec.inc(vector.reduce_max(cstat2[:, 1:2], k_sb[:], AX.X,
                                      apply_absolute_value=True), 1)
            vector.wait_ge(vec.sem, vec.v)
            kv[f"f_vtev_{tau}"] = vec.v
        block.vector(f)

        def f(tensor, tau=tau):
            tensor.wait_ge(vec.sem, kv[f"f_vtev_{tau}"])
            mmc.inc(nc.tensor.transpose(ps[5][0:4, 0:128], cstat2[:], id_sb[:]), 1)
            kv[f"f_ctr_{tau}"] = mmc.v
        block.tensor(f)

        def f(vector, tau=tau):
            vector.wait_ge(mmc.sem, kv[f"f_ctr_{tau}"])
            vec.inc(vector.reduce_max(cm2[:], ps[5][0:2, 0:128], AX.X), 1)
            vector.wait_ge(vec.sem, vec.v)
            kv[f"f_c1_{tau}"] = vec.v
        block.vector(f)

        def f(tensor, tau=tau):
            tensor.wait_ge(vec.sem, kv[f"f_c1_{tau}"])
            mmc.inc(nc.tensor.transpose(ps[5][0:1, 128:130], cm2[:],
                                        id_sb[0:2, 0:2]), 1)
            kv[f"f_c2_{tau}"] = mmc.v
        block.tensor(f)

        def f(vector, tau=tau):
            vector.wait_ge(mmc.sem, kv[f"f_c2_{tau}"])
            vec.inc(vector.tensor_copy(crow[:, 0:2], ps[5][0:1, 128:130]), 1)
            vector.wait_ge(vec.sem, vec.v)
            vec.inc(vector.tensor_mul(crow[:, 0:2], crow[:, 0:2], crow[:, 0:2]), 1)
            vector.wait_ge(vec.sem, vec.v)
            vec.inc(vector.tensor_reduce(crow[:, 2:3], crow[:, 0:2], AX.X,
                                         ALU.add), 1)
            vector.wait_ge(vec.sem, vec.v)
            vec.inc(vector.tensor_scalar_mul(crow[:, 3:4], crow[:, 2:3], -0.5), 1)
            vector.wait_ge(vec.sem, vec.v)
            kv[f"f_c_{tau}"] = vec.v
        block.vector(f)

        def f(tensor, tau=tau):
            tensor.wait_ge(vec.sem, kv[f"f_c_{tau}"])
            ins = nc.tensor.matmul(ps[6][:, 128:129], on1[:], crow[:, 3:4],
                                   start=True, stop=True)
            mmc.inc(ins, 1)
            kv[f"f_cbmm_{tau}"] = mmc.v
        block.tensor(f)

        def f(vector, tau=tau):
            vector.wait_ge(mmc.sem, kv[f"f_cbmm_{tau}"])
            vec.inc(vector.tensor_copy(cbias[:], ps[6][:, 128:129]), 1)
            vector.wait_ge(vec.sem, vec.v)
            kv[f"f_cb_{tau}"] = vec.v
        block.vector(f)

        for g in range(16):
            rb = g % 2

            def f(sync, tau=tau, g=g, rb=rb):
                if g == 0:
                    sync.wait_ge(vec.sem, kv[f"f_vtev_{tau}"])
                elif g == 1:
                    pass
                else:
                    sync.wait_ge(vec.sem, kv[f"f_nd4_{tau}_{(g - 2) * 2 + 1}"])
                dm(sync, out=krow[:, rb, :],
                                       in_=kT[g * 8:(g + 1) * 8, :])
                dm(sync, out=vrow[:, rb, :],
                                       in_=vT[g * 8:(g + 1) * 8, :])
                kv[f"f_row_{tau}_{g}"] = seal()
            block.sync(f)

            for sg in range(2):
                u = g * 2 + sg
                t0 = u * 4
                kb = u % 2

                def f(tensor, tau=tau, g=g, sg=sg, rb=rb, kb=kb, u=u):
                    wt(tensor, kv[f"f_row_{tau}_{g}"])
                    if u >= 2:
                        tensor.wait_ge(act.sem, kv[f"f_exp_{tau}_{u - 2}"])
                        tensor.wait_ge(vec.sem, kv[f"f_nd4_{tau}_{u - 2}"])
                    ins = nc.tensor.matmul(ps[0 + kb][:, 0:512], on1[:],
                                           krow[0:1, rb, sg * 512:(sg + 1) * 512],
                                           start=True, stop=True)
                    mmc.inc(ins, 1)
                    ins = nc.tensor.matmul(ps[2 + kb][:, 0:512], on1[:],
                                           vrow[0:1, rb, sg * 512:(sg + 1) * 512],
                                           start=True, stop=True)
                    mmc.inc(ins, 1)
                    kv[f"f_rep_{tau}_{u}"] = mmc.v
                block.tensor(f)

                def f(scalar, tau=tau, u=u, kb=kb, t0=t0):
                    scalar.wait_ge(mmc.sem, kv[f"f_rep_{tau}_{u}"])
                    if u == 0:
                        scalar.wait_ge(vec.sem, kv[f"f_cb_{tau}"])
                    if u >= 2:
                        scalar.wait_ge(vec.sem, kv[f"f_nd4_{tau}_{u - 2}"])
                    for i in range(4):
                        t = t0 + i
                        ins = scalar.activation(
                            psb[:, kb, i * 128:(i + 1) * 128],
                            ps[0 + kb][:, i * 128:(i + 1) * 128],
                            AF.Exp, scale=qs_sb[:, t:t + 1], bias=cbias[:, 0:1])
                        act.inc(ins, 1)
                    kv[f"f_exp_{tau}_{u}"] = act.v
                block.scalar(f)

                def f(vector, tau=tau, u=u, kb=kb, t0=t0):
                    vector.wait_ge(act.sem, kv[f"f_exp_{tau}_{u}"])
                    ins = vector.tensor_tensor(prodb[:], psb[:, kb, :],
                                               ps[2 + kb][:, 0:512], ALU.mult)
                    vec.inc(ins, 1)
                    vector.wait_ge(vec.sem, vec.v)
                    pr3 = prodb[:].rearrange("p (t e) -> p t e", e=128)
                    vec.inc(vector.tensor_reduce(numt[:, t0:t0 + 4], pr3,
                                                 AX.X, ALU.add), 1)
                    vector.wait_ge(vec.sem, vec.v)
                    vector.wait_ge(vec.sem, vec.v)
                    p3 = psb[:, kb, :].rearrange("p (t e) -> p t e", e=128)
                    vec.inc(vector.tensor_reduce(dent[:, t0:t0 + 4], p3,
                                                 AX.X, ALU.add), 1)
                    vector.wait_ge(vec.sem, vec.v)
                    vector.wait_ge(vec.sem, vec.v)
                    kv[f"f_nd4_{tau}_{u}"] = vec.v
                block.vector(f)

        def f(vector, tau=tau):
            vector.wait_ge(vec.sem, kv[f"f_nd4_{tau}_31"])
            vec.inc(vector.reciprocal(dent[:], dent[:]), 1)
            vector.wait_ge(vec.sem, vec.v)
            vec.inc(vector.tensor_mul(o_sb[:], numt[:], dent[:]), 1)
            vector.wait_ge(vec.sem, vec.v)
            kv[f"f_o_{tau}"] = vec.v
        block.vector(f)

        for fc in range(8):
            ob = fc % 2

            def f(tensor, tau=tau, fc=fc, ob=ob):
                if fc == 0:
                    tensor.wait_ge(vec.sem, kv[f"f_o_{tau}"])
                    tensor.wait_ge(vec.sem, kv[f"f_wh_{tau}"])
                if fc >= 2:
                    tensor.wait_ge(vec.sem, kv[f"f_oev_{tau}_{fc - 2}"])
                nc.tensor.matmul(ps[0 + ob][:, 0:512], wh_sb[:],
                                 wb1[:, fc * 512:(fc + 1) * 512],
                                 start=True, stop=False)
                ins = nc.tensor.matmul(ps[0 + ob][:, 0:512], o_sb[:],
                                       sel32[:, fc * 512:(fc + 1) * 512],
                                       start=False, stop=True)
                mmc.inc(ins, 1)
                kv[f"f_omm_{tau}_{fc}"] = mmc.v
            block.tensor(f)

            def f(vector, tau=tau, fc=fc, ob=ob):
                vector.wait_ge(mmc.sem, kv[f"f_omm_{tau}_{fc}"])
                if fc >= 2:
                    wt(vector, kv[f"f_ost_{tau}_{fc - 2}"])
                ins = vector.tensor_tensor(outev[:, ob, :],
                                           x1sb[:, fc * 512:(fc + 1) * 512],
                                           ps[0 + ob][:, 0:512], ALU.add)
                vec.inc(ins, 1)
                vector.wait_ge(vec.sem, vec.v)
                kv[f"f_oev_{tau}_{fc}"] = vec.v
            block.vector(f)

            def f(sync, tau=tau, fc=fc, ob=ob):
                sync.wait_ge(vec.sem, kv[f"f_oev_{tau}_{fc}"])
                dm(sync, 
                    out=out_h[tau * 128:(tau + 1) * 128, fc * 512:(fc + 1) * 512],
                    in_=outev[:, ob, :])
                kv[f"f_ost_{tau}_{fc}"] = seal()
                out_toks.extend(kv[f"f_ost_{tau}_{fc}"])
            block.sync(f)

    def f(sync):
        wt(sync, out_toks)
    block.sync(f)

    for g in reversed(gF):
        g.__exit__(None, None, None)

    block_cm.__exit__(None, None, None)
    return nc


def stage_inputs(inputs):
    f4 = np.float32
    C = np.ascontiguousarray

    x = np.asarray(inputs["L0_attn_hidden_states"], dtype=f4)

    w_in = np.zeros((4096, 8576), f4)
    w_in[:, :PROJ] = np.asarray(inputs["L0_attn_in_proj_w"], f4)
    w_in_st = C(w_in.reshape(NKD, 128, MIN, 128).transpose(2, 1, 0, 3))

    mup = np.zeros(8576, f4)
    mup[:PROJ] = np.asarray(inputs["L0_attn_mup_vector"], f4)
    mup_st = C(mup.reshape(MIN, 128).T)

    lnw_st = C(np.asarray(inputs["L0_attn_ln_w"], f4).reshape(NKD, 128).T)
    normw_st = C(np.asarray(inputs["L0_attn_norm_w"], f4).reshape(NKD, 128).T)
    convw_st = C(np.asarray(inputs["L0_attn_conv_w"], f4).reshape(NCV, 128, 4))
    dtb_st = C(np.asarray(inputs["L0_attn_dt_bias"], f4).reshape(64, 1))
    alog_st = C(np.asarray(inputs["L0_attn_A_log"], f4).reshape(64, 1))
    dvec_st = C(np.asarray(inputs["L0_attn_D"], f4).reshape(1, 64))

    w_out = np.asarray(inputs["L0_attn_out_proj_w"], f4)
    w_out_st = C(w_out.reshape(NKD, 128, NKD, 128).transpose(2, 1, 0, 3))

    w_base = np.asarray(inputs["L0_ffn_base_W"], f4)
    w_base_st = C(w_base.T.reshape(NKD, 128, NKD, 128).transpose(2, 1, 0, 3))

    w_gate_st = C(np.asarray(inputs["L0_ffn_gate_W"], f4).T.reshape(NKD, 128, 28))

    la = np.asarray(inputs["L0_ffn_lora_A"], f4).reshape(224, 4096)
    w_la = np.zeros((4096, 256), f4)
    w_la[:, :224] = la.T
    w_la_st = C(w_la.reshape(NKD, 128, 256))

    lb = np.asarray(inputs["L0_ffn_lora_B"], f4).transpose(0, 2, 1).reshape(224, 4096)
    w_lb = np.zeros((256, 4096), f4)
    w_lb[:224] = lb
    w_lb_st = C(w_lb.reshape(2, 128, NKD, 128).transpose(2, 0, 1, 3))

    wq = np.asarray(inputs["L1_attn_Wq"], f4)
    bq_st = C(np.asarray(inputs["L1_attn_bq"], f4))
    bk_st = C(np.asarray(inputs["L1_attn_bk"], f4))
    bv_st = C(np.asarray(inputs["L1_attn_bv"], f4))
    w_k_st = C(np.asarray(inputs["L1_attn_Wk"], f4)[:, :, 0].reshape(NKD, 128, 128))
    w_v_st = C(np.asarray(inputs["L1_attn_Wv"], f4)[:, :, 0].reshape(NKD, 128, 128))
    w_route_st = C(np.asarray(inputs["L1_ffn_route_w"], f4).T.reshape(NKD, 128, 8))
    a1 = np.asarray(inputs["L1_ffn_A"], f4).reshape(128, 4096)
    w_a1_st = C(a1.T.reshape(NKD, 128, 128))
    b1 = np.asarray(inputs["L1_ffn_Bw"], f4).transpose(0, 2, 1).reshape(128, 4096)
    w_b1_st = C(b1)

    shared = dict(
        w_in=w_in_st, mupv=mup_st, lnw=lnw_st, normw=normw_st, convw=convw_st,
        dtb=dtb_st, alog=alog_st, dvecr=dvec_st, w_out=w_out_st, w_base=w_base_st,
        w_gate=w_gate_st, w_la=w_la_st, w_lb=w_lb_st, bq=bq_st, bk=bk_st,
        bv=bv_st, w_k=w_k_st, w_v=w_v_st, w_route=w_route_st, w_a1=w_a1_st,
        w_b1=w_b1_st)

    in_maps = []
    for c in range(8):
        bb, ci = c // 4, c % 4
        m = dict(shared)
        m["x"] = C(x[bb, ci * 1024:(ci + 1) * 1024, :])
        if ci == 0:
            m["xh"] = np.zeros((3, 4096), f4)
        else:
            m["xh"] = C(x[bb, ci * 1024 - 3:ci * 1024, :])
        m["wq_sl"] = C(wq[c * 512:(c + 1) * 512].reshape(4, 128, 128, 32))
        pm = np.zeros((1, 8), f4)
        pw = np.zeros((1, 64), f4)
        for j in range(8):
            jb, jci = j // 4, j % 4
            if jb == bb and jci < ci:
                pm[0, j] = 1.0
                for k in range(8):
                    kb, kci = k // 4, k % 4
                    if kb == bb and jci < kci < ci:
                        pw[0, j * 8 + k] = 1.0
        m["pm8"] = pm
        m["ppw"] = pw
        in_maps.append(m)
    return in_maps


def kernel(**inputs):
    in_maps = stage_inputs(inputs)
    nc = build_nc()
    res = run_bass_kernel_spmd(nc, in_maps, core_ids=list(range(8)))
    out = np.empty((2, 4096, 4096), np.float32)
    for c in range(8):
        bb, ci = c // 4, c % 4
        out[bb, ci * 1024:(ci + 1) * 1024, :] = res.results[c]["out"]
    return out
